# version 01
# baseline (speedup 1.0000x reference)
import contextlib

import numpy as np
import ml_dtypes
import concourse.bass as bass
import concourse.mybir as mybir
from concourse.bass_utils import run_bass_kernel_spmd

F32 = mybir.dt.float32
BF16 = mybir.dt.bfloat16
FP8 = mybir.dt.float8e4
AF = mybir.ActivationFunctionType
ALU = mybir.AluOpType
AXX = mybir.AxisListType.X

MAGIC = 3 * 2.0 ** 22

D = 1024
DI = 2048
DH = 1024
NST = 16
DTR = 64
S = 4096
TOWN = 2048
NC = 8
TC = 1024
NCHUNK = S // TC
PAIRS = [[2 * k, 2 * k + 1] for k in range(NC // 2)]
ALLG = [list(range(NC))]
OUT_SHAPE = (4, 4096, 1024)

ENGINES = ("sync", "vector", "scalar", "tensor", "gpsimd")


class PB:

    def __init__(self, nc):
        self.nc = nc
        self.q = {e: [] for e in ENGINES}
        self.cnt = {}
        self.sems = {}
        self.alias = {}
        self._stack = contextlib.ExitStack()

    def sem(self, name):
        if name not in self.sems:
            self.sems[name] = self._stack.enter_context(self.nc.semaphore(name))
            self.cnt[name] = 0
        return self.sems[name]

    KEEP = {"lntpe", "ipmm", "fcmm", "pjmm", "qkmm", "womm", "dqe", "pjev",
            "p3e", "cc_mix", "cc_kv", "cc_q"}
    CANON = {"vector": "cv", "scalar": "cs", "tensor": "ct", "gpsimd": "cg"}

    def op(self, engine, method, *args, inc=None, winc=1, wait=None, **kw):
        if (inc is not None and method != "dma_start"
                and method != "collective_compute" and inc not in self.KEEP):
            canon = self.CANON[engine]
            old = self.alias.get(inc)
            assert old in (None, canon), (inc, old, canon)
            self.alias[inc] = canon
            inc = canon
        waits = []
        if wait:
            if not isinstance(wait, list):
                wait = [wait]
            for w in wait:
                if isinstance(w, str):
                    wname, wval = w, None
                else:
                    wname, wval = w
                wname = self.alias.get(wname, wname)
                self.sem(wname)
                wv = self.cnt[wname] if wval is None else wval
                if wv > 0:
                    waits.append((self.sem(wname), wv))
        sem = None
        if inc is not None:
            sem = self.sem(inc)
            self.cnt[inc] += winc

        kw.pop("drain", None)
        drain = (engine in ("vector", "scalar", "gpsimd")
                 and method not in ("dma_start", "collective_compute"))

        def fn(eng, _m=method, _a=args, _k=kw, _s=sem, _w=waits, _wi=winc,
               _d=drain):
            for (sh, wv) in _w:
                eng.wait_ge(sh, wv)
            if _d:
                eng.drain()
            r = getattr(eng, _m)(*_a, **_k)
            if _s is not None:
                r.then_inc(_s, _wi)
        self.q[engine].append(fn)

    def dma(self, engine, out, in_, inc, wait=None, **kw):
        self.op(engine, "dma_start", inc=inc, winc=16, wait=wait,
                out=out, in_=in_, **kw)

    def wait(self, engine, name, value=None):
        name = self.alias.get(name, name)
        self.sem(name)
        v = self.cnt[name] if value is None else value
        if v <= 0:
            return
        sem = self.sem(name)
        self.q[engine].append(lambda eng, _s=sem, _v=v: eng.wait_ge(_s, _v))

    def val(self, name):
        name = self.alias.get(name, name)
        self.sem(name)
        return self.cnt[name]

    def replay(self, block):
        pb = self

        @block.sync
        def _(eng):
            for fn in pb.q["sync"]:
                fn(eng)

        @block.vector
        def _(eng):
            for fn in pb.q["vector"]:
                fn(eng)

        @block.scalar
        def _(eng):
            for fn in pb.q["scalar"]:
                fn(eng)

        @block.tensor
        def _(eng):
            for fn in pb.q["tensor"]:
                fn(eng)

        @block.gpsimd
        def _(eng):
            for fn in pb.q["gpsimd"]:
                fn(eng)


def build_program(debug=()):
    nc = bass.Bass("TRN2", target_bir_lowering=False, num_devices=NC)
    stack = contextlib.ExitStack()

    def din(name, shape, dt=F32):
        return nc.dram_tensor(name, list(shape), dt, kind="ExternalInput").ap()

    def sbt(name, shape, dt=F32):
        return stack.enter_context(nc.sbuf_tensor(name, list(shape), dt))

    x_batch = din("x_batch", [S, D])
    x_own = din("x_own", [TOWN, D])
    winT = din("winT", [24, 128, 1024])
    conv_w_cols = din("conv_w_cols", [128, 64])
    conv_b_cols = din("conv_b_cols", [128, 16])
    ln1w_col = din("ln1w_col", [128, 8])
    ln2w_col_in = din("ln2w_col_in", [128, 8])
    xprojT = din("xprojT", [DI, 112])
    dtwT = din("dtwT", [DTR, DH])
    dtb_col = din("dtb_col", [128, 8])
    alog_scan = din("alog_scan", [128, 128])
    dp_col = din("dp_col", [128, 8])
    woutT = din("woutT", [DH, D])
    fcT_tiled = din("fcT_tiled", [32, 128, 1024])
    l1_qkvo = din("l1_qkvo", [4, 4, 8, 128, 1024])
    l1_w1 = din("l1_w1", [4, 32, 128, 1024])
    l1_w2 = din("l1_w2", [4, 4096, 1024])
    l1_bqkvo = din("l1_bqkvo", [4, 4, 128, 8])
    l1_b1c = din("l1_b1c", [4, 128, 32])
    l1_b2r = din("l1_b2r", [4, 1, 1024])
    l1_lnsb = din("l1_lnsb", [4, 4, 1, 1024])
    projT = din("projT", [4096, 1024])
    w8T = din("w8T", [D, D], BF16)
    ws_in = din("ws_in", [1, 1])
    bias_row_in = din("bias_row_in", [1, D])
    sel16 = din("sel16", [16, 128], BF16)
    sely = din("sely", [128, 16 * 128], BF16)
    eye128 = din("eye128", [128, 128], BF16)
    eye128f = din("eye128f", [128, 128])
    wup_in = din("wup_in", [3072, 1024])
    wg_in = din("wg_in", [3072, 1024])
    wd_in = din("wd_in", [1024, 3072])

    out = nc.dram_tensor("out", [TOWN, D], F32, kind="ExternalOutput").ap()
    dbg = {}
    for name, shape in debug:
        dbg[name] = nc.dram_tensor("dbg_" + name, list(shape), F32,
                                   kind="ExternalOutput").ap()

    dt_hbm = nc.dram_tensor("dt_hbm", [DH, 1, TC], F32)
    w_hbm = nc.dram_tensor("w_hbm", [DH, 1, TC], F32)
    mix_part = nc.dram_tensor("mix_part", [2 * TOWN, D], F32)
    mix_sum = nc.dram_tensor("mix_sum", [TOWN, D], F32)

    pb = PB(nc)

    xnb16 = sbt("xnb16", [128, 1024], BF16)
    sq32 = sbt("sq32", [128, 1024])
    stat = sbt("stat", [128, 16])
    ccol = sbt("ccol", [128, 128])
    kcol = sbt("kcol", [128, 8])
    eye_sb = sbt("eye_sb", [128, 128], BF16)
    mstack = contextlib.ExitStack()

    def mbt(name, shape, dt=F32):
        return mstack.enter_context(nc.sbuf_tensor(name, list(shape), dt))

    xtile = mbt("xtile", [128, 1024])
    sp32a = mbt("sp32a", [128, 1024])
    sp32b = mbt("sp32b", [128, 1024])
    ev32 = mbt("ev32", [128, 1024])
    xn_fm = mbt("xn_fm", [128, 8 * TC], BF16)
    wib = mbt("wib", [128, 2 * 1024], BF16)
    xr_rot = mbt("xr_rot", [128, TC + 4])
    halo = mbt("halo", [128, 64])
    cacc = mbt("cacc", [128, TC])
    xc_own = mbt("xc_own", [128, 8 * TC], BF16)
    xc_rot = mbt("xc_rot", [128, TC], BF16)
    z_b = mbt("z_b", [128, 8 * TC], BF16)
    xprojT_sb = mbt("xprojT_sb", [128, 16 * 112], BF16)
    dtw_sb = mbt("dtw_sb", [64, DH], BF16)
    wout_sb = mbt("wout_sb", [128, 8 * 1024], BF16)
    dcb_dtr = mbt("dcb_dtr", [64, TC], BF16)
    dcb_bc = mbt("dcb_bc", [16, TC], BF16)
    dcb_cc = mbt("dcb_cc", [16, TC], BF16)
    dt32 = mbt("dt32", [128, TC])
    w32 = mbt("w32", [128, TC])
    bcrep = mbt("bcrep", [128, TC], BF16)
    ccrep = mbt("ccrep", [128, TC], BF16)
    dtrep = mbt("dtrep", [128, 2 * TC], BF16)
    wrep = mbt("wrep", [128, 2 * TC], BF16)
    dA_b = mbt("dA_b", [128, TC], BF16)
    dBx_b = mbt("dBx_b", [128, TC], BF16)
    h32 = mbt("h32", [128, TC])
    hC_b = mbt("hC_b", [128, TC], BF16)
    hcarry = mbt("hcarry", [128, 128])
    yg_b = mbt("yg_b", [128, 8 * TC], BF16)
    zsil = mbt("zsil", [128, 512])
    acol = mbt("acol", [128, 128])
    sely_sb = mbt("sely_sb", [128, 16 * 128], BF16)
    sel16_sb = mbt("sel16_sb", [16, 128], BF16)

    psum = [stack.enter_context(nc.psum_tensor(
                f"ps{i}", [128, 512], BF16 if i in (4, 5) else F32))
            for i in range(8)]

    LN1W, CONVW, CONVB, DTB, DP = 0, 16, 80, 96, 104

    pb.dma("sync", ccol[:, LN1W:LN1W + 8], ln1w_col, "cld")
    pb.dma("sync", ccol[:, CONVW:CONVW + 64], conv_w_cols, "cld")
    pb.dma("sync", ccol[:, CONVB:CONVB + 16], conv_b_cols, "cld")
    pb.dma("sync", ccol[:, DTB:DTB + 8], dtb_col, "cld")
    pb.dma("sync", ccol[:, DP:DP + 8], dp_col, "cld")
    pb.dma("sync", sel16_sb[:], sel16, "cld")
    pb.dma("sync", sely_sb[:], sely, "cld")
    pb.dma("sync", eye_sb[:], eye128, "cld")
    pb.dma("sync", acol[:], alog_scan, "cld")
    pb.op("gpsimd", "dma_start", inc="cldg", winc=16,
          out=xprojT_sb[:].rearrange("p (c r) -> p c r", c=16),
          in_=xprojT.rearrange("(c p) r -> p c r", p=128))
    pb.dma("gpsimd", dtw_sb[:], dtwT, "cldg")
    pb.op("gpsimd", "dma_start", inc="cldg", winc=16,
          out=wout_sb[:].rearrange("p (c e) -> p c e", c=8),
          in_=woutT.rearrange("(c p) e -> p c e", p=128))

    pb.op("scalar", "activation", wait=[("cld", None)], inc="aprep",
          out=acol[:], in_=acol[:], func=AF.Exp)
    pb.op("vector", "tensor_scalar", wait=[("aprep", None)], inc="aprepv",
          out=acol[:], in0=acol[:], scalar1=-1.0, scalar2=0.0,
          op0=ALU.mult, op1=ALU.add)
    pb.op("vector", "memset", kcol[:, 0:1], 1e-5, inc="minit")
    pb.op("vector", "memset", kcol[:, 1:2], 1.0, inc="minit")
    pb.op("vector", "memset", kcol[:, 2:3], 1e-6, inc="minit")
    pb.op("vector", "memset", halo[:], 0.0, inc="minit")
    pb.op("vector", "memset", hcarry[:], 0.0, inc="minit")
    pb.wait("vector", "cld")
    pb.wait("scalar", "cld")
    pb.wait("tensor", "cld")
    pb.wait("tensor", "cldg")

    for ci in range(NCHUNK):
        t0 = ci * TC

        for j in range(TC // 128):
            pb.dma("sync", xtile[:], x_batch[t0 + j * 128: t0 + (j + 1) * 128, :],
                   "d_xld", wait=[("lnap", None)])
            pb.op("scalar", "activation",
                  wait=[("d_xld", None), ("st_v", None)], inc="st_s",
                  out=sq32[:], in_=xtile[:], func=AF.Square,
                  accum_out=stat[:, 0:1])
            pb.op("vector", "tensor_reduce",
                  wait=[("d_xld", None), ("st_s", None)],
                  inc="st_v", out=stat[:, 1:2], in_=xtile[:], axis=AXX,
                  op=ALU.add)
            pb.op("vector", "tensor_scalar", inc="st_v", out=stat[:, 2:3],
                  in0=stat[:, 1:2], scalar1=1.0 / D, scalar2=0.0,
                  op0=ALU.mult, op1=ALU.add, drain=True)
            pb.op("vector", "tensor_scalar", inc="st_v", out=stat[:, 3:4],
                  in0=stat[:, 0:1], scalar1=1.0 / D, scalar2=0.0,
                  op0=ALU.mult, op1=ALU.add)
            pb.op("vector", "tensor_tensor", inc="st_v", out=stat[:, 4:5],
                  in0=stat[:, 2:3], in1=stat[:, 2:3], op=ALU.mult, drain=True)
            pb.op("vector", "tensor_tensor", inc="st_v", out=stat[:, 5:6],
                  in0=stat[:, 3:4], in1=stat[:, 4:5], op=ALU.subtract,
                  drain=True)
            pb.op("scalar", "activation",
                  wait=[("st_v", None), ("minit", None)], inc="st_s2",
                  out=stat[:, 6:7], in_=stat[:, 5:6], func=AF.Sqrt,
                  bias=kcol[:, 0:1], scale=1.0)
            pb.op("vector", "reciprocal", wait=[("st_s2", None)], inc="st_v",
                  out=stat[:, 7:8], in_=stat[:, 6:7])
            pb.op("vector", "tensor_scalar", inc="lnap",
                  out=xnb16[:], in0=xtile[:], scalar1=stat[:, 2:3],
                  scalar2=stat[:, 7:8], op0=ALU.subtract, op1=ALU.mult,
                  drain=True)
            for k in range(8):
                pk = psum[4 + (k % 2)]
                tp_wait = [("lnap", None)]
                if pb.val("lntpe") >= 2:
                    tp_wait.append(("lntpe", pb.val("lntpe") - 1))
                pb.op("tensor", "transpose", pk[:, 0:128],
                      xnb16[:, k * 128:(k + 1) * 128], eye_sb[:],
                      wait=tp_wait, inc="lntpp")
                pb.op("vector", "tensor_scalar", wait=[("lntpp", None)],
                      inc="lntpe",
                      out=xn_fm[:, k * TC + j * 128: k * TC + (j + 1) * 128],
                      in0=pk[:, 0:128], scalar1=ccol[:, LN1W + k:LN1W + k + 1],
                      scalar2=0.0, op0=ALU.mult, op1=ALU.add)

        for et in range(24):
            wb = wib[:, (et % 2) * 1024:(et % 2) * 1024 + 1024]
            wsem = "d_wib%d" % (et % 2)
            dma_wait = None
            if pb.val("ipmm") >= 32:
                dma_wait = [("ipmm", pb.val("ipmm") - 16)]
            pb.op("gpsimd", "dma_start", inc=wsem, winc=16, wait=dma_wait,
                  out=wb, in_=winT[et])
            pb.wait("tensor", wsem)
            pb.wait("tensor", "lntpe")
            for th in range(2):
                pe = psum[th] if th == 0 else psum[6 + (et % 2)]
                if th == 0:
                    pe = psum[et % 2]
                for k in range(8):
                    mm_wait = [("scn", None)] if False else None
                    if k == 0:
                        mm_wait = [("ipev", None), ("cnv", None),
                                   ("xpmm", None)]
                    pb.op("tensor", "matmul", pe[:, 0:512],
                          wb[:, k * 128:(k + 1) * 128],
                          xn_fm[:, k * TC + th * 512: k * TC + th * 512 + 512],
                          start=(k == 0), stop=(k == 7),
                          wait=mm_wait, inc="ipmm")
            if et < 16:
                pb.op("vector", "tensor_copy", wait=[("ipmm", None)],
                      inc="ipev", out=xr_rot[:, 4:4 + 512],
                      in_=psum[et % 2][:, 0:512])
                pb.op("vector", "tensor_copy", inc="ipev",
                      out=xr_rot[:, 4 + 512:4 + 1024],
                      in_=psum[6 + (et % 2)][:, 0:512])
                pb.op("vector", "tensor_copy", inc="cnv",
                      out=xr_rot[:, 0:4], in_=halo[:, et * 4:(et + 1) * 4])
                pb.op("vector", "tensor_scalar", inc="cnv", out=cacc[:],
                      drain=True, in0=xr_rot[:, 1:1 + TC],
                      scalar1=ccol[:, CONVW + et * 4:CONVW + et * 4 + 1],
                      scalar2=0.0, op0=ALU.mult, op1=ALU.add)
                for tap in range(1, 4):
                    pb.op("vector", "scalar_tensor_tensor", inc="cnv",
                          drain=True, out=cacc[:],
                          in0=xr_rot[:, tap + 1:tap + 1 + TC],
                          scalar=ccol[:, CONVW + et * 4 + tap:
                                       CONVW + et * 4 + tap + 1],
                          in1=cacc[:], op0=ALU.mult, op1=ALU.add)
                pb.op("vector", "tensor_copy", inc="cnv",
                      out=halo[:, et * 4:(et + 1) * 4],
                      in_=xr_rot[:, TC:TC + 4])
                if et < 8:
                    xc_dst = xc_own[:, et * TC:(et + 1) * TC]
                else:
                    xc_dst = xc_rot[:]
                pb.op("scalar", "activation",
                      wait=[("cnv", None), ("xpmm", None)], inc="cnv_s",
                      out=xc_dst, in_=cacc[:], func=AF.Silu,
                      bias=ccol[:, CONVB + et:CONVB + et + 1], scale=1.0)
                for th in range(2):
                    pb.op("tensor", "matmul", psum[2 + th][0:112, 0:512],
                          xprojT_sb[:, et * 112:(et + 1) * 112],
                          xc_dst[:, th * 512:th * 512 + 512],
                          start=(et == 0), stop=(et == 15),
                          wait=[("cnv_s", None)], inc="xpmm")
            else:
                zi = (et - 16) * TC
                pb.op("vector", "tensor_copy", wait=[("ipmm", None)],
                      inc="ipev", out=z_b[:, zi:zi + 512],
                      in_=psum[et % 2][:, 0:512])
                pb.op("vector", "tensor_copy", inc="ipev",
                      out=z_b[:, zi + 512:zi + 1024],
                      in_=psum[6 + (et % 2)][:, 0:512])

        for th in range(2):
            pb.op("vector", "tensor_copy", wait=[("xpmm", None)], inc="xde",
                  out=dcb_dtr[:, th * 512:th * 512 + 512],
                  in_=psum[2 + th][0:64, 0:512])
            pb.op("vector", "tensor_copy", inc="xde",
                  out=dcb_bc[:, th * 512:th * 512 + 512],
                  in_=psum[2 + th][64:80, 0:512])
            pb.op("vector", "tensor_copy", inc="xde",
                  out=dcb_cc[:, th * 512:th * 512 + 512],
                  in_=psum[2 + th][96:112, 0:512])

        for th in range(2):
            pb.op("tensor", "matmul", psum[2][:, 0:512], sel16_sb[:],
                  dcb_bc[:, th * 512:th * 512 + 512], start=True, stop=True,
                  wait=[("xde", None), ("bcre", None)], inc="bcr")
            pb.op("vector", "tensor_copy", wait=[("bcr", None)], inc="bcre",
                  out=bcrep[:, th * 512:th * 512 + 512], in_=psum[2][:, 0:512])
            pb.op("tensor", "matmul", psum[3][:, 0:512], sel16_sb[:],
                  dcb_cc[:, th * 512:th * 512 + 512], start=True, stop=True,
                  wait=[("bcre", None)], inc="bcr")
            pb.op("vector", "tensor_copy", wait=[("bcr", None)], inc="bcre",
                  out=ccrep[:, th * 512:th * 512 + 512], in_=psum[3][:, 0:512])

        for dtile in range(8):
            for th in range(2):
                mm_wait = [("xde", None), ("bcre", None), ("dtv", None)]
                pb.op("tensor", "matmul", psum[2 + th][:, 0:512],
                      dtw_sb[:, dtile * 128:(dtile + 1) * 128],
                      dcb_dtr[:, th * 512:th * 512 + 512],
                      start=True, stop=True, wait=mm_wait, inc="dtmm")
            for th in range(2):
                ps = psum[2 + th]
                sl = slice(th * 512, th * 512 + 512)
                pb.op("vector", "tensor_scalar",
                      wait=[("dtmm", None), ("d_dtw", None)],
                      inc="dtv", out=sp32a[:, sl], in0=ps[:, 0:512],
                      scalar1=ccol[:, DTB + dtile:DTB + dtile + 1],
                      scalar2=0.0, op0=ALU.add, op1=ALU.add)
                pb.op("vector", "tensor_scalar", inc="dtv", out=sp32b[:, sl],
                      in0=sp32a[:, sl], scalar1=-1.0, scalar2=0.0,
                      op0=ALU.mult, op1=ALU.add, drain=True)
                pb.op("vector", "tensor_tensor", inc="dtv", out=sp32b[:, sl],
                      in0=sp32a[:, sl], in1=sp32b[:, sl], op=ALU.max,
                      drain=True)
                pb.op("scalar", "activation", wait=[("dtv", None)], inc="dts",
                      out=sp32b[:, sl], in_=sp32b[:, sl], func=AF.Exp,
                      scale=-1.0)
                pb.op("scalar", "activation", inc="dts", out=sp32b[:, sl],
                      in_=sp32b[:, sl], func=AF.Ln, bias=kcol[:, 1:2],
                      scale=1.0, drain=True)
                pb.op("vector", "scalar_tensor_tensor", wait=[("dts", None)],
                      inc="dtev", out=dt32[:, sl], in0=sp32a[:, sl],
                      scalar=0.0, in1=sp32b[:, sl], op0=ALU.max, op1=ALU.add,
                      drain=True)
            pb.op("vector", "tensor_tensor", inc="dtev", out=w32[:],
                  in0=dt32[:], in1=xc_own[:, dtile * TC:(dtile + 1) * TC],
                  op=ALU.mult, drain=True)
            dtw_wait = [("dtev", None)]
            if ci > 0 and dtile == 0:
                dtw_wait.append(("d_rep", None))
            pb.dma("sync", dt_hbm.ap()[dtile * 128:(dtile + 1) * 128, 0, :],
                   dt32[:], "d_dtw", wait=dtw_wait)
            pb.dma("sync", w_hbm.ap()[dtile * 128:(dtile + 1) * 128, 0, :],
                   w32[:], "d_dtw")

        for g in range(8):
            for i16 in range(16):
                s = g * 16 + i16
                rot = s % 2
                src_dt = dt_hbm.ap()[s * 8:(s + 1) * 8, :, :] \
                    .broadcast_to([8, 16, TC])
                src_w = w_hbm.ap()[s * 8:(s + 1) * 8, :, :] \
                    .broadcast_to([8, 16, TC])
                dst_dt = dtrep[:, rot * TC:rot * TC + TC]
                dst_w = wrep[:, rot * TC:rot * TC + TC]
                wdep = [("d_dtw", None), ("sc_s", None), ("sc_v", None)]
                pb.op("gpsimd", "dma_start", inc="d_rep", winc=16,
                      out=dst_dt, in_=src_dt, wait=wdep)
                pb.op("gpsimd", "dma_start", inc="d_rep", winc=16,
                      out=dst_w, in_=src_w)
                pb.op("scalar", "activation",
                      wait=[("d_rep", None), ("aprepv", None), ("sc_v", None)],
                      inc="sc_s",
                      out=dA_b[:], in_=dtrep[:, rot * TC:rot * TC + TC],
                      func=AF.Exp, scale=acol[:, s:s + 1])
                pb.op("vector", "tensor_tensor",
                      wait=[("d_rep", None), ("bcre", None)],
                      inc="sc_v", out=dBx_b[:],
                      in0=wrep[:, rot * TC:rot * TC + TC], in1=bcrep[:],
                      op=ALU.mult)
                pb.op("vector", "tensor_tensor_scan", wait=[("sc_s", None)],
                      inc="sc_v", out=h32[:], data0=dA_b[:], data1=dBx_b[:],
                      initial=hcarry[:, s:s + 1], op0=ALU.mult, op1=ALU.add,
                      drain=True)
                pb.op("vector", "tensor_copy", inc="sc_v",
                      out=hcarry[:, s:s + 1], in_=h32[:, TC - 1:TC],
                      drain=True)
                pb.op("vector", "tensor_tensor", inc="scn", out=hC_b[:],
                      in0=h32[:], in1=ccrep[:], op=ALU.mult,
                      wait=[("ymm", None)])
                for th in range(2):
                    ym_wait = [("scn", None)]
                    if i16 == 0:
                        ym_wait.append(("ygv", None))
                    pb.op("tensor", "matmul", psum[6 + th][:, 0:512],
                          sely_sb[:, i16 * 128:(i16 + 1) * 128],
                          hC_b[:, th * 512:th * 512 + 512],
                          start=(i16 == 0), stop=(i16 == 15),
                          wait=ym_wait, inc="ymm")
            for th in range(2):
                ps = psum[6 + th]
                sl = slice(g * TC + th * 512, g * TC + th * 512 + 512)
                pb.op("vector", "scalar_tensor_tensor", wait=[("ymm", None)],
                      inc="ygv0", out=sp32a[:, 0:512],
                      in0=xc_own[:, sl], scalar=ccol[:, DP + g:DP + g + 1],
                      in1=ps[:, 0:512], op0=ALU.mult, op1=ALU.add)
                pb.op("scalar", "activation",
                      wait=[("ygv", None), ("ygv0", None)], inc="ygs",
                      out=zsil[:], in_=z_b[:, sl], func=AF.Silu)
                pb.op("vector", "tensor_tensor", wait=[("ygs", None)],
                      inc="ygv", out=yg_b[:, sl], in0=sp32a[:, 0:512],
                      in1=zsil[:], op=ALU.mult, drain=True)

        for tt in range(TC // 128):
            for th in range(2):
                for k in range(8):
                    mm_wait = [("ygv", None)]
                    if k == 0:
                        mm_wait.append(("opev", None))
                    pb.op("tensor", "matmul", psum[th][:, 0:512],
                          yg_b[:, k * TC + tt * 128: k * TC + (tt + 1) * 128],
                          wout_sb[:, k * 1024 + th * 512:
                                  k * 1024 + th * 512 + 512],
                          start=(k == 0), stop=(k == 7),
                          wait=mm_wait, inc="opmm")
                pb.op("vector", "tensor_copy",
                      wait=[("opmm", None), ("d_mp", None)], inc="opev",
                      out=ev32[:, th * 512:th * 512 + 512],
                      in_=psum[th][:, 0:512])
            pb.dma("sync", mix_part.ap()[t0 + tt * 128: t0 + (tt + 1) * 128, :],
                   ev32[:, 0:1024], "d_mp", wait=[("opev", None)])

    pb.wait("gpsimd", "d_mp")
    pb.op("gpsimd", "collective_compute", "ReduceScatter", ALU.add,
          replica_groups=PAIRS, ins=[mix_part.ap().opt()],
          outs=[mix_sum.ap().opt()], inc="cc_mix")

    mstack.close()
    stream = sbt("stream", [128, 16 * 1024])
    for j in range(16):
        pb.dma("sync", stream[:, j * 1024:(j + 1) * 1024],
               x_own[j * 128:(j + 1) * 128, :], "d_so",
               wait=[("opev", None), ("d_mp", None)])
    for j in range(16):
        pb.dma("gpsimd", sq32[:],
               mix_sum.ap()[j * 128:(j + 1) * 128, :], "d_sm",
               wait=[("cc_mix", None), ("res0", None)])
        pb.op("vector", "tensor_tensor",
              wait=[("d_so", None), ("d_sm", None)],
              inc="res0", out=stream[:, j * 1024:(j + 1) * 1024],
              in0=stream[:, j * 1024:(j + 1) * 1024], in1=sq32[:], op=ALU.add)

    if "after_mixer" in dbg:
        for j in range(16):
            pb.dma("sync", dbg["after_mixer"][j * 128:(j + 1) * 128, :],
                   stream[:, j * 1024:(j + 1) * 1024], "d_dbg",
                   wait=[("res0", None)])


    bstack = contextlib.ExitStack()

    def bbt(name, shape, dt=F32):
        return bstack.enter_context(nc.sbuf_tensor(name, list(shape), dt))

    xfmA = sbt("xfmA", [128, 8 * 2048], BF16)
    wstr = sbt("wstr", [128, 2 * 2048], BF16)
    h_fm = bbt("h_fm", [128, 32 * 512], BF16)
    ln2w2_col = bbt("ln2w2_col", [128, 8])
    pb.dma("sync", ln2w2_col[:], ln2w_col_in, "d_ln2w")

    def stream_ln(ready, dst_fm=None, wcol=None, inplace=False):
        for j in range(16):
            xt = stream[:, j * 1024:(j + 1) * 1024]
            pb.op("scalar", "activation",
                  wait=ready + [("st_v", None)], inc="st_s",
                  out=sq32[:], in_=xt, func=AF.Square, accum_out=stat[:, 0:1])
            pb.op("vector", "tensor_reduce", wait=ready + [("st_s", None)],
                  inc="st_v", out=stat[:, 1:2], in_=xt, axis=AXX, op=ALU.add)
            pb.op("vector", "tensor_scalar", inc="st_v", out=stat[:, 2:3],
                  in0=stat[:, 1:2], scalar1=1.0 / D, scalar2=0.0,
                  op0=ALU.mult, op1=ALU.add)
            pb.op("vector", "tensor_scalar", inc="st_v", out=stat[:, 3:4],
                  in0=stat[:, 0:1], scalar1=1.0 / D, scalar2=0.0,
                  op0=ALU.mult, op1=ALU.add)
            pb.op("vector", "tensor_tensor", inc="st_v", out=stat[:, 4:5],
                  in0=stat[:, 2:3], in1=stat[:, 2:3], op=ALU.mult)
            pb.op("vector", "tensor_tensor", inc="st_v", out=stat[:, 5:6],
                  in0=stat[:, 3:4], in1=stat[:, 4:5], op=ALU.subtract)
            pb.op("scalar", "activation", wait=[("st_v", None)], inc="st_s2",
                  out=stat[:, 6:7], in_=stat[:, 5:6], func=AF.Sqrt,
                  bias=kcol[:, 0:1], scale=1.0)
            pb.op("vector", "reciprocal", wait=[("st_s2", None)], inc="st_v",
                  out=stat[:, 7:8], in_=stat[:, 6:7])
            if inplace:
                pb.op("vector", "tensor_scalar", inc="lnap", out=xt, in0=xt,
                      scalar1=stat[:, 2:3], scalar2=stat[:, 7:8],
                      op0=ALU.subtract, op1=ALU.mult)
            else:
                pb.op("vector", "tensor_scalar", inc="lnap", out=xnb16[:],
                      in0=xt, scalar1=stat[:, 2:3], scalar2=stat[:, 7:8],
                      op0=ALU.subtract, op1=ALU.mult)
                for k in range(8):
                    pk = psum[4 + (k % 2)]
                    tp_wait = [("lnap", None)]
                    if pb.val("lntpe") >= 2:
                        tp_wait.append(("lntpe", pb.val("lntpe") - 1))
                    pb.op("tensor", "transpose", pk[:, 0:128],
                          xnb16[:, k * 128:(k + 1) * 128], eye_sb[:],
                          wait=tp_wait, inc="lntpp")
                    if wcol is None:
                        pb.op("vector", "tensor_copy", wait=[("lntpp", None)],
                              inc="lntpe",
                              out=dst_fm[:, k * 2048 + j * 128:
                                         k * 2048 + (j + 1) * 128],
                              in_=pk[:, 0:128])
                    else:
                        pb.op("vector", "tensor_scalar", wait=[("lntpp", None)],
                              inc="lntpe",
                              out=dst_fm[:, k * 2048 + j * 128:
                                         k * 2048 + (j + 1) * 128],
                              in0=pk[:, 0:128], scalar1=wcol[:, k:k + 1],
                              scalar2=0.0, op0=ALU.mult, op1=ALU.add)

    pb.wait("vector", "d_ln2w")
    stream_ln([("res0", None), ("d_dbg", None)], dst_fm=xfmA, wcol=ln2w2_col)

    for tch in range(4):
        ts0 = tch * 512
        for ft in range(32):
            par = ft % 2
            g = []
            if pb.val("fcmm") >= 16:
                g.append(("fcmm", pb.val("fcmm") - 8))
            if tch > 0 and ft < 2:
                g.append(("pjev", None))
            pb.op("gpsimd", "dma_start", inc="d_fc%d" % par, winc=16,
                  wait=g or None, out=wstr[:, par * 2048:par * 2048 + 1024],
                  in_=fcT_tiled[ft])
            pb.wait("tensor", "d_fc%d" % par)
            pb.wait("tensor", "lntpe")
            for k in range(8):
                mmw = None
                if k == 0:
                    mmw = [("fcev", None), ("pjmm", None), ("pjev", None)]
                pb.op("tensor", "matmul", psum[par][:, 0:512],
                      wstr[:, par * 2048 + k * 128:par * 2048 + (k + 1) * 128],
                      xfmA[:, k * 2048 + ts0: k * 2048 + ts0 + 512],
                      start=(k == 0), stop=(k == 7), wait=mmw, inc="fcmm")
            pb.op("scalar", "activation",
                  wait=[("fcmm", None), ("pjmm", None)], inc="fcev",
                  out=h_fm[:, ft * 512:(ft + 1) * 512],
                  in_=psum[par][:, 0:512], func=AF.Silu)
        for wave in range(2):
            for k in range(32):
                par = k % 2
                g = []
                if pb.val("pjmm") >= 8:
                    g.append(("pjmm", pb.val("pjmm") - 4))
                pb.op("gpsimd", "dma_start", inc="d_pj%d" % par, winc=16,
                      wait=g or None,
                      out=wstr[:, par * 2048 + 1024:par * 2048 + 2048],
                      in_=projT[k * 128:(k + 1) * 128, :])
                pb.wait("tensor", "d_pj%d" % par)
                for tt in range(2):
                    for eh in range(2):
                        mmw = None
                        if k == 0:
                            mmw = [("fcev", None), ("pjev", None)]
                        pb.op("tensor", "matmul", psum[tt * 2 + eh][:, 0:512],
                              h_fm[:, k * 512 + (wave * 2 + tt) * 128:
                                   k * 512 + (wave * 2 + tt + 1) * 128],
                              wstr[:, par * 2048 + 1024 + eh * 512:
                                   par * 2048 + 1024 + eh * 512 + 512],
                              start=(k == 0), stop=(k == 31),
                              wait=mmw, inc="pjmm")
            for tt in range(2):
                jj = 4 * tch + wave * 2 + tt
                for eh in range(2):
                    pb.op("vector", "tensor_tensor", wait=[("pjmm", None)],
                          inc="pjev",
                          out=stream[:, jj * 1024 + eh * 512:
                                     jj * 1024 + eh * 512 + 512],
                          in0=stream[:, jj * 1024 + eh * 512:
                                     jj * 1024 + eh * 512 + 512],
                          in1=psum[tt * 2 + eh][:, 0:512], op=ALU.add)

    for j in range(16):
        pb.dma("gpsimd", sq32[:], x_own[j * 128:(j + 1) * 128, :], "d_so2",
               wait=[("res2", None), ("pjev", None)])
        pb.op("vector", "tensor_tensor", wait=[("pjev", None), ("d_so2", None)],
              inc="res2", out=stream[:, j * 1024:(j + 1) * 1024],
              in0=stream[:, j * 1024:(j + 1) * 1024], in1=sq32[:], op=ALU.add)
    stream_ln([("res2", None), ("d_dbg", None)], inplace=True)
    if "after_l0attn" in dbg:
        for j in range(16):
            pb.dma("sync", dbg["after_l0attn"][j * 128:(j + 1) * 128, :],
                   stream[:, j * 1024:(j + 1) * 1024], "d_dbg",
                   wait=[("lnap", None)])

    ws_col = bbt("ws_col", [128, 1])
    bias_row = bbt("bias_row", [1, 1024])
    bias_row_b = bbt("bias_row_b", [1, 1024], BF16)
    w8sb = bbt("w8sb", [128, 8 * 1024], BF16)
    wq_b = bbt("wq_b", [128, 8 * 1024], BF16)
    q8t = bbt("q8t", [128, 1024], FP8)
    sclC = bbt("sclC", [128, 16])
    rcol_b = bbt("rcol_b", [128, 1], BF16)
    rrow_b = bbt("rrow_b", [1, 128], BF16)

    pb.dma("sync", ws_col[:], ws_in.broadcast_to([128, 1]), "d_c2")
    pb.dma("sync", bias_row[:], bias_row_in, "d_c2")
    pb.op("gpsimd", "dma_start", inc="d_c2g", winc=16,
          out=w8sb[:].rearrange("p (c e) -> p c e", c=8),
          in_=w8T.rearrange("(c p) e -> p c e", p=128))
    pb.op("vector", "tensor_copy", wait=[("d_c2", None)], inc="c2v",
          out=bias_row_b[:], in_=bias_row[:])
    pb.op("vector", "tensor_scalar", wait=[("d_c2", None), ("d_c2g", None)],
          inc="c2v",
          out=wq_b[:], in0=w8sb[:], scalar1=ws_col[:, 0:1], scalar2=0.0,
          op0=ALU.mult, op1=ALU.add)
    for j in range(16):
        xt = stream[:, j * 1024:(j + 1) * 1024]
        pb.op("vector", "tensor_reduce",
              wait=[("lnap", None), ("d_dbg", None)], inc="q8v",
              out=stat[:, 9:10], in_=xt, axis=AXX, op=ALU.max,
              apply_absolute_value=True)
        pb.op("vector", "reciprocal", inc="q8v", out=stat[:, 10:11],
              in_=stat[:, 9:10])
        pb.op("vector", "tensor_scalar", inc="q8v", out=sq32[:], in0=xt,
              scalar1=stat[:, 10:11], scalar2=224.0, op0=ALU.mult,
              op1=ALU.mult)
        pb.op("vector", "tensor_copy", inc="q8v", out=q8t[:], in_=sq32[:])
        pb.op("vector", "tensor_copy", inc="q8v", out=xnb16[:], in_=q8t[:])
        pb.op("vector", "tensor_scalar", inc="q8v", out=sclC[:, j:j + 1],
              in0=stat[:, 9:10], scalar1=1.0 / 224.0, scalar2=0.0,
              op0=ALU.mult, op1=ALU.add)
        for k in range(8):
            pk = psum[4 + (k % 2)]
            tp_wait = [("q8v", None)]
            if pb.val("lntpe") >= 2:
                tp_wait.append(("lntpe", pb.val("lntpe") - 1))
            pb.op("tensor", "transpose", pk[:, 0:128],
                  xnb16[:, k * 128:(k + 1) * 128], eye_sb[:],
                  wait=tp_wait, inc="lntpp")
            pb.op("vector", "tensor_copy", wait=[("lntpp", None)],
                  inc="lntpe",
                  out=xfmA[:, k * 2048 + j * 128: k * 2048 + (j + 1) * 128],
                  in_=pk[:, 0:128])

    for j in range(16):
        pb.op("vector", "reciprocal", wait=[("q8v", None)], inc="r1v",
              out=stat[:, 11:12], in_=sclC[:, j:j + 1])
        pb.op("vector", "tensor_copy", inc="r1v", out=rcol_b[:],
              in_=stat[:, 11:12])
        pb.op("tensor", "transpose", psum[5][0:1, 0:128], rcol_b[:],
              eye_sb[:], wait=[("r1v", None), ("lntpe", None)], inc="r1t")
        pb.op("vector", "tensor_copy", wait=[("r1t", None)], inc="r1e",
              out=rrow_b[:], in_=psum[5][0:1, 0:128])
        for eh in range(2):
            pe = psum[eh]
            for k in range(8):
                pb.op("tensor", "matmul", pe[:, 0:512],
                      xfmA[:, k * 2048 + j * 128: k * 2048 + (j + 1) * 128],
                      wq_b[:, k * 1024 + eh * 512: k * 1024 + eh * 512 + 512],
                      start=(k == 0), stop=False,
                      wait=([("q8ev", None)] if k == 0 else None),
                      inc="q8mm")
            pb.op("tensor", "matmul", pe[:, 0:512], rrow_b[:],
                  bias_row_b[:, eh * 512:eh * 512 + 512], start=False,
                  stop=True, wait=[("r1e", None)], inc="q8mm")
            pb.op("vector", "scalar_tensor_tensor", wait=[("q8mm", None)],
                  inc="q8ev",
                  out=stream[:, j * 1024 + eh * 512: j * 1024 + eh * 512 + 512],
                  in0=pe[:, 0:512], scalar=sclC[:, j:j + 1],
                  in1=stream[:, j * 1024 + eh * 512: j * 1024 + eh * 512 + 512],
                  op0=ALU.mult, op1=ALU.add)

    stream_ln([("q8ev", None), ("d_dbg", None)], inplace=True)
    if "after_l0" in dbg:
        for j in range(16):
            pb.dma("sync", dbg["after_l0"][j * 128:(j + 1) * 128, :],
                   stream[:, j * 1024:(j + 1) * 1024], "d_dbg",
                   wait=[("lnap", None)])
    bstack.close()


    dstack = contextlib.ExitStack()

    def dbt(name, shape, dt=F32):
        return dstack.enter_context(nc.sbuf_tensor(name, list(shape), dt))

    xfmB = dbt("xfmB", [128, 8 * 2048], BF16)
    Kt = dbt("Kt", [128, 2048])
    vt = dbt("vt", [128, 2048])
    kvcols = dbt("kvcols", [128, 48])
    sbc = dbt("sbc", [128, 2 * 1024])
    bql = dbt("bql", [128, 8 * 4])
    b1l = dbt("b1l", [128, 32])
    b2r_b = dbt("b2r_b", [1, 1024], BF16)
    ones_row = dbt("ones_row", [1, 128], BF16)
    h_fm2 = dbt("h_fm2", [128, 32 * 512], BF16)

    kv_part = nc.dram_tensor("kv_part", [128, 16], F32)
    kv_sum = nc.dram_tensor("kv_sum", [128, 16], F32)
    xl1_hbm = nc.dram_tensor("xl1_hbm", [TOWN, D], F32)

    pb.op("vector", "memset", ones_row[:], 1.0, inc="c2v")
    for j in range(16):
        pb.dma("sync", xl1_hbm.ap()[j * 128:(j + 1) * 128, :],
               stream[:, j * 1024:(j + 1) * 1024], "d_xl1",
               wait=[("lnap", None)])

    def stream_to_fm(ready, dst_fm):
        for j in range(16):
            pb.op("vector", "tensor_copy", wait=ready + [("lntpp", None)],
                  inc="lnap", out=xnb16[:],
                  in_=stream[:, j * 1024:(j + 1) * 1024])
            for k in range(8):
                pk = psum[4 + (k % 2)]
                tp_wait = [("lnap", None)]
                if pb.val("lntpe") >= 2:
                    tp_wait.append(("lntpe", pb.val("lntpe") - 1))
                pb.op("tensor", "transpose", pk[:, 0:128],
                      xnb16[:, k * 128:(k + 1) * 128], eye_sb[:],
                      wait=tp_wait, inc="lntpp")
                pb.op("vector", "tensor_copy", wait=[("lntpp", None)],
                      inc="lntpe",
                      out=dst_fm[:, k * 2048 + j * 128:
                                 k * 2048 + (j + 1) * 128],
                      in_=pk[:, 0:128])

    for blk in range(4):
        pb.dma("sync", bql[:], l1_bqkvo[blk].rearrange("w p c -> p w c"),
               "d_l1c", wait=[("qkev", None), ("wos", None), ("fcev", None),
                              ("q8ev", None), ("lnap", None)])
        pb.dma("sync", b1l[:], l1_b1c[blk], "d_l1c")
        pb.dma("gpsimd", b2r_b[:], l1_b2r[blk], "d_l1cg",
               wait=[("pjmm", None), ("q8ev", None), ("lnap", None)])

        stream_to_fm([("lnap", None), ("d_xl1", None), ("d_dbg", None)], xfmA)

        for at in range(8):
            for w in range(3):
                par = (at * 3 + w) % 2
                g = []
                if pb.val("qkmm") >= 64:
                    g.append(("qkmm", pb.val("qkmm") - 32))
                if at == 0 and w == 0:
                    g.append(("fcmm", None))
                pb.op("gpsimd", "dma_start", inc="d_qk%d" % par, winc=16,
                      wait=g or None,
                      out=wstr[:, par * 2048:par * 2048 + 1024],
                      in_=l1_qkvo[blk, w, at])
                pb.wait("tensor", "d_qk%d" % par)
                pb.wait("tensor", "lntpe")
                for tch in range(4):
                    pe = psum[par]
                    for k in range(8):
                        mmw = None
                        if k == 0:
                            mmw = [("qkev", None), ("pjev", None)]
                        pb.op("tensor", "matmul", pe[:, 0:512],
                              wstr[:, par * 2048 + k * 128:
                                   par * 2048 + (k + 1) * 128],
                              xfmA[:, k * 2048 + tch * 512:
                                   k * 2048 + tch * 512 + 512],
                              start=(k == 0), stop=(k == 7),
                              wait=mmw, inc="qkmm")
                    bcol = bql[:, w * 8 + at:w * 8 + at + 1]
                    sl = slice(tch * 512, tch * 512 + 512)
                    if w < 2:
                        dst = Kt if w == 1 else vt
                        pb.op("vector", "tensor_scalar",
                              wait=[("qkmm", None), ("d_l1c", None)],
                              inc="qkv1", out=sq32[:, 0:512], in0=pe[:, 0:512],
                              scalar1=bcol, scalar2=0.0, op0=ALU.add,
                              op1=ALU.min)
                        pb.op("scalar", "activation", wait=[("qkv1", None)],
                              inc="qks1", out=sq32[:, 512:1024],
                              in_=sq32[:, 0:512], func=AF.Exp)
                        pb.op("vector", "tensor_scalar", inc="qkv1",
                              out=dst[:, sl], in0=pe[:, 0:512],
                              scalar1=bcol, scalar2=0.0, op0=ALU.add,
                              op1=ALU.max)
                        if w == 0:
                            pb.op("vector", "tensor_tensor",
                                  wait=[("qks1", None)], inc="qkev",
                                  out=h_fm2[:, at * 2048 + tch * 512:
                                            at * 2048 + tch * 512 + 512],
                                  in0=dst[:, sl], in1=sq32[:, 512:1024],
                                  op=ALU.add)
                        else:
                            pb.op("vector", "scalar_tensor_tensor",
                                  wait=[("qks1", None)], inc="qkev",
                                  out=Kt[:, sl], in0=Kt[:, sl], scalar=1.0,
                                  in1=sq32[:, 512:1024], op0=ALU.mult,
                                  op1=ALU.add)
                    else:
                        pb.op("vector", "tensor_scalar",
                              wait=[("qkmm", None), ("d_l1c", None)],
                              inc="qkev", out=vt[:, sl], in0=pe[:, 0:512],
                              scalar1=bcol, scalar2=0.0, op0=ALU.add,
                              op1=ALU.add)
            pb.op("vector", "tensor_reduce",
                  wait=[("qkev", None), ("d_kvp", None)], inc="kvv",
                  out=kvcols[:, 8 + at:8 + at + 1], in_=Kt[:], axis=AXX,
                  op=ALU.add)
            pb.op("vector", "scalar_tensor_tensor", inc="kvv",
                  out=Kt[:], in0=Kt[:], scalar=1.0, in1=vt[:],
                  op0=ALU.mult, op1=ALU.mult,
                  accum_out=kvcols[:, at:at + 1])

        pb.dma("sync", kv_part.ap()[:, 0:16], kvcols[:, 0:16], "d_kvp",
               wait=[("kvv", None)])
        pb.wait("gpsimd", "d_kvp")
        pb.op("gpsimd", "collective_compute", "AllReduce", ALU.add,
              replica_groups=PAIRS, ins=[kv_part.ap().opt()],
              outs=[kv_sum.ap().opt()], inc="cc_kv")
        pb.dma("gpsimd", kvcols[:, 16:32], kv_sum.ap()[:, 0:16], "d_kvs",
               wait=[("cc_kv", None)])

        for at in range(8):
            qsl = slice(at * 2048, (at + 1) * 2048)
            pb.op("vector", "tensor_scalar", wait=[("d_kvs", None)], inc="vv",
                  out=Kt[:], in0=h_fm2[:, qsl],
                  scalar1=kvcols[:, 24 + at:24 + at + 1], scalar2=1e-6,
                  op0=ALU.mult, op1=ALU.add)
            pb.op("vector", "reciprocal", inc="vv", out=Kt[:], in_=Kt[:])
            pb.op("vector", "tensor_scalar", inc="vv", out=vt[:],
                  in0=h_fm2[:, qsl],
                  scalar1=kvcols[:, 16 + at:16 + at + 1], scalar2=0.0,
                  op0=ALU.mult, op1=ALU.add)
            pb.op("vector", "tensor_tensor", inc="vv", out=xfmB[:, qsl],
                  in0=vt[:], in1=Kt[:], op=ALU.mult)

        for et in range(8):
            par = et % 2
            g = []
            if pb.val("womm") >= 64:
                g.append(("womm", pb.val("womm") - 32))
            pb.op("gpsimd", "dma_start", inc="d_wo%d" % par, winc=16,
                  wait=g or None, out=wstr[:, par * 2048:par * 2048 + 1024],
                  in_=l1_qkvo[blk, 3, et])
            pb.wait("tensor", "d_wo%d" % par)
            for tch in range(4):
                pe = psum[2 + par]
                for k in range(8):
                    mmw = [("vv", None), ("woev", None)] if k == 0 else None
                    pb.op("tensor", "matmul", pe[:, 0:512],
                          wstr[:, par * 2048 + k * 128:
                               par * 2048 + (k + 1) * 128],
                          xfmB[:, k * 2048 + tch * 512:
                               k * 2048 + tch * 512 + 512],
                          start=(k == 0), stop=(k == 7),
                          wait=mmw, inc="womm")
                bcol = bql[:, 24 + et:24 + et + 1]
                pb.op("scalar", "activation",
                      wait=[("womm", None), ("d_l1c", None)],
                      inc="wos", out=sq32[:, 0:512], in_=pe[:, 0:512],
                      func=AF.Gelu_apprx_tanh, bias=bcol, scale=1.0)
                pb.op("scalar", "activation", inc="woev",
                      out=h_fm2[:, et * 2048 + tch * 512:
                                et * 2048 + tch * 512 + 512],
                      in_=sq32[:, 0:512], func=AF.Gelu_apprx_tanh)

        for j in range(16):
            for k in range(8):
                pk = psum[4 + (k % 2)]
                tp_wait = [("woev", None)]
                if pb.val("lntpe") >= 2:
                    tp_wait.append(("lntpe", pb.val("lntpe") - 1))
                pb.op("tensor", "transpose", pk[:, 0:128],
                      h_fm2[:, k * 2048 + j * 128: k * 2048 + (j + 1) * 128],
                      eye_sb[:], wait=tp_wait, inc="lntpp")
                pb.op("vector", "tensor_tensor", wait=[("lntpp", None)],
                      inc="lntpe",
                      out=stream[:, j * 1024 + k * 128: j * 1024 + (k + 1) * 128],
                      in0=stream[:, j * 1024 + k * 128:
                                 j * 1024 + (k + 1) * 128],
                      in1=pk[:, 0:128], op=ALU.add)

        pb.dma("gpsimd", sbc[:, 0:1024],
               l1_lnsb[blk, 0].broadcast_to([128, 1024]), "d_sbc",
               wait=[("lnsb", None), ("q8ev", None), ("lnap", None)])
        pb.dma("gpsimd", sbc[:, 1024:2048],
               l1_lnsb[blk, 1].broadcast_to([128, 1024]), "d_sbc")
        stream_ln([("lntpe", None), ("d_dbg", None)], inplace=True)
        for j in range(16):
            pb.op("vector", "tensor_tensor", wait=[("d_sbc", None)],
                  inc="lnsb", out=stream[:, j * 1024:(j + 1) * 1024],
                  in0=stream[:, j * 1024:(j + 1) * 1024],
                  in1=sbc[:, 0:1024], op=ALU.mult)
            pb.op("vector", "tensor_tensor", inc="lnsb",
                  out=stream[:, j * 1024:(j + 1) * 1024],
                  in0=stream[:, j * 1024:(j + 1) * 1024],
                  in1=sbc[:, 1024:2048], op=ALU.add)

        stream_to_fm([("lnsb", None)], xfmA)
        for tch in range(4):
            ts0 = tch * 512
            for ft in range(32):
                par = ft % 2
                g = []
                if pb.val("fcmm") >= 16:
                    g.append(("fcmm", pb.val("fcmm") - 8))
                if ft < 2:
                    g.append(("pjev", None))
                    if tch == 0:
                        g.append(("womm", None))
                pb.op("gpsimd", "dma_start", inc="d_fc%d" % par, winc=16,
                      wait=g or None,
                      out=wstr[:, par * 2048:par * 2048 + 1024],
                      in_=l1_w1[blk, ft])
                pb.wait("tensor", "d_fc%d" % par)
                pb.wait("tensor", "lntpe")
                for k in range(8):
                    mmw = None
                    if k == 0:
                        mmw = [("fcev", None), ("pjmm", None), ("pjev", None)]
                    pb.op("tensor", "matmul", psum[par][:, 0:512],
                          wstr[:, par * 2048 + k * 128:
                               par * 2048 + (k + 1) * 128],
                          xfmA[:, k * 2048 + ts0: k * 2048 + ts0 + 512],
                          start=(k == 0), stop=(k == 7), wait=mmw, inc="fcmm")
                pb.op("scalar", "activation",
                      wait=[("fcmm", None), ("pjmm", None), ("d_l1c", None)],
                      inc="fcev",
                      out=h_fm2[:, ft * 512:(ft + 1) * 512],
                      in_=psum[par][:, 0:512], func=AF.Gelu_apprx_tanh,
                      bias=b1l[:, ft:ft + 1], scale=1.0)
            for wave in range(2):
                for k in range(32):
                    par = k % 2
                    g = []
                    if pb.val("pjmm") >= 8:
                        g.append(("pjmm", pb.val("pjmm") - 4))
                    pb.op("gpsimd", "dma_start", inc="d_pj%d" % par, winc=16,
                          wait=g or None,
                          out=wstr[:, par * 2048 + 1024:par * 2048 + 2048],
                          in_=l1_w2[blk, k * 128:(k + 1) * 128, :])
                    pb.wait("tensor", "d_pj%d" % par)
                    for tt in range(2):
                        for eh in range(2):
                            mmw = None
                            if k == 0:
                                mmw = [("fcev", None), ("pjev", None),
                                       ("c2v", None), ("d_l1cg", None)]
                            pb.op("tensor", "matmul",
                                  psum[tt * 2 + eh][:, 0:512],
                                  h_fm2[:, k * 512 + (wave * 2 + tt) * 128:
                                        k * 512 + (wave * 2 + tt + 1) * 128],
                                  wstr[:, par * 2048 + 1024 + eh * 512:
                                       par * 2048 + 1024 + eh * 512 + 512],
                                  start=(k == 0), stop=False, wait=mmw,
                                  inc="pjmm")
                for tt in range(2):
                    for eh in range(2):
                        pb.op("tensor", "matmul", psum[tt * 2 + eh][:, 0:512],
                              ones_row[:],
                              b2r_b[:, eh * 512:eh * 512 + 512],
                              start=False, stop=True, inc="pjmm")
                for tt in range(2):
                    jj = 4 * tch + wave * 2 + tt
                    for eh in range(2):
                        slot = (tt * 2 + eh) % 2
                        gw = [("pjmm", None)]
                        if pb.val("pjev") >= 2:
                            gw.append(("pjev", pb.val("pjev") - 1))
                        pb.op("scalar", "activation", wait=gw, inc="pjs",
                              out=sq32[:, slot * 512:slot * 512 + 512],
                              in_=psum[tt * 2 + eh][:, 0:512],
                              func=AF.Gelu_apprx_tanh)
                        pb.op("vector", "tensor_tensor", wait=[("pjs", None)],
                              inc="pjev",
                              out=stream[:, jj * 1024 + eh * 512:
                                         jj * 1024 + eh * 512 + 512],
                              in0=stream[:, jj * 1024 + eh * 512:
                                         jj * 1024 + eh * 512 + 512],
                              in1=sq32[:, slot * 512:slot * 512 + 512],
                              op=ALU.add)

        pb.dma("gpsimd", sbc[:, 0:1024],
               l1_lnsb[blk, 2].broadcast_to([128, 1024]), "d_sbc",
               wait=[("lnsb", None)])
        pb.dma("gpsimd", sbc[:, 1024:2048],
               l1_lnsb[blk, 3].broadcast_to([128, 1024]), "d_sbc")
        stream_ln([("pjev", None), ("d_dbg", None)], inplace=True)
        for j in range(16):
            pb.op("vector", "tensor_tensor", wait=[("d_sbc", None)],
                  inc="lnsb", out=stream[:, j * 1024:(j + 1) * 1024],
                  in0=stream[:, j * 1024:(j + 1) * 1024],
                  in1=sbc[:, 0:1024], op=ALU.mult)
            pb.op("vector", "tensor_tensor", inc="lnsb",
                  out=stream[:, j * 1024:(j + 1) * 1024],
                  in0=stream[:, j * 1024:(j + 1) * 1024],
                  in1=sbc[:, 1024:2048], op=ALU.add)

    for j in range(16):
        pb.dma("gpsimd", sq32[:], xl1_hbm.ap()[j * 128:(j + 1) * 128, :],
               "d_so3", wait=[("res3", None), ("lnsb", None)])
        pb.op("vector", "tensor_tensor", wait=[("lnsb", None), ("d_so3", None)],
              inc="res3", out=stream[:, j * 1024:(j + 1) * 1024],
              in0=stream[:, j * 1024:(j + 1) * 1024], in1=sq32[:], op=ALU.add)
    stream_ln([("res3", None), ("d_dbg", None)], inplace=True)
    if "after_l1attn" in dbg:
        for j in range(16):
            pb.dma("sync", dbg["after_l1attn"][j * 128:(j + 1) * 128, :],
                   stream[:, j * 1024:(j + 1) * 1024], "d_dbg",
                   wait=[("lnap", None)])
    dstack.close()


    estack = contextlib.ExitStack()

    def ebt(name, shape, dt=F32):
        return estack.enter_context(nc.sbuf_tensor(name, list(shape), dt))

    wbig = ebt("wbig", [128, 24 * 1024], BF16)
    wt32 = ebt("wt32", [128, 3072])
    wt32b = ebt("wt32b", [128, 3072])
    g32 = ebt("g32", [128, 3072])
    rmax = ebt("rmax", [128, 96])
    rinv = ebt("rinv", [128, 96])
    r7 = ebt("r7", [128, 96])
    s7 = ebt("s7", [128, 96])
    wqb = ebt("wqb", [128, 3072], BF16)
    eye32_sb = ebt("eye32_sb", [128, 128])
    gcol = ebt("gcol", [128, 16])

    wupdq_hbm = nc.dram_tensor("wupdq_hbm", [1024, 3072], BF16)
    wgdq_hbm = nc.dram_tensor("wgdq_hbm", [1024, 3072], BF16)
    wddq_hbm = nc.dram_tensor("wddq_hbm", [3072, 1024], BF16)
    up_hbm = nc.dram_tensor("up_hbm", [TOWN, 3072], F32)
    gate_hbm = nc.dram_tensor("gate_hbm", [TOWN, 3072], F32)
    g2_hbm = nc.dram_tensor("g2_hbm", [TOWN, 3072], F32)
    upq_hbm = nc.dram_tensor("upq_hbm", [TOWN, 3072], F32)
    qs_part = nc.dram_tensor("qs_part", [1, 4], F32)
    qs_sum = nc.dram_tensor("qs_sum", [1, 4], F32)

    o_strip = xfmA
    pb.dma("sync", eye32_sb[:], eye128f, "d_ey", wait=[("lnap", None)])
    pb.op("vector", "memset", gcol[:, 0:8], 0.0, wait=[("lnap", None)],
          inc="einit")
    pb.dma("sync", qs_part.ap()[0:1, 0:4], gcol[0:1, 0:4], "d_qsp",
           wait=[("einit", None)])

    def dequant_weight(src_ap, ntiles, nfree, dst_hbm, tag):
        nblk = nfree // 32
        for pt in range(ntiles):
            pb.dma("gpsimd", wt32[:, 0:nfree],
                   src_ap[pt * 128:(pt + 1) * 128, :], "d_wt",
                   wait=[("dqv", None), ("lnap", None)])
            pb.op("vector", "tensor_reduce", wait=[("d_wt", None)], inc="dqv",
                  out=rmax[:, 0:nblk],
                  in_=wt32[:, 0:nfree].rearrange("p (b t) -> p b t", t=32),
                  axis=AXX, op=ALU.max, apply_absolute_value=True)
            pb.op("vector", "reciprocal", inc="dqv", out=rinv[:, 0:nblk],
                  in_=rmax[:, 0:nblk])
            pb.op("vector", "tensor_scalar", inc="dqv", out=r7[:, 0:nblk],
                  in0=rinv[:, 0:nblk], scalar1=7.0, scalar2=0.0,
                  op0=ALU.mult, op1=ALU.add)
            pb.op("vector", "tensor_scalar", inc="dqv", out=s7[:, 0:nblk],
                  in0=rmax[:, 0:nblk], scalar1=1.0 / 7.0, scalar2=0.0,
                  op0=ALU.mult, op1=ALU.add)
            pb.op("vector", "tensor_tensor", inc="dqv", out=wt32b[:, 0:nfree]
                  .rearrange("p (b t) -> p b t", t=32),
                  in0=wt32[:, 0:nfree].rearrange("p (b t) -> p b t", t=32),
                  in1=r7[:, 0:nblk].rearrange("p (b t) -> p b t", t=1)
                  .broadcast_to([128, nblk, 32]), op=ALU.mult)
            pb.op("vector", "tensor_scalar", inc="dqv", out=wt32b[:, 0:nfree],
                  in0=wt32b[:, 0:nfree], scalar1=MAGIC, scalar2=MAGIC,
                  op0=ALU.add, op1=ALU.subtract)
            pb.op("vector", "tensor_tensor", inc="dqv", out=wqb[:, 0:nfree]
                  .rearrange("p (b t) -> p b t", t=32),
                  in0=wt32b[:, 0:nfree].rearrange("p (b t) -> p b t", t=32),
                  in1=s7[:, 0:nblk].rearrange("p (b t) -> p b t", t=1)
                  .broadcast_to([128, nblk, 32]), op=ALU.mult)
            for fc in range(nfree // 128):
                pk = psum[4 + (fc % 2)]
                tp_wait = [("dqv", None)]
                if pb.val("dqtp") >= 2:
                    tp_wait.append(("dqe", pb.val("dqe") - 1))
                pb.op("tensor", "transpose", pk[:, 0:128],
                      wqb[:, fc * 128:(fc + 1) * 128], eye_sb[:],
                      wait=tp_wait, inc="dqtp")
                pb.op("vector", "tensor_copy",
                      wait=[("dqtp", None), ("d_dq", None)],
                      inc="dqe", out=xnb16[:, 0:128], in_=pk[:, 0:128])
                pb.dma("sync", dst_hbm.ap()[fc * 128:(fc + 1) * 128,
                                            pt * 128:(pt + 1) * 128],
                       xnb16[:, 0:128], "d_dq", wait=[("dqe", None)])

    dequant_weight(wup_in, 24, 1024, wupdq_hbm, "up")
    dequant_weight(wg_in, 24, 1024, wgdq_hbm, "g")
    dequant_weight(wd_in, 8, 3072, wddq_hbm, "d")

    for j in range(16):
        pb.op("vector", "tensor_reduce",
              wait=[("lnap", None), ("d_dbg", None), ("d_qsp", None)],
              inc="gxv", out=gcol[:, 0:1],
              in_=stream[:, j * 1024:(j + 1) * 1024], axis=AXX, op=ALU.max,
              apply_absolute_value=True)
        if j == 0:
            pb.op("vector", "tensor_copy", inc="gxv", out=gcol[:, 1:2],
                  in_=gcol[:, 0:1])
        else:
            pb.op("vector", "tensor_tensor", inc="gxv", out=gcol[:, 1:2],
                  in0=gcol[:, 1:2], in1=gcol[:, 0:1], op=ALU.max)

    def part_max_to_slot(src_col, slot, tag):
        pb.op("tensor", "transpose", psum[2][0:1, 0:128], src_col, eye32_sb[:],
              wait=[(tag, None), ("pmx", None), ("d_ey", None)], inc="pmt")
        pb.op("vector", "tensor_copy",
              wait=[("pmt", None), ("d_ug", None), ("d_p1o", None),
                    ("d_p2o", None), ("d_wt", None)],
              inc="pmx",
              out=wt32[0:1, 0:128], in_=psum[2][0:1, 0:128])
        pb.op("vector", "tensor_reduce", inc="pmx", out=wt32[0:1, 128:129],
              in_=wt32[0:1, 0:128], axis=AXX, op=ALU.max)
        pb.dma("sync", qs_part.ap()[0:1, slot:slot + 1], wt32[0:1, 128:129],
               "d_qsp", wait=[("pmx", None), ("cc_q", None)])

    def ar_max():
        pb.wait("gpsimd", "d_qsp")
        pb.wait("gpsimd", "d_qsb")
        pb.op("gpsimd", "collective_compute", "AllReduce", ALU.max,
              replica_groups=ALLG, ins=[qs_part.ap().opt()],
              outs=[qs_sum.ap().opt()], inc="cc_q")

    part_max_to_slot(gcol[:, 1:2], 0, "gxv")
    ar_max()
    pb.dma("gpsimd", gcol[:, 2:3], qs_sum.ap()[0:1, 0:1].broadcast_to([128, 1]),
           "d_qsb", wait=[("cc_q", None)])
    pb.op("vector", "reciprocal", wait=[("d_qsb", None)], inc="qsv",
          out=gcol[:, 3:4], in_=gcol[:, 2:3])
    pb.op("vector", "tensor_scalar", inc="qsv", out=gcol[:, 3:4],
          in0=gcol[:, 3:4], scalar1=32767.0, scalar2=0.0, op0=ALU.mult,
          op1=ALU.add)
    pb.op("vector", "tensor_scalar", inc="qsv", out=gcol[:, 4:5],
          in0=gcol[:, 2:3], scalar1=1.0 / 32767.0, scalar2=0.0, op0=ALU.mult,
          op1=ALU.add)

    for j in range(16):
        pb.op("vector", "tensor_scalar", wait=[("qsv", None)], inc="xqv",
              out=sq32[:], in0=stream[:, j * 1024:(j + 1) * 1024],
              scalar1=gcol[:, 3:4], scalar2=MAGIC, op0=ALU.mult, op1=ALU.add)
        pb.op("vector", "tensor_scalar", inc="xqv", out=sq32[:],
              in0=sq32[:], scalar1=MAGIC, scalar2=0.0, op0=ALU.subtract,
              op1=ALU.add)
        pb.op("vector", "tensor_copy", inc="xqv", out=xnb16[:], in_=sq32[:],
              wait=[("d_dq", None)])
        for k in range(8):
            pk = psum[4 + (k % 2)]
            tp_wait = [("xqv", None)]
            if pb.val("lntpe") >= 2:
                tp_wait.append(("lntpe", pb.val("lntpe") - 1))
            pb.op("tensor", "transpose", pk[:, 0:128],
                  xnb16[:, k * 128:(k + 1) * 128], eye_sb[:],
                  wait=tp_wait, inc="lntpp")
            pb.op("vector", "tensor_copy", wait=[("lntpp", None)],
                  inc="lntpe",
                  out=xfmA[:, k * 2048 + j * 128: k * 2048 + (j + 1) * 128],
                  in_=pk[:, 0:128])

    def upgate(dq_hbm, out_hbm, maxslot, tag):
        pb.op("gpsimd", "dma_start", inc="d_wb", winc=16,
              out=wbig[:].rearrange("p (c e) -> p c e", c=8),
              in_=dq_hbm.ap().rearrange("(c p) e -> p c e", p=128),
              wait=[("upmm", None), ("gamm", None), ("lnap", None)])
        pb.wait("tensor", "d_wb")
        pb.op("vector", "memset", gcol[:, 5:6], 0.0, inc=tag + "m")
        for j in range(16):
            for ih in range(6):
                pe = psum[ih % 2]
                for k in range(8):
                    mmw = None
                    if k == 0:
                        mmw = [("lntpe", None), (tag + "e", None)]
                    pb.op("tensor", "matmul", pe[:, 0:512],
                          xfmA[:, k * 2048 + j * 128: k * 2048 + (j + 1) * 128],
                          wbig[:, k * 3072 + ih * 512: k * 3072 + ih * 512 + 512],
                          start=(k == 0), stop=(k == 7), wait=mmw,
                          inc=tag + "mm")
                pb.op("vector", "tensor_scalar",
                      wait=[(tag + "mm", None), ("d_ug", None),
                            ("d_qsp", None)],
                      inc=tag + "e", out=wt32[:, 0:512], in0=pe[:, 0:512],
                      scalar1=gcol[:, 4:5], scalar2=0.0, op0=ALU.mult,
                      op1=ALU.add)
                pb.op("vector", "tensor_reduce", inc=tag + "e",
                      out=gcol[:, 6:7], in_=wt32[:, 0:512], axis=AXX,
                      op=ALU.max, apply_absolute_value=True)
                pb.op("vector", "tensor_tensor", inc=tag + "m",
                      out=gcol[:, 5:6], in0=gcol[:, 5:6], in1=gcol[:, 6:7],
                      op=ALU.max)
                pb.dma("sync", out_hbm.ap()[j * 128:(j + 1) * 128,
                                            ih * 512:ih * 512 + 512],
                       wt32[:, 0:512], "d_ug", wait=[(tag + "e", None)])
        part_max_to_slot(gcol[:, 5:6], maxslot, tag + "m")

    upgate(wupdq_hbm, up_hbm, 1, "up")
    upgate(wgdq_hbm, gate_hbm, 2, "ga")
    ar_max()
    for slot, rc, sc in ((1, 8, 9), (2, 10, 11)):
        pb.dma("gpsimd", gcol[:, 7:8],
               qs_sum.ap()[0:1, slot:slot + 1].broadcast_to([128, 1]), "d_qsb",
               wait=[("cc_q", None), ("qsv", None)])
        pb.op("vector", "reciprocal", wait=[("d_qsb", None)], inc="qsv",
              out=gcol[:, rc:rc + 1], in_=gcol[:, 7:8])
        pb.op("vector", "tensor_scalar", inc="qsv", out=gcol[:, rc:rc + 1],
              in0=gcol[:, rc:rc + 1], scalar1=32767.0, scalar2=0.0,
              op0=ALU.mult, op1=ALU.add)
        pb.op("vector", "tensor_scalar", inc="qsv", out=gcol[:, sc:sc + 1],
              in0=gcol[:, 7:8], scalar1=1.0 / 32767.0, scalar2=0.0,
              op0=ALU.mult, op1=ALU.add)

    pb.op("vector", "memset", gcol[:, 6:7], 0.0, wait=[("qsv", None)],
          inc="p1m")
    for j in range(16):
        pb.dma("gpsimd", wt32[:], up_hbm.ap()[j * 128:(j + 1) * 128, :],
               "d_p1", wait=[("p1v", None), ("d_p1o", None), ("d_ug", None),
                             ("d_qsp", None)])
        pb.dma("gpsimd", g32[:], gate_hbm.ap()[j * 128:(j + 1) * 128, :],
               "d_p1")
        pb.op("vector", "tensor_scalar", wait=[("d_p1", None)], inc="p1v",
              out=wt32[:], in0=wt32[:], scalar1=gcol[:, 8:9], scalar2=MAGIC,
              op0=ALU.mult, op1=ALU.add)
        pb.op("vector", "tensor_scalar", inc="p1v", out=wt32[:], in0=wt32[:],
              scalar1=MAGIC, scalar2=gcol[:, 9:10], op0=ALU.subtract,
              op1=ALU.mult)
        pb.op("vector", "tensor_scalar", inc="p1v", out=g32[:], in0=g32[:],
              scalar1=gcol[:, 10:11], scalar2=MAGIC, op0=ALU.mult,
              op1=ALU.add)
        pb.op("vector", "tensor_scalar", inc="p1v", out=g32[:], in0=g32[:],
              scalar1=MAGIC, scalar2=gcol[:, 11:12], op0=ALU.subtract,
              op1=ALU.mult)
        pb.op("scalar", "activation", wait=[("p1v", None)], inc="p1s",
              out=wt32b[:], in_=g32[:], func=AF.Sigmoid)
        pb.op("vector", "tensor_scalar", wait=[("p1s", None)], inc="p1v",
              out=wt32b[:], in0=wt32b[:], scalar1=65536.0, scalar2=MAGIC,
              op0=ALU.mult, op1=ALU.add)
        pb.op("vector", "tensor_scalar", inc="p1v", out=wt32b[:],
              in0=wt32b[:], scalar1=MAGIC, scalar2=65535.0,
              op0=ALU.subtract, op1=ALU.min)
        pb.op("vector", "tensor_scalar", inc="p1v", out=wt32b[:],
              in0=wt32b[:], scalar1=1.0 / 65536.0, scalar2=0.0,
              op0=ALU.mult, op1=ALU.add)
        pb.op("vector", "tensor_tensor", inc="p1v", out=g32[:], in0=g32[:],
              in1=wt32b[:], op=ALU.mult)
        pb.op("vector", "tensor_reduce", inc="p1v", out=gcol[:, 7:8],
              in_=g32[:], axis=AXX, op=ALU.max, apply_absolute_value=True)
        pb.op("vector", "tensor_tensor", inc="p1m", out=gcol[:, 6:7],
              in0=gcol[:, 6:7], in1=gcol[:, 7:8], op=ALU.max)
        pb.dma("sync", g2_hbm.ap()[j * 128:(j + 1) * 128, :], g32[:], "d_p1o",
               wait=[("p1v", None)])
        pb.dma("sync", upq_hbm.ap()[j * 128:(j + 1) * 128, :], wt32[:],
               "d_p1o")
    part_max_to_slot(gcol[:, 6:7], 3, "p1m")
    ar_max()
    pb.dma("gpsimd", gcol[:, 7:8],
           qs_sum.ap()[0:1, 3:4].broadcast_to([128, 1]), "d_qsb",
           wait=[("cc_q", None), ("qsv", None)])
    pb.op("vector", "reciprocal", wait=[("d_qsb", None)], inc="qsv",
          out=gcol[:, 12:13], in_=gcol[:, 7:8])
    pb.op("vector", "tensor_scalar", inc="qsv", out=gcol[:, 12:13],
          in0=gcol[:, 12:13], scalar1=32767.0, scalar2=0.0, op0=ALU.mult,
          op1=ALU.add)
    pb.op("vector", "tensor_scalar", inc="qsv", out=gcol[:, 13:14],
          in0=gcol[:, 7:8], scalar1=1.0 / 32767.0, scalar2=0.0,
          op0=ALU.mult, op1=ALU.add)

    pb.op("vector", "memset", gcol[:, 5:6], 0.0, wait=[("qsv", None)],
          inc="p2m")
    for j in range(16):
        pb.dma("gpsimd", g32[:], g2_hbm.ap()[j * 128:(j + 1) * 128, :],
               "d_p2", wait=[("p2v", None), ("d_p2o", None), ("d_p1o", None),
                             ("d_qsp", None)])
        pb.dma("gpsimd", wt32[:], upq_hbm.ap()[j * 128:(j + 1) * 128, :],
               "d_p2")
        pb.op("vector", "tensor_scalar", wait=[("d_p2", None), ("d_ug", None)],
              inc="p2v", out=g32[:], in0=g32[:], scalar1=gcol[:, 12:13],
              scalar2=MAGIC, op0=ALU.mult, op1=ALU.add)
        pb.op("vector", "tensor_scalar", inc="p2v", out=g32[:], in0=g32[:],
              scalar1=MAGIC, scalar2=gcol[:, 13:14], op0=ALU.subtract,
              op1=ALU.mult)
        pb.op("vector", "tensor_tensor", inc="p2v", out=g32[:], in0=g32[:],
              in1=wt32[:], op=ALU.mult)
        pb.op("vector", "tensor_reduce", inc="p2v", out=gcol[:, 7:8],
              in_=g32[:], axis=AXX, op=ALU.max, apply_absolute_value=True)
        pb.op("vector", "tensor_tensor", inc="p2m", out=gcol[:, 5:6],
              in0=gcol[:, 5:6], in1=gcol[:, 7:8], op=ALU.max)
        pb.dma("sync", up_hbm.ap()[j * 128:(j + 1) * 128, :], g32[:], "d_p2o",
               wait=[("p2v", None)])
    part_max_to_slot(gcol[:, 5:6], 0, "p2m")
    ar_max()
    pb.dma("gpsimd", gcol[:, 7:8],
           qs_sum.ap()[0:1, 0:1].broadcast_to([128, 1]), "d_qsb",
           wait=[("cc_q", None), ("qsv", None)])
    pb.op("vector", "reciprocal", wait=[("d_qsb", None)], inc="qsv",
          out=gcol[:, 14:15], in_=gcol[:, 7:8])
    pb.op("vector", "tensor_scalar", inc="qsv", out=gcol[:, 14:15],
          in0=gcol[:, 14:15], scalar1=32767.0, scalar2=0.0, op0=ALU.mult,
          op1=ALU.add)
    pb.op("vector", "tensor_scalar", inc="qsv", out=gcol[:, 15:16],
          in0=gcol[:, 7:8], scalar1=1.0 / 32767.0, scalar2=0.0,
          op0=ALU.mult, op1=ALU.add)

    pb.op("gpsimd", "dma_start", inc="d_wd", winc=16,
          out=wbig[:].rearrange("p (c e) -> p c e", c=24),
          in_=wddq_hbm.ap().rearrange("(c p) e -> p c e", p=128),
          wait=[("upmm", None), ("gamm", None)])
    pb.wait("tensor", "d_wd")
    for j in range(16):
        pb.dma("gpsimd", g32[:], up_hbm.ap()[j * 128:(j + 1) * 128, :],
               "d_p3", wait=[("p3v", None), ("d_p2o", None), ("d_qsp", None)])
        pb.op("vector", "tensor_scalar", wait=[("d_p3", None), ("p2v", None)],
              inc="p3v", out=g32[:], in0=g32[:], scalar1=gcol[:, 14:15],
              scalar2=MAGIC, op0=ALU.mult, op1=ALU.add)
        pb.op("vector", "tensor_scalar", inc="p3v", out=g32[:], in0=g32[:],
              scalar1=MAGIC, scalar2=gcol[:, 15:16], op0=ALU.subtract,
              op1=ALU.mult)
        pb.op("vector", "tensor_copy", inc="p3v", out=o_strip[:, 0:3072]
              if False else wqb[:, 0:3072], in_=g32[:])
        for fc in range(24):
            pk = psum[4 + (fc % 2)]
            tp_wait = [("p3v", None)]
            if pb.val("p3tp") >= 2:
                tp_wait.append(("p3e", pb.val("p3e") - 1))
            pb.op("tensor", "transpose", pk[:, 0:128],
                  wqb[:, fc * 128:(fc + 1) * 128], eye_sb[:],
                  wait=tp_wait, inc="p3tp")
            pb.op("vector", "tensor_copy",
                  wait=[("p3tp", None), ("upmm", None), ("gamm", None)],
                  inc="p3e",
                  out=o_strip[:, fc * 128:(fc + 1) * 128], in_=pk[:, 0:128])
        for eh in range(2):
            pe = psum[eh]
            for k in range(24):
                mmw = None
                if k == 0:
                    mmw = [("p3e", None), ("dnev", None)]
                pb.op("tensor", "matmul", pe[:, 0:512],
                      o_strip[:, k * 128:(k + 1) * 128],
                      wbig[:, k * 1024 + eh * 512: k * 1024 + eh * 512 + 512],
                      start=(k == 0), stop=(k == 23), wait=mmw, inc="dnmm")
            pb.op("vector", "tensor_tensor", wait=[("dnmm", None)], inc="dnev",
                  out=stream[:, j * 1024 + eh * 512: j * 1024 + eh * 512 + 512],
                  in0=stream[:, j * 1024 + eh * 512: j * 1024 + eh * 512 + 512],
                  in1=pe[:, 0:512], op=ALU.add)

    stream_ln([("dnev", None), ("d_dbg", None)], inplace=True)
    estack.close()

    for j in range(16):
        pb.dma("gpsimd", out[j * 128:(j + 1) * 128, :],
               stream[:, j * 1024:(j + 1) * 1024], "d_out",
               wait=[("lnap", None)])
    pb.wait("gpsimd", "d_out")
    if "after_mixer" in dbg:
        pb.wait("gpsimd", "d_dbg")

    pb.replay(stack.enter_context(nc.Block()))
    stack.close()
    return nc



def _prep_core_inputs(c, I):
    h, b = c % 2, c // 2
    f32 = np.float32
    perm = np.concatenate([np.arange(h * DH, (h + 1) * DH),
                           np.arange((1 - h) * DH, (2 - h) * DH)])
    own = perm[:DH]
    ipw = np.asarray(I["L0_attn_in_proj_w"], f32)
    xr_w = ipw[:DI][perm]
    z_w = ipw[DI:][own]
    winT_full = np.concatenate([xr_w, z_w], 0).T
    winT = np.ascontiguousarray(
        winT_full.reshape(8, 128, 24, 128).transpose(2, 1, 0, 3)
        .reshape(24, 128, 1024))
    conv_w = np.asarray(I["L0_attn_conv_w"], f32)[perm, 0, :]
    conv_w_cols = np.ascontiguousarray(
        conv_w.reshape(16, 128, 4).transpose(1, 0, 2).reshape(128, 64))
    conv_b_cols = np.ascontiguousarray(
        np.asarray(I["L0_attn_conv_b"], f32)[perm].reshape(16, 128).T)
    ln1w_col = np.ascontiguousarray(
        np.asarray(I["L0_attn_ln1_w"], f32).reshape(8, 128).T)
    xpw = np.asarray(I["L0_attn_x_proj_w"], f32)[:, perm]
    xpw_pad = np.zeros((112, DI), f32)
    xpw_pad[0:80] = xpw[0:80]
    xpw_pad[96:112] = xpw[80:96]
    xprojT = np.ascontiguousarray(xpw_pad.T)
    dtwT = np.ascontiguousarray(np.asarray(I["L0_attn_dt_proj_w"], f32)[own].T)
    dtb_col = np.ascontiguousarray(
        np.asarray(I["L0_attn_dt_proj_b"], f32)[own].reshape(8, 128).T)
    alog_scan = np.ascontiguousarray(
        np.asarray(I["L0_attn_A_log"], f32)[own].reshape(128, 128).T)
    dp_col = np.ascontiguousarray(
        np.asarray(I["L0_attn_Dp"], f32)[own].reshape(8, 128).T)
    woutT = np.ascontiguousarray(
        np.asarray(I["L0_attn_out_proj_w"], f32)[:, own].T)

    bf = ml_dtypes.bfloat16
    sel16 = np.zeros((16, 128), bf)
    for k in range(16):
        sel16[k, np.arange(k, 128, 16)] = 1
    sely = np.zeros((128, 16 * 128), bf)
    for s in range(16):
        for p in range(128):
            sely[p, s * 128 + s * 8 + p // 16] = 1
    eye128 = np.eye(128, dtype=bf)

    fcT = np.asarray(I["L0_attn_fc_w"], f32).T
    fcT_tiled = np.ascontiguousarray(
        fcT.reshape(8, 128, 32, 128).transpose(2, 1, 0, 3).reshape(32, 128, 1024))
    projT = np.ascontiguousarray(np.asarray(I["L0_attn_proj_w"], f32).T)
    w8T = np.ascontiguousarray(
        np.asarray(I["L0_ffn_weight"]).T.astype(ml_dtypes.bfloat16))
    ws_in = np.asarray(I["L0_ffn_weight_scale"], f32).reshape(1, 1)
    bias_row_in = np.ascontiguousarray(
        np.asarray(I["L0_ffn_bias"], f32).reshape(1, 1024))
    ln2w_col_in = np.ascontiguousarray(
        np.asarray(I["L0_attn_ln2_w"], f32).reshape(8, 128).T)

    def tile_w(w):
        return w.reshape(8, 128, 8, 128).transpose(2, 1, 0, 3).reshape(8, 128, 1024)

    l1_qkvo = np.zeros((4, 4, 8, 128, 1024), f32)
    for i in range(4):
        l1_qkvo[i, 0] = tile_w(np.asarray(I["L1_attn_Wq"], f32)[i])
        l1_qkvo[i, 1] = tile_w(np.asarray(I["L1_attn_Wk"], f32)[i])
        l1_qkvo[i, 2] = tile_w(np.asarray(I["L1_attn_Wv"], f32)[i])
        l1_qkvo[i, 3] = tile_w(np.asarray(I["L1_attn_Wo"], f32)[i])
    l1_w1 = np.zeros((4, 32, 128, 1024), f32)
    for i in range(4):
        w1 = np.asarray(I["L1_attn_W1"], f32)[i]
        l1_w1[i] = w1.reshape(8, 128, 32, 128).transpose(2, 1, 0, 3) \
            .reshape(32, 128, 1024)
    l1_w2 = np.ascontiguousarray(np.asarray(I["L1_attn_W2"], f32))
    l1_bqkvo = np.zeros((4, 4, 128, 8), f32)
    for i in range(4):
        for wi, nm in enumerate(("bq", "bk", "bv", "bo")):
            l1_bqkvo[i, wi] = np.asarray(I["L1_attn_" + nm], f32)[i] \
                .reshape(8, 128).T
    l1_b1c = np.zeros((4, 128, 32), f32)
    for i in range(4):
        l1_b1c[i] = np.asarray(I["L1_attn_b1"], f32)[i].reshape(32, 128).T
    l1_b2r = np.asarray(I["L1_attn_b2"], f32).reshape(4, 1, 1024)
    l1_lnsb = np.stack([
        np.stack([np.asarray(I["L1_attn_ln1_s"], f32)[i].reshape(1, 1024),
                  np.asarray(I["L1_attn_ln1_b"], f32)[i].reshape(1, 1024),
                  np.asarray(I["L1_attn_ln2_s"], f32)[i].reshape(1, 1024),
                  np.asarray(I["L1_attn_ln2_b"], f32)[i].reshape(1, 1024)])
        for i in range(4)])

    x = np.asarray(I["L0_attn_x"], f32)
    return {
        "eye128f": np.eye(128, dtype=f32),
        "wup_in": np.ascontiguousarray(np.asarray(I["L1_ffn_w_gate"], f32)) if False
        else np.ascontiguousarray(np.asarray(I["L1_ffn_w_up"], f32)),
        "wg_in": np.ascontiguousarray(np.asarray(I["L1_ffn_w_gate"], f32)),
        "wd_in": np.ascontiguousarray(np.asarray(I["L1_ffn_w_down"], f32)),
        "l1_qkvo": l1_qkvo, "l1_w1": l1_w1, "l1_w2": l1_w2,
        "l1_bqkvo": l1_bqkvo, "l1_b1c": l1_b1c, "l1_b2r": l1_b2r,
        "l1_lnsb": np.ascontiguousarray(l1_lnsb),
        "fcT_tiled": fcT_tiled, "projT": projT, "w8T": w8T, "ws_in": ws_in,
        "bias_row_in": bias_row_in, "ln2w_col_in": ln2w_col_in,
        "x_batch": np.ascontiguousarray(x[b]),
        "x_own": np.ascontiguousarray(x[b, h * TOWN:(h + 1) * TOWN]),
        "winT": winT, "conv_w_cols": conv_w_cols, "conv_b_cols": conv_b_cols,
        "ln1w_col": ln1w_col, "xprojT": xprojT, "dtwT": dtwT,
        "dtb_col": dtb_col, "alog_scan": alog_scan, "dp_col": dp_col,
        "woutT": woutT, "sel16": sel16, "sely": sely, "eye128": eye128,
    }


_DEBUG = ()


def kernel(**inputs):
    nc = build_program(debug=_DEBUG)
    in_maps = [_prep_core_inputs(c, inputs) for c in range(NC)]
    res = run_bass_kernel_spmd(nc, in_maps, core_ids=list(range(NC)))
    outs = [np.asarray(res.results[c]["out"]) for c in range(NC)]
    full = np.stack(outs).reshape(OUT_SHAPE)
    if _DEBUG:
        kernel._dbg = [{k: np.asarray(v) for k, v in res.results[c].items()}
                       for c in range(NC)]
    return full


# version 02
# speedup vs baseline: 1.1884x; 1.1884x over previous
import contextlib

import numpy as np
import ml_dtypes
import concourse.bass as bass
import concourse.mybir as mybir
from concourse.bass_utils import run_bass_kernel_spmd

F32 = mybir.dt.float32
BF16 = mybir.dt.bfloat16
FP8 = mybir.dt.float8e4
AF = mybir.ActivationFunctionType
ALU = mybir.AluOpType
AXX = mybir.AxisListType.X

MAGIC = 3 * 2.0 ** 22

D = 1024
DI = 2048
DH = 1024
NST = 16
DTR = 64
S = 4096
TOWN = 2048
NC = 8
TC = 1024
NCHUNK = S // TC
PAIRS = [[2 * k, 2 * k + 1] for k in range(NC // 2)]
ALLG = [list(range(NC))]
OUT_SHAPE = (4, 4096, 1024)

ENGINES = ("sync", "vector", "scalar", "tensor", "gpsimd")


class PB:

    def __init__(self, nc):
        self.nc = nc
        self.q = {e: [] for e in ENGINES}
        self.cnt = {}
        self.sems = {}
        self.alias = {}
        self.autodrain = True
        self._stack = contextlib.ExitStack()

    def sem(self, name):
        if name not in self.sems:
            self.sems[name] = self._stack.enter_context(self.nc.semaphore(name))
            self.cnt[name] = 0
        return self.sems[name]

    KEEP = {"lntpe", "ipmm", "fcmm", "pjmm", "qkmm", "womm", "dqe", "pjev",
            "p3e", "cc_mix", "cc_kv", "cc_q", "sc_v", "sc_s", "scn", "ymm"}
    CANON = {"vector": "cv", "scalar": "cs", "tensor": "ct", "gpsimd": "cg"}

    def op(self, engine, method, *args, inc=None, winc=1, wait=None, **kw):
        if (inc is not None and method != "dma_start"
                and method != "collective_compute" and inc not in self.KEEP):
            canon = self.CANON[engine]
            old = self.alias.get(inc)
            assert old in (None, canon), (inc, old, canon)
            self.alias[inc] = canon
            inc = canon
        waits = []
        if wait:
            if not isinstance(wait, list):
                wait = [wait]
            for w in wait:
                if isinstance(w, str):
                    wname, wval = w, None
                else:
                    wname, wval = w
                wname = self.alias.get(wname, wname)
                self.sem(wname)
                wv = self.cnt[wname] if wval is None else wval
                if wv > 0:
                    waits.append((self.sem(wname), wv))
        sem = None
        if inc is not None:
            sem = self.sem(inc)
            self.cnt[inc] += winc

        drain = kw.pop("drain", False)
        if (self.autodrain and engine in ("vector", "scalar", "gpsimd")
                and method not in ("dma_start", "collective_compute")):
            drain = True

        def fn(eng, _m=method, _a=args, _k=kw, _s=sem, _w=waits, _wi=winc,
               _d=drain):
            for (sh, wv) in _w:
                eng.wait_ge(sh, wv)
            if _d:
                eng.drain()
            r = getattr(eng, _m)(*_a, **_k)
            if _s is not None:
                r.then_inc(_s, _wi)
        self.q[engine].append(fn)

    def dma(self, engine, out, in_, inc, wait=None, **kw):
        self.op(engine, "dma_start", inc=inc, winc=16, wait=wait,
                out=out, in_=in_, **kw)

    def wait(self, engine, name, value=None):
        name = self.alias.get(name, name)
        self.sem(name)
        v = self.cnt[name] if value is None else value
        if v <= 0:
            return
        sem = self.sem(name)
        self.q[engine].append(lambda eng, _s=sem, _v=v: eng.wait_ge(_s, _v))

    def val(self, name):
        name = self.alias.get(name, name)
        self.sem(name)
        return self.cnt[name]

    def replay(self, block):
        pb = self

        @block.sync
        def _(eng):
            for fn in pb.q["sync"]:
                fn(eng)

        @block.vector
        def _(eng):
            for fn in pb.q["vector"]:
                fn(eng)

        @block.scalar
        def _(eng):
            for fn in pb.q["scalar"]:
                fn(eng)

        @block.tensor
        def _(eng):
            for fn in pb.q["tensor"]:
                fn(eng)

        @block.gpsimd
        def _(eng):
            for fn in pb.q["gpsimd"]:
                fn(eng)


def build_program(debug=()):
    nc = bass.Bass("TRN2", target_bir_lowering=False, num_devices=NC)
    stack = contextlib.ExitStack()

    def din(name, shape, dt=F32):
        return nc.dram_tensor(name, list(shape), dt, kind="ExternalInput").ap()

    def sbt(name, shape, dt=F32):
        return stack.enter_context(nc.sbuf_tensor(name, list(shape), dt))

    x_batch = din("x_batch", [S, D])
    x_own = din("x_own", [TOWN, D])
    winT = din("winT", [24, 128, 1024])
    conv_w_cols = din("conv_w_cols", [128, 64])
    conv_b_cols = din("conv_b_cols", [128, 16])
    ln1w_col = din("ln1w_col", [128, 8])
    ln2w_col_in = din("ln2w_col_in", [128, 8])
    xprojT = din("xprojT", [DI, 112])
    dtwT = din("dtwT", [DTR, DH])
    dtb_col = din("dtb_col", [128, 8])
    alog_scan = din("alog_scan", [128, 128])
    dp_col = din("dp_col", [128, 8])
    woutT = din("woutT", [DH, D])
    fcT_tiled = din("fcT_tiled", [32, 128, 1024])
    l1_qkvo = din("l1_qkvo", [4, 4, 8, 128, 1024])
    l1_w1 = din("l1_w1", [4, 32, 128, 1024])
    l1_w2 = din("l1_w2", [4, 4096, 1024])
    l1_bqkvo = din("l1_bqkvo", [4, 4, 128, 8])
    l1_b1c = din("l1_b1c", [4, 128, 32])
    l1_b2r = din("l1_b2r", [4, 1, 1024])
    l1_lnsb = din("l1_lnsb", [4, 4, 1, 1024])
    projT = din("projT", [4096, 1024])
    w8T = din("w8T", [D, D], BF16)
    ws_in = din("ws_in", [1, 1])
    bias_row_in = din("bias_row_in", [1, D])
    sel16 = din("sel16", [16, 128], BF16)
    sely = din("sely", [128, 16 * 128], BF16)
    eye128 = din("eye128", [128, 128], BF16)
    eye128f = din("eye128f", [128, 128])
    wup_in = din("wup_in", [3072, 1024])
    wg_in = din("wg_in", [3072, 1024])
    wd_in = din("wd_in", [1024, 3072])

    out = nc.dram_tensor("out", [TOWN, D], F32, kind="ExternalOutput").ap()
    dbg = {}
    for name, shape in debug:
        dbg[name] = nc.dram_tensor("dbg_" + name, list(shape), F32,
                                   kind="ExternalOutput").ap()

    dt_hbm = nc.dram_tensor("dt_hbm", [DH, 1, TC], F32)
    w_hbm = nc.dram_tensor("w_hbm", [DH, 1, TC], F32)
    mix_part = nc.dram_tensor("mix_part", [2 * TOWN, D], F32)
    mix_sum = nc.dram_tensor("mix_sum", [TOWN, D], F32)

    pb = PB(nc)
    pb.autodrain = False

    xnb16 = sbt("xnb16", [128, 1024], BF16)
    sq32 = sbt("sq32", [128, 1024])
    stat = sbt("stat", [128, 16])
    ccol = sbt("ccol", [128, 128])
    kcol = sbt("kcol", [128, 8])
    eye_sb = sbt("eye_sb", [128, 128], BF16)
    mstack = contextlib.ExitStack()

    def mbt(name, shape, dt=F32):
        return mstack.enter_context(nc.sbuf_tensor(name, list(shape), dt))

    xtile = mbt("xtile", [128, 1024])
    sp32a = mbt("sp32a", [128, 1024])
    sp32b = mbt("sp32b", [128, 1024])
    ev32 = mbt("ev32", [128, 1024])
    xn_fm = mbt("xn_fm", [128, 8 * TC], BF16)
    wib = mbt("wib", [128, 2 * 1024], BF16)
    xr_rot = mbt("xr_rot", [128, TC + 4])
    halo = mbt("halo", [128, 64])
    cacc = mbt("cacc", [128, TC])
    xc_own = mbt("xc_own", [128, 8 * TC], BF16)
    xc_rot = mbt("xc_rot", [128, TC], BF16)
    z_b = mbt("z_b", [128, 8 * TC], BF16)
    xprojT_sb = mbt("xprojT_sb", [128, 16 * 112], BF16)
    dtw_sb = mbt("dtw_sb", [64, DH], BF16)
    wout_sb = mbt("wout_sb", [128, 8 * 1024], BF16)
    dcb_dtr = mbt("dcb_dtr", [64, TC], BF16)
    dcb_bc = mbt("dcb_bc", [16, TC], BF16)
    dcb_cc = mbt("dcb_cc", [16, TC], BF16)
    dt32 = mbt("dt32", [128, TC])
    w32 = mbt("w32", [128, TC])
    bcrep = mbt("bcrep", [128, TC], BF16)
    ccrep = mbt("ccrep", [128, TC], BF16)
    dtrep = mbt("dtrep", [128, 2 * TC], BF16)
    wrep = mbt("wrep", [128, 2 * TC], BF16)
    dA_b = mbt("dA_b", [128, 2 * TC], BF16)
    dBx_b = mbt("dBx_b", [128, 2 * TC], BF16)
    h32 = mbt("h32", [128, 2 * TC])
    hC_b = mbt("hC_b", [128, 2 * TC], BF16)
    hcarry = mbt("hcarry", [128, 128])
    yg_b = mbt("yg_b", [128, 8 * TC], BF16)
    zsil = mbt("zsil", [128, 512])
    acol = mbt("acol", [128, 128])
    sely_sb = mbt("sely_sb", [128, 16 * 128], BF16)
    sel16_sb = mbt("sel16_sb", [16, 128], BF16)

    psum = [stack.enter_context(nc.psum_tensor(
                f"ps{i}", [128, 512], BF16 if i in (4, 5) else F32))
            for i in range(8)]

    LN1W, CONVW, CONVB, DTB, DP = 0, 16, 80, 96, 104

    pb.dma("sync", ccol[:, LN1W:LN1W + 8], ln1w_col, "cld")
    pb.dma("sync", ccol[:, CONVW:CONVW + 64], conv_w_cols, "cld")
    pb.dma("sync", ccol[:, CONVB:CONVB + 16], conv_b_cols, "cld")
    pb.dma("sync", ccol[:, DTB:DTB + 8], dtb_col, "cld")
    pb.dma("sync", ccol[:, DP:DP + 8], dp_col, "cld")
    pb.dma("sync", sel16_sb[:], sel16, "cld")
    pb.dma("sync", sely_sb[:], sely, "cld")
    pb.dma("sync", eye_sb[:], eye128, "cld")
    pb.dma("sync", acol[:], alog_scan, "cld")
    pb.op("gpsimd", "dma_start", inc="cldg", winc=16,
          out=xprojT_sb[:].rearrange("p (c r) -> p c r", c=16),
          in_=xprojT.rearrange("(c p) r -> p c r", p=128))
    pb.dma("gpsimd", dtw_sb[:], dtwT, "cldg")
    pb.op("gpsimd", "dma_start", inc="cldg", winc=16,
          out=wout_sb[:].rearrange("p (c e) -> p c e", c=8),
          in_=woutT.rearrange("(c p) e -> p c e", p=128))

    pb.op("scalar", "activation", wait=[("cld", None)], inc="aprep",
          out=acol[:], in_=acol[:], func=AF.Exp)
    pb.op("vector", "tensor_scalar", wait=[("aprep", None)], inc="aprepv",
          out=acol[:], in0=acol[:], scalar1=-1.0, scalar2=0.0,
          op0=ALU.mult, op1=ALU.add)
    apv_val = pb.val("aprepv")
    pb.op("vector", "memset", kcol[:, 0:1], 1e-5, inc="minit")
    pb.op("vector", "memset", kcol[:, 1:2], 1.0, inc="minit")
    pb.op("vector", "memset", kcol[:, 2:3], 1e-6, inc="minit")
    pb.op("vector", "memset", halo[:], 0.0, inc="minit")
    pb.op("vector", "memset", hcarry[:], 0.0, inc="minit")
    pb.wait("vector", "cld")
    pb.wait("scalar", "cld")
    pb.wait("tensor", "cld")
    pb.wait("tensor", "cldg")

    for ci in range(NCHUNK):
        t0 = ci * TC

        for j in range(TC // 128):
            pb.dma("sync", xtile[:], x_batch[t0 + j * 128: t0 + (j + 1) * 128, :],
                   "d_xld", wait=[("lnap", None)])
            pb.op("scalar", "activation",
                  wait=[("d_xld", None), ("st_v", None)], inc="st_s",
                  out=sq32[:], in_=xtile[:], func=AF.Square,
                  accum_out=stat[:, 0:1])
            pb.op("vector", "tensor_reduce",
                  wait=[("d_xld", None), ("st_s", None)],
                  inc="st_v", out=stat[:, 1:2], in_=xtile[:], axis=AXX,
                  op=ALU.add)
            pb.op("vector", "tensor_scalar", inc="st_v", out=stat[:, 2:3],
                  in0=stat[:, 1:2], scalar1=1.0 / D, scalar2=0.0,
                  op0=ALU.mult, op1=ALU.add, drain=True)
            pb.op("vector", "tensor_scalar", inc="st_v", out=stat[:, 3:4],
                  in0=stat[:, 0:1], scalar1=1.0 / D, scalar2=0.0,
                  op0=ALU.mult, op1=ALU.add)
            pb.op("vector", "tensor_tensor", inc="st_v", out=stat[:, 4:5],
                  in0=stat[:, 2:3], in1=stat[:, 2:3], op=ALU.mult, drain=True)
            pb.op("vector", "tensor_tensor", inc="st_v", out=stat[:, 5:6],
                  in0=stat[:, 3:4], in1=stat[:, 4:5], op=ALU.subtract,
                  drain=True)
            pb.op("scalar", "activation",
                  wait=[("st_v", None), ("minit", None)], inc="st_s2",
                  out=stat[:, 6:7], in_=stat[:, 5:6], func=AF.Sqrt,
                  bias=kcol[:, 0:1], scale=1.0)
            pb.op("vector", "reciprocal", wait=[("st_s2", None)], inc="st_v",
                  out=stat[:, 7:8], in_=stat[:, 6:7])
            pb.op("vector", "tensor_scalar", inc="lnap",
                  out=xnb16[:], in0=xtile[:], scalar1=stat[:, 2:3],
                  scalar2=stat[:, 7:8], op0=ALU.subtract, op1=ALU.mult,
                  drain=True)
            for k in range(8):
                pk = psum[4 + (k % 2)]
                tp_wait = [("lnap", None)]
                if pb.val("lntpe") >= 2:
                    tp_wait.append(("lntpe", pb.val("lntpe") - 1))
                pb.op("tensor", "transpose", pk[:, 0:128],
                      xnb16[:, k * 128:(k + 1) * 128], eye_sb[:],
                      wait=tp_wait, inc="lntpp")
                pb.op("vector", "tensor_scalar", wait=[("lntpp", None)],
                      inc="lntpe",
                      out=xn_fm[:, k * TC + j * 128: k * TC + (j + 1) * 128],
                      in0=pk[:, 0:128], scalar1=ccol[:, LN1W + k:LN1W + k + 1],
                      scalar2=0.0, op0=ALU.mult, op1=ALU.add)

        for et in range(24):
            wb = wib[:, (et % 2) * 1024:(et % 2) * 1024 + 1024]
            wsem = "d_wib%d" % (et % 2)
            dma_wait = None
            if pb.val("ipmm") >= 32:
                dma_wait = [("ipmm", pb.val("ipmm") - 16)]
            pb.op("gpsimd", "dma_start", inc=wsem, winc=16, wait=dma_wait,
                  out=wb, in_=winT[et])
            pb.wait("tensor", wsem)
            pb.wait("tensor", "lntpe")
            for th in range(2):
                pe = psum[th] if th == 0 else psum[6 + (et % 2)]
                if th == 0:
                    pe = psum[et % 2]
                for k in range(8):
                    mm_wait = [("scn", None)] if False else None
                    if k == 0:
                        mm_wait = [("ipev", None), ("cnv", None),
                                   ("xpmm", None)]
                    pb.op("tensor", "matmul", pe[:, 0:512],
                          wb[:, k * 128:(k + 1) * 128],
                          xn_fm[:, k * TC + th * 512: k * TC + th * 512 + 512],
                          start=(k == 0), stop=(k == 7),
                          wait=mm_wait, inc="ipmm")
            if et < 16:
                pb.op("vector", "tensor_copy", wait=[("ipmm", None)],
                      inc="ipev", out=xr_rot[:, 4:4 + 512],
                      in_=psum[et % 2][:, 0:512])
                pb.op("vector", "tensor_copy", inc="ipev",
                      out=xr_rot[:, 4 + 512:4 + 1024],
                      in_=psum[6 + (et % 2)][:, 0:512])
                pb.op("vector", "tensor_copy", inc="cnv",
                      out=xr_rot[:, 0:4], in_=halo[:, et * 4:(et + 1) * 4])
                pb.op("vector", "tensor_scalar", inc="cnv", out=cacc[:],
                      drain=True, in0=xr_rot[:, 1:1 + TC],
                      scalar1=ccol[:, CONVW + et * 4:CONVW + et * 4 + 1],
                      scalar2=0.0, op0=ALU.mult, op1=ALU.add)
                for tap in range(1, 4):
                    pb.op("vector", "scalar_tensor_tensor", inc="cnv",
                          drain=True, out=cacc[:],
                          in0=xr_rot[:, tap + 1:tap + 1 + TC],
                          scalar=ccol[:, CONVW + et * 4 + tap:
                                       CONVW + et * 4 + tap + 1],
                          in1=cacc[:], op0=ALU.mult, op1=ALU.add)
                pb.op("vector", "tensor_copy", inc="cnv",
                      out=halo[:, et * 4:(et + 1) * 4],
                      in_=xr_rot[:, TC:TC + 4])
                if et < 8:
                    xc_dst = xc_own[:, et * TC:(et + 1) * TC]
                else:
                    xc_dst = xc_rot[:]
                pb.op("scalar", "activation",
                      wait=[("cnv", None), ("xpmm", None)], inc="cnv_s",
                      out=xc_dst, in_=cacc[:], func=AF.Silu,
                      bias=ccol[:, CONVB + et:CONVB + et + 1], scale=1.0)
                for th in range(2):
                    pb.op("tensor", "matmul", psum[2 + th][0:112, 0:512],
                          xprojT_sb[:, et * 112:(et + 1) * 112],
                          xc_dst[:, th * 512:th * 512 + 512],
                          start=(et == 0), stop=(et == 15),
                          wait=[("cnv_s", None)], inc="xpmm")
            else:
                zi = (et - 16) * TC
                pb.op("vector", "tensor_copy", wait=[("ipmm", None)],
                      inc="ipev", out=z_b[:, zi:zi + 512],
                      in_=psum[et % 2][:, 0:512])
                pb.op("vector", "tensor_copy", inc="ipev",
                      out=z_b[:, zi + 512:zi + 1024],
                      in_=psum[6 + (et % 2)][:, 0:512])

        for th in range(2):
            pb.op("vector", "tensor_copy", wait=[("xpmm", None)], inc="xde",
                  out=dcb_dtr[:, th * 512:th * 512 + 512],
                  in_=psum[2 + th][0:64, 0:512])
            pb.op("vector", "tensor_copy", inc="xde",
                  out=dcb_bc[:, th * 512:th * 512 + 512],
                  in_=psum[2 + th][64:80, 0:512])
            pb.op("vector", "tensor_copy", inc="xde",
                  out=dcb_cc[:, th * 512:th * 512 + 512],
                  in_=psum[2 + th][96:112, 0:512])

        for th in range(2):
            pb.op("tensor", "matmul", psum[2][:, 0:512], sel16_sb[:],
                  dcb_bc[:, th * 512:th * 512 + 512], start=True, stop=True,
                  wait=[("xde", None), ("bcre", None)], inc="bcr")
            pb.op("vector", "tensor_copy", wait=[("bcr", None)], inc="bcre",
                  out=bcrep[:, th * 512:th * 512 + 512], in_=psum[2][:, 0:512])
            pb.op("tensor", "matmul", psum[3][:, 0:512], sel16_sb[:],
                  dcb_cc[:, th * 512:th * 512 + 512], start=True, stop=True,
                  wait=[("bcre", None)], inc="bcr")
            pb.op("vector", "tensor_copy", wait=[("bcr", None)], inc="bcre",
                  out=ccrep[:, th * 512:th * 512 + 512], in_=psum[3][:, 0:512])

        for dtile in range(8):
            for th in range(2):
                mm_wait = [("xde", None), ("bcre", None), ("dtv", None)]
                pb.op("tensor", "matmul", psum[2 + th][:, 0:512],
                      dtw_sb[:, dtile * 128:(dtile + 1) * 128],
                      dcb_dtr[:, th * 512:th * 512 + 512],
                      start=True, stop=True, wait=mm_wait, inc="dtmm")
            for th in range(2):
                ps = psum[2 + th]
                sl = slice(th * 512, th * 512 + 512)
                pb.op("vector", "tensor_scalar",
                      wait=[("dtmm", None), ("d_dtw", None)],
                      inc="dtv", out=sp32a[:, sl], in0=ps[:, 0:512],
                      scalar1=ccol[:, DTB + dtile:DTB + dtile + 1],
                      scalar2=0.0, op0=ALU.add, op1=ALU.add)
                pb.op("vector", "tensor_scalar", inc="dtv", out=sp32b[:, sl],
                      in0=sp32a[:, sl], scalar1=-1.0, scalar2=0.0,
                      op0=ALU.mult, op1=ALU.add, drain=True)
                pb.op("vector", "tensor_tensor", inc="dtv", out=sp32b[:, sl],
                      in0=sp32a[:, sl], in1=sp32b[:, sl], op=ALU.max,
                      drain=True)
                pb.op("scalar", "activation", wait=[("dtv", None)], inc="dts",
                      out=sp32b[:, sl], in_=sp32b[:, sl], func=AF.Exp,
                      scale=-1.0)
                pb.op("scalar", "activation", inc="dts", out=sp32b[:, sl],
                      in_=sp32b[:, sl], func=AF.Ln, bias=kcol[:, 1:2],
                      scale=1.0, drain=True)
                pb.op("vector", "scalar_tensor_tensor", wait=[("dts", None)],
                      inc="dtev", out=dt32[:, sl], in0=sp32a[:, sl],
                      scalar=0.0, in1=sp32b[:, sl], op0=ALU.max, op1=ALU.add,
                      drain=True)
            pb.op("vector", "tensor_tensor", inc="dtev", out=w32[:],
                  in0=dt32[:], in1=xc_own[:, dtile * TC:(dtile + 1) * TC],
                  op=ALU.mult, drain=True)
            dtw_wait = [("dtev", None)]
            if ci > 0 and dtile == 0:
                dtw_wait.append(("d_rep0", None))
                dtw_wait.append(("d_rep1", None))
            pb.dma("sync", dt_hbm.ap()[dtile * 128:(dtile + 1) * 128, 0, :],
                   dt32[:], "d_dtw", wait=dtw_wait)
            pb.dma("sync", w_hbm.ap()[dtile * 128:(dtile + 1) * 128, 0, :],
                   w32[:], "d_dtw")

        for g in range(8):
            for i16 in range(16):
                s = g * 16 + i16
                rot = s % 2
                src_dt = dt_hbm.ap()[s * 8:(s + 1) * 8, :, :] \
                    .broadcast_to([8, 16, TC])
                src_w = w_hbm.ap()[s * 8:(s + 1) * 8, :, :] \
                    .broadcast_to([8, 16, TC])
                dst_dt = dtrep[:, rot * TC:rot * TC + TC]
                dst_w = wrep[:, rot * TC:rot * TC + TC]
                sN = ci * 128 + s
                repsem = "d_rep%d" % rot
                wdep = [("d_dtw", None)]
                if pb.val("sc_s") >= 2:
                    wdep.append(("sc_s", pb.val("sc_s") - 1))
                    wdep.append(("sc_v", pb.val("sc_v") - 5))
                pb.op("gpsimd", "dma_start", inc=repsem, winc=16,
                      out=dst_dt, in_=src_dt, wait=wdep)
                pb.op("gpsimd", "dma_start", inc=repsem, winc=16,
                      out=dst_w, in_=src_w)
                expw = [(repsem, None), ("aprepv", apv_val)]
                if pb.val("sc_v") >= 5:
                    expw.append(("sc_v", pb.val("sc_v") - 4))
                pb.op("scalar", "activation", wait=expw, inc="sc_s",
                      out=dA_b[:, rot * TC:rot * TC + TC],
                      in_=dtrep[:, rot * TC:rot * TC + TC],
                      func=AF.Exp, scale=acol[:, s:s + 1])
                pb.op("vector", "tensor_tensor",
                      wait=[(repsem, None)],
                      inc="sc_v", out=dBx_b[:, rot * TC:rot * TC + TC],
                      in0=wrep[:, rot * TC:rot * TC + TC], in1=bcrep[:],
                      op=ALU.mult)
                pb.op("vector", "tensor_tensor_scan", wait=[("sc_s", None)],
                      inc="sc_v", out=h32[:, rot * TC:rot * TC + TC],
                      data0=dA_b[:, rot * TC:rot * TC + TC],
                      data1=dBx_b[:, rot * TC:rot * TC + TC],
                      initial=hcarry[:, s:s + 1], op0=ALU.mult, op1=ALU.add,
                      drain=True)
                pb.op("vector", "tensor_copy", inc="sc_v",
                      out=hcarry[:, s:s + 1],
                      in_=h32[:, rot * TC + TC - 1:rot * TC + TC],
                      drain=True)
                hcw = []
                if pb.val("ymm") >= 4:
                    hcw.append(("ymm", pb.val("ymm") - 2))
                pb.op("vector", "tensor_tensor", inc="scn",
                      out=hC_b[:, rot * TC:rot * TC + TC],
                      in0=h32[:, rot * TC:rot * TC + TC], in1=ccrep[:],
                      op=ALU.mult, wait=hcw)
                for th in range(2):
                    ym_wait = [("scn", None)]
                    if i16 == 0:
                        ym_wait.append(("ygv", None))
                    pb.op("tensor", "matmul", psum[6 + th][:, 0:512],
                          sely_sb[:, i16 * 128:(i16 + 1) * 128],
                          hC_b[:, rot * TC + th * 512:rot * TC + th * 512 + 512],
                          start=(i16 == 0), stop=(i16 == 15),
                          wait=ym_wait, inc="ymm")
            for th in range(2):
                ps = psum[6 + th]
                sl = slice(g * TC + th * 512, g * TC + th * 512 + 512)
                pb.op("vector", "scalar_tensor_tensor", wait=[("ymm", None)],
                      inc="ygv0", drain=True, out=sp32a[:, 0:512],
                      in0=xc_own[:, sl], scalar=ccol[:, DP + g:DP + g + 1],
                      in1=ps[:, 0:512], op0=ALU.mult, op1=ALU.add)
                pb.op("scalar", "activation",
                      wait=[("ygv", None), ("ygv0", None)], inc="ygs",
                      out=zsil[:], in_=z_b[:, sl], func=AF.Silu)
                pb.op("vector", "tensor_tensor", wait=[("ygs", None)],
                      inc="ygv", out=yg_b[:, sl], in0=sp32a[:, 0:512],
                      in1=zsil[:], op=ALU.mult, drain=True)

        for tt in range(TC // 128):
            for th in range(2):
                for k in range(8):
                    mm_wait = [("ygv", None)]
                    if k == 0:
                        mm_wait.append(("opev", None))
                    pb.op("tensor", "matmul", psum[th][:, 0:512],
                          yg_b[:, k * TC + tt * 128: k * TC + (tt + 1) * 128],
                          wout_sb[:, k * 1024 + th * 512:
                                  k * 1024 + th * 512 + 512],
                          start=(k == 0), stop=(k == 7),
                          wait=mm_wait, inc="opmm")
                pb.op("vector", "tensor_copy",
                      wait=[("opmm", None), ("d_mp", None)], inc="opev",
                      out=ev32[:, th * 512:th * 512 + 512],
                      in_=psum[th][:, 0:512])
            pb.dma("sync", mix_part.ap()[t0 + tt * 128: t0 + (tt + 1) * 128, :],
                   ev32[:, 0:1024], "d_mp", wait=[("opev", None)])

    pb.wait("gpsimd", "d_mp")
    pb.op("gpsimd", "collective_compute", "ReduceScatter", ALU.add,
          replica_groups=PAIRS, ins=[mix_part.ap().opt()],
          outs=[mix_sum.ap().opt()], inc="cc_mix")

    mstack.close()
    stream = sbt("stream", [128, 16 * 1024])
    for j in range(16):
        pb.dma("sync", stream[:, j * 1024:(j + 1) * 1024],
               x_own[j * 128:(j + 1) * 128, :], "d_so",
               wait=[("opev", None), ("d_mp", None)])
    for j in range(16):
        pb.dma("gpsimd", sq32[:],
               mix_sum.ap()[j * 128:(j + 1) * 128, :], "d_sm",
               wait=[("cc_mix", None), ("res0", None)])
        pb.op("vector", "tensor_tensor",
              wait=[("d_so", None), ("d_sm", None)],
              inc="res0", out=stream[:, j * 1024:(j + 1) * 1024],
              in0=stream[:, j * 1024:(j + 1) * 1024], in1=sq32[:], op=ALU.add,
              drain=True)

    if "after_mixer" in dbg:
        for j in range(16):
            pb.dma("sync", dbg["after_mixer"][j * 128:(j + 1) * 128, :],
                   stream[:, j * 1024:(j + 1) * 1024], "d_dbg",
                   wait=[("res0", None)])


    bstack = contextlib.ExitStack()

    def bbt(name, shape, dt=F32):
        return bstack.enter_context(nc.sbuf_tensor(name, list(shape), dt))

    xfmA = sbt("xfmA", [128, 8 * 2048], BF16)
    wstr = sbt("wstr", [128, 2 * 2048], BF16)
    h_fm = bbt("h_fm", [128, 32 * 512], BF16)
    ln2w2_col = bbt("ln2w2_col", [128, 8])
    pb.dma("sync", ln2w2_col[:], ln2w_col_in, "d_ln2w")

    def stream_ln(ready, dst_fm=None, wcol=None, inplace=False):
        for j in range(16):
            xt = stream[:, j * 1024:(j + 1) * 1024]
            pb.op("scalar", "activation",
                  wait=ready + [("st_v", None)], inc="st_s",
                  out=sq32[:], in_=xt, func=AF.Square, accum_out=stat[:, 0:1])
            pb.op("vector", "tensor_reduce", wait=ready + [("st_s", None)],
                  inc="st_v", out=stat[:, 1:2], in_=xt, axis=AXX, op=ALU.add)
            pb.op("vector", "tensor_scalar", inc="st_v", out=stat[:, 2:3],
                  in0=stat[:, 1:2], scalar1=1.0 / D, scalar2=0.0,
                  op0=ALU.mult, op1=ALU.add, drain=True)
            pb.op("vector", "tensor_scalar", inc="st_v", out=stat[:, 3:4],
                  in0=stat[:, 0:1], scalar1=1.0 / D, scalar2=0.0,
                  op0=ALU.mult, op1=ALU.add)
            pb.op("vector", "tensor_tensor", inc="st_v", out=stat[:, 4:5],
                  in0=stat[:, 2:3], in1=stat[:, 2:3], op=ALU.mult, drain=True)
            pb.op("vector", "tensor_tensor", inc="st_v", out=stat[:, 5:6],
                  in0=stat[:, 3:4], in1=stat[:, 4:5], op=ALU.subtract,
                  drain=True)
            pb.op("scalar", "activation", wait=[("st_v", None)], inc="st_s2",
                  out=stat[:, 6:7], in_=stat[:, 5:6], func=AF.Sqrt,
                  bias=kcol[:, 0:1], scale=1.0)
            pb.op("vector", "reciprocal", wait=[("st_s2", None)], inc="st_v",
                  out=stat[:, 7:8], in_=stat[:, 6:7])
            if inplace:
                pb.op("vector", "tensor_scalar", inc="lnap", out=xt, in0=xt,
                      scalar1=stat[:, 2:3], scalar2=stat[:, 7:8],
                      op0=ALU.subtract, op1=ALU.mult, drain=True)
            else:
                pb.op("vector", "tensor_scalar", inc="lnap", out=xnb16[:],
                      in0=xt, scalar1=stat[:, 2:3], scalar2=stat[:, 7:8],
                      op0=ALU.subtract, op1=ALU.mult, drain=True)
                for k in range(8):
                    pk = psum[4 + (k % 2)]
                    tp_wait = [("lnap", None)]
                    if pb.val("lntpe") >= 2:
                        tp_wait.append(("lntpe", pb.val("lntpe") - 1))
                    pb.op("tensor", "transpose", pk[:, 0:128],
                          xnb16[:, k * 128:(k + 1) * 128], eye_sb[:],
                          wait=tp_wait, inc="lntpp")
                    if wcol is None:
                        pb.op("vector", "tensor_copy", wait=[("lntpp", None)],
                              inc="lntpe",
                              out=dst_fm[:, k * 2048 + j * 128:
                                         k * 2048 + (j + 1) * 128],
                              in_=pk[:, 0:128])
                    else:
                        pb.op("vector", "tensor_scalar", wait=[("lntpp", None)],
                              inc="lntpe",
                              out=dst_fm[:, k * 2048 + j * 128:
                                         k * 2048 + (j + 1) * 128],
                              in0=pk[:, 0:128], scalar1=wcol[:, k:k + 1],
                              scalar2=0.0, op0=ALU.mult, op1=ALU.add)

    pb.wait("vector", "d_ln2w")
    stream_ln([("res0", None), ("d_dbg", None)], dst_fm=xfmA, wcol=ln2w2_col)

    for tch in range(4):
        ts0 = tch * 512
        for ft in range(32):
            par = ft % 2
            g = []
            if pb.val("fcmm") >= 16:
                g.append(("fcmm", pb.val("fcmm") - 8))
            if tch > 0 and ft < 2:
                g.append(("pjev", None))
            pb.op("gpsimd", "dma_start", inc="d_fc%d" % par, winc=16,
                  wait=g or None, out=wstr[:, par * 2048:par * 2048 + 1024],
                  in_=fcT_tiled[ft])
            pb.wait("tensor", "d_fc%d" % par)
            pb.wait("tensor", "lntpe")
            for k in range(8):
                mmw = None
                if k == 0:
                    mmw = [("fcev", None), ("pjmm", None), ("pjev", None)]
                pb.op("tensor", "matmul", psum[par][:, 0:512],
                      wstr[:, par * 2048 + k * 128:par * 2048 + (k + 1) * 128],
                      xfmA[:, k * 2048 + ts0: k * 2048 + ts0 + 512],
                      start=(k == 0), stop=(k == 7), wait=mmw, inc="fcmm")
            pb.op("scalar", "activation",
                  wait=[("fcmm", None), ("pjmm", None)], inc="fcev",
                  out=h_fm[:, ft * 512:(ft + 1) * 512],
                  in_=psum[par][:, 0:512], func=AF.Silu)
        for wave in range(2):
            for k in range(32):
                par = k % 2
                g = []
                if pb.val("pjmm") >= 8:
                    g.append(("pjmm", pb.val("pjmm") - 4))
                pb.op("gpsimd", "dma_start", inc="d_pj%d" % par, winc=16,
                      wait=g or None,
                      out=wstr[:, par * 2048 + 1024:par * 2048 + 2048],
                      in_=projT[k * 128:(k + 1) * 128, :])
                pb.wait("tensor", "d_pj%d" % par)
                for tt in range(2):
                    for eh in range(2):
                        mmw = None
                        if k == 0:
                            mmw = [("fcev", None), ("pjev", None)]
                        pb.op("tensor", "matmul", psum[tt * 2 + eh][:, 0:512],
                              h_fm[:, k * 512 + (wave * 2 + tt) * 128:
                                   k * 512 + (wave * 2 + tt + 1) * 128],
                              wstr[:, par * 2048 + 1024 + eh * 512:
                                   par * 2048 + 1024 + eh * 512 + 512],
                              start=(k == 0), stop=(k == 31),
                              wait=mmw, inc="pjmm")
            for tt in range(2):
                jj = 4 * tch + wave * 2 + tt
                for eh in range(2):
                    pb.op("vector", "tensor_tensor", wait=[("pjmm", None)],
                          inc="pjev",
                          out=stream[:, jj * 1024 + eh * 512:
                                     jj * 1024 + eh * 512 + 512],
                          in0=stream[:, jj * 1024 + eh * 512:
                                     jj * 1024 + eh * 512 + 512],
                          in1=psum[tt * 2 + eh][:, 0:512], op=ALU.add,
                          drain=True)

    for j in range(16):
        pb.dma("gpsimd", sq32[:], x_own[j * 128:(j + 1) * 128, :], "d_so2",
               wait=[("res2", None), ("pjev", None)])
        pb.op("vector", "tensor_tensor", wait=[("pjev", None), ("d_so2", None)],
              inc="res2", out=stream[:, j * 1024:(j + 1) * 1024],
              in0=stream[:, j * 1024:(j + 1) * 1024], in1=sq32[:], op=ALU.add,
              drain=True)
    stream_ln([("res2", None), ("d_dbg", None)], inplace=True)
    if "after_l0attn" in dbg:
        for j in range(16):
            pb.dma("sync", dbg["after_l0attn"][j * 128:(j + 1) * 128, :],
                   stream[:, j * 1024:(j + 1) * 1024], "d_dbg",
                   wait=[("lnap", None)])

    ws_col = bbt("ws_col", [128, 1])
    bias_row = bbt("bias_row", [1, 1024])
    bias_row_b = bbt("bias_row_b", [1, 1024], BF16)
    w8sb = bbt("w8sb", [128, 8 * 1024], BF16)
    wq_b = bbt("wq_b", [128, 8 * 1024], BF16)
    q8t = bbt("q8t", [128, 1024], FP8)
    sclC = bbt("sclC", [128, 16])
    rcol_b = bbt("rcol_b", [128, 1], BF16)
    rrow_b = bbt("rrow_b", [1, 128], BF16)

    pb.dma("sync", ws_col[:], ws_in.broadcast_to([128, 1]), "d_c2")
    pb.dma("sync", bias_row[:], bias_row_in, "d_c2")
    pb.op("gpsimd", "dma_start", inc="d_c2g", winc=16,
          out=w8sb[:].rearrange("p (c e) -> p c e", c=8),
          in_=w8T.rearrange("(c p) e -> p c e", p=128))
    pb.op("vector", "tensor_copy", wait=[("d_c2", None)], inc="c2v",
          out=bias_row_b[:], in_=bias_row[:])
    pb.op("vector", "tensor_scalar", wait=[("d_c2", None), ("d_c2g", None)],
          inc="c2v",
          out=wq_b[:], in0=w8sb[:], scalar1=ws_col[:, 0:1], scalar2=0.0,
          op0=ALU.mult, op1=ALU.add)
    for j in range(16):
        xt = stream[:, j * 1024:(j + 1) * 1024]
        pb.op("vector", "tensor_reduce",
              wait=[("lnap", None), ("d_dbg", None)], inc="q8v",
              out=stat[:, 9:10], in_=xt, axis=AXX, op=ALU.max,
              apply_absolute_value=True)
        pb.op("vector", "reciprocal", inc="q8v", out=stat[:, 10:11],
              in_=stat[:, 9:10], drain=True)
        pb.op("vector", "tensor_scalar", inc="q8v", out=sq32[:], in0=xt,
              scalar1=stat[:, 10:11], scalar2=224.0, op0=ALU.mult,
              op1=ALU.mult, drain=True)
        pb.op("vector", "tensor_copy", inc="q8v", out=q8t[:], in_=sq32[:],
              drain=True)
        pb.op("vector", "tensor_copy", inc="q8v", out=xnb16[:], in_=q8t[:],
              drain=True)
        pb.op("vector", "tensor_scalar", inc="q8v", out=sclC[:, j:j + 1],
              in0=stat[:, 9:10], scalar1=1.0 / 224.0, scalar2=0.0,
              op0=ALU.mult, op1=ALU.add, drain=True)
        for k in range(8):
            pk = psum[4 + (k % 2)]
            tp_wait = [("q8v", None)]
            if pb.val("lntpe") >= 2:
                tp_wait.append(("lntpe", pb.val("lntpe") - 1))
            pb.op("tensor", "transpose", pk[:, 0:128],
                  xnb16[:, k * 128:(k + 1) * 128], eye_sb[:],
                  wait=tp_wait, inc="lntpp")
            pb.op("vector", "tensor_copy", wait=[("lntpp", None)],
                  inc="lntpe",
                  out=xfmA[:, k * 2048 + j * 128: k * 2048 + (j + 1) * 128],
                  in_=pk[:, 0:128])

    for j in range(16):
        pb.op("vector", "reciprocal", wait=[("q8v", None)], inc="r1v",
              out=stat[:, 11:12], in_=sclC[:, j:j + 1], drain=True)
        pb.op("vector", "tensor_copy", inc="r1v", out=rcol_b[:],
              in_=stat[:, 11:12], drain=True)
        pb.op("tensor", "transpose", psum[5][0:1, 0:128], rcol_b[:],
              eye_sb[:], wait=[("r1v", None), ("lntpe", None)], inc="r1t")
        pb.op("vector", "tensor_copy", wait=[("r1t", None)], inc="r1e",
              out=rrow_b[:], in_=psum[5][0:1, 0:128])
        for eh in range(2):
            pe = psum[eh]
            for k in range(8):
                pb.op("tensor", "matmul", pe[:, 0:512],
                      xfmA[:, k * 2048 + j * 128: k * 2048 + (j + 1) * 128],
                      wq_b[:, k * 1024 + eh * 512: k * 1024 + eh * 512 + 512],
                      start=(k == 0), stop=False,
                      wait=([("q8ev", None)] if k == 0 else None),
                      inc="q8mm")
            pb.op("tensor", "matmul", pe[:, 0:512], rrow_b[:],
                  bias_row_b[:, eh * 512:eh * 512 + 512], start=False,
                  stop=True, wait=[("r1e", None)], inc="q8mm")
            pb.op("vector", "scalar_tensor_tensor", wait=[("q8mm", None)],
                  inc="q8ev", drain=True,
                  out=stream[:, j * 1024 + eh * 512: j * 1024 + eh * 512 + 512],
                  in0=pe[:, 0:512], scalar=sclC[:, j:j + 1],
                  in1=stream[:, j * 1024 + eh * 512: j * 1024 + eh * 512 + 512],
                  op0=ALU.mult, op1=ALU.add)

    stream_ln([("q8ev", None), ("d_dbg", None)], inplace=True)
    if "after_l0" in dbg:
        for j in range(16):
            pb.dma("sync", dbg["after_l0"][j * 128:(j + 1) * 128, :],
                   stream[:, j * 1024:(j + 1) * 1024], "d_dbg",
                   wait=[("lnap", None)])
    bstack.close()


    pb.autodrain = True
    dstack = contextlib.ExitStack()

    def dbt(name, shape, dt=F32):
        return dstack.enter_context(nc.sbuf_tensor(name, list(shape), dt))

    xfmB = dbt("xfmB", [128, 8 * 2048], BF16)
    Kt = dbt("Kt", [128, 2048])
    vt = dbt("vt", [128, 2048])
    kvcols = dbt("kvcols", [128, 48])
    sbc = dbt("sbc", [128, 2 * 1024])
    bql = dbt("bql", [128, 8 * 4])
    b1l = dbt("b1l", [128, 32])
    b2r_b = dbt("b2r_b", [1, 1024], BF16)
    ones_row = dbt("ones_row", [1, 128], BF16)
    h_fm2 = dbt("h_fm2", [128, 32 * 512], BF16)

    kv_part = nc.dram_tensor("kv_part", [128, 16], F32)
    kv_sum = nc.dram_tensor("kv_sum", [128, 16], F32)
    xl1_hbm = nc.dram_tensor("xl1_hbm", [TOWN, D], F32)

    pb.op("vector", "memset", ones_row[:], 1.0, inc="c2v")
    for j in range(16):
        pb.dma("sync", xl1_hbm.ap()[j * 128:(j + 1) * 128, :],
               stream[:, j * 1024:(j + 1) * 1024], "d_xl1",
               wait=[("lnap", None)])

    def stream_to_fm(ready, dst_fm):
        for j in range(16):
            pb.op("vector", "tensor_copy", wait=ready + [("lntpp", None)],
                  inc="lnap", out=xnb16[:], drain=True,
                  in_=stream[:, j * 1024:(j + 1) * 1024])
            for k in range(8):
                pk = psum[4 + (k % 2)]
                tp_wait = [("lnap", None)]
                if pb.val("lntpe") >= 2:
                    tp_wait.append(("lntpe", pb.val("lntpe") - 1))
                pb.op("tensor", "transpose", pk[:, 0:128],
                      xnb16[:, k * 128:(k + 1) * 128], eye_sb[:],
                      wait=tp_wait, inc="lntpp")
                pb.op("vector", "tensor_copy", wait=[("lntpp", None)],
                      inc="lntpe",
                      out=dst_fm[:, k * 2048 + j * 128:
                                 k * 2048 + (j + 1) * 128],
                      in_=pk[:, 0:128])

    for blk in range(4):
        pb.dma("sync", bql[:], l1_bqkvo[blk].rearrange("w p c -> p w c"),
               "d_l1c", wait=[("qkev", None), ("wos", None), ("fcev", None),
                              ("q8ev", None), ("lnap", None)])
        pb.dma("sync", b1l[:], l1_b1c[blk], "d_l1c")
        pb.dma("gpsimd", b2r_b[:], l1_b2r[blk], "d_l1cg",
               wait=[("pjmm", None), ("q8ev", None), ("lnap", None)])

        stream_to_fm([("lnap", None), ("d_xl1", None), ("d_dbg", None)], xfmA)

        for at in range(8):
            for w in range(3):
                par = (at * 3 + w) % 2
                g = []
                if pb.val("qkmm") >= 64:
                    g.append(("qkmm", pb.val("qkmm") - 32))
                if at == 0 and w == 0:
                    g.append(("fcmm", None))
                pb.op("gpsimd", "dma_start", inc="d_qk%d" % par, winc=16,
                      wait=g or None,
                      out=wstr[:, par * 2048:par * 2048 + 1024],
                      in_=l1_qkvo[blk, w, at])
                pb.wait("tensor", "d_qk%d" % par)
                pb.wait("tensor", "lntpe")
                for tch in range(4):
                    pe = psum[par]
                    for k in range(8):
                        mmw = None
                        if k == 0:
                            mmw = [("qkev", None), ("pjev", None)]
                        pb.op("tensor", "matmul", pe[:, 0:512],
                              wstr[:, par * 2048 + k * 128:
                                   par * 2048 + (k + 1) * 128],
                              xfmA[:, k * 2048 + tch * 512:
                                   k * 2048 + tch * 512 + 512],
                              start=(k == 0), stop=(k == 7),
                              wait=mmw, inc="qkmm")
                    bcol = bql[:, w * 8 + at:w * 8 + at + 1]
                    sl = slice(tch * 512, tch * 512 + 512)
                    if w < 2:
                        dst = Kt if w == 1 else vt
                        pb.op("vector", "tensor_scalar",
                              wait=[("qkmm", None), ("d_l1c", None)],
                              inc="qkv1", out=sq32[:, 0:512], in0=pe[:, 0:512],
                              scalar1=bcol, scalar2=0.0, op0=ALU.add,
                              op1=ALU.min)
                        pb.op("scalar", "activation", wait=[("qkv1", None)],
                              inc="qks1", out=sq32[:, 512:1024],
                              in_=sq32[:, 0:512], func=AF.Exp)
                        pb.op("vector", "tensor_scalar", inc="qkv1",
                              out=dst[:, sl], in0=pe[:, 0:512],
                              scalar1=bcol, scalar2=0.0, op0=ALU.add,
                              op1=ALU.max)
                        if w == 0:
                            pb.op("vector", "tensor_tensor",
                                  wait=[("qks1", None)], inc="qkev",
                                  out=h_fm2[:, at * 2048 + tch * 512:
                                            at * 2048 + tch * 512 + 512],
                                  in0=dst[:, sl], in1=sq32[:, 512:1024],
                                  op=ALU.add)
                        else:
                            pb.op("vector", "scalar_tensor_tensor",
                                  wait=[("qks1", None)], inc="qkev",
                                  out=Kt[:, sl], in0=Kt[:, sl], scalar=1.0,
                                  in1=sq32[:, 512:1024], op0=ALU.mult,
                                  op1=ALU.add)
                    else:
                        pb.op("vector", "tensor_scalar",
                              wait=[("qkmm", None), ("d_l1c", None)],
                              inc="qkev", out=vt[:, sl], in0=pe[:, 0:512],
                              scalar1=bcol, scalar2=0.0, op0=ALU.add,
                              op1=ALU.add)
            pb.op("vector", "tensor_reduce",
                  wait=[("qkev", None), ("d_kvp", None)], inc="kvv",
                  out=kvcols[:, 8 + at:8 + at + 1], in_=Kt[:], axis=AXX,
                  op=ALU.add)
            pb.op("vector", "scalar_tensor_tensor", inc="kvv",
                  out=Kt[:], in0=Kt[:], scalar=1.0, in1=vt[:],
                  op0=ALU.mult, op1=ALU.mult,
                  accum_out=kvcols[:, at:at + 1])

        pb.dma("sync", kv_part.ap()[:, 0:16], kvcols[:, 0:16], "d_kvp",
               wait=[("kvv", None)])
        pb.wait("gpsimd", "d_kvp")
        pb.op("gpsimd", "collective_compute", "AllReduce", ALU.add,
              replica_groups=PAIRS, ins=[kv_part.ap().opt()],
              outs=[kv_sum.ap().opt()], inc="cc_kv")
        pb.dma("gpsimd", kvcols[:, 16:32], kv_sum.ap()[:, 0:16], "d_kvs",
               wait=[("cc_kv", None)])

        for at in range(8):
            qsl = slice(at * 2048, (at + 1) * 2048)
            pb.op("vector", "tensor_scalar", wait=[("d_kvs", None)], inc="vv",
                  out=Kt[:], in0=h_fm2[:, qsl],
                  scalar1=kvcols[:, 24 + at:24 + at + 1], scalar2=1e-6,
                  op0=ALU.mult, op1=ALU.add)
            pb.op("vector", "reciprocal", inc="vv", out=Kt[:], in_=Kt[:])
            pb.op("vector", "tensor_scalar", inc="vv", out=vt[:],
                  in0=h_fm2[:, qsl],
                  scalar1=kvcols[:, 16 + at:16 + at + 1], scalar2=0.0,
                  op0=ALU.mult, op1=ALU.add)
            pb.op("vector", "tensor_tensor", inc="vv", out=xfmB[:, qsl],
                  in0=vt[:], in1=Kt[:], op=ALU.mult)

        for et in range(8):
            par = et % 2
            g = []
            if pb.val("womm") >= 64:
                g.append(("womm", pb.val("womm") - 32))
            pb.op("gpsimd", "dma_start", inc="d_wo%d" % par, winc=16,
                  wait=g or None, out=wstr[:, par * 2048:par * 2048 + 1024],
                  in_=l1_qkvo[blk, 3, et])
            pb.wait("tensor", "d_wo%d" % par)
            for tch in range(4):
                pe = psum[2 + par]
                for k in range(8):
                    mmw = [("vv", None), ("woev", None)] if k == 0 else None
                    pb.op("tensor", "matmul", pe[:, 0:512],
                          wstr[:, par * 2048 + k * 128:
                               par * 2048 + (k + 1) * 128],
                          xfmB[:, k * 2048 + tch * 512:
                               k * 2048 + tch * 512 + 512],
                          start=(k == 0), stop=(k == 7),
                          wait=mmw, inc="womm")
                bcol = bql[:, 24 + et:24 + et + 1]
                pb.op("scalar", "activation",
                      wait=[("womm", None), ("d_l1c", None)],
                      inc="wos", out=sq32[:, 0:512], in_=pe[:, 0:512],
                      func=AF.Gelu_apprx_tanh, bias=bcol, scale=1.0)
                pb.op("scalar", "activation", inc="woev",
                      out=h_fm2[:, et * 2048 + tch * 512:
                                et * 2048 + tch * 512 + 512],
                      in_=sq32[:, 0:512], func=AF.Gelu_apprx_tanh)

        for j in range(16):
            for k in range(8):
                pk = psum[4 + (k % 2)]
                tp_wait = [("woev", None)]
                if pb.val("lntpe") >= 2:
                    tp_wait.append(("lntpe", pb.val("lntpe") - 1))
                pb.op("tensor", "transpose", pk[:, 0:128],
                      h_fm2[:, k * 2048 + j * 128: k * 2048 + (j + 1) * 128],
                      eye_sb[:], wait=tp_wait, inc="lntpp")
                pb.op("vector", "tensor_tensor", wait=[("lntpp", None)],
                      inc="lntpe",
                      out=stream[:, j * 1024 + k * 128: j * 1024 + (k + 1) * 128],
                      in0=stream[:, j * 1024 + k * 128:
                                 j * 1024 + (k + 1) * 128],
                      in1=pk[:, 0:128], op=ALU.add)

        pb.dma("gpsimd", sbc[:, 0:1024],
               l1_lnsb[blk, 0].broadcast_to([128, 1024]), "d_sbc",
               wait=[("lnsb", None), ("q8ev", None), ("lnap", None)])
        pb.dma("gpsimd", sbc[:, 1024:2048],
               l1_lnsb[blk, 1].broadcast_to([128, 1024]), "d_sbc")
        stream_ln([("lntpe", None), ("d_dbg", None)], inplace=True)
        for j in range(16):
            pb.op("vector", "tensor_tensor", wait=[("d_sbc", None)],
                  inc="lnsb", out=stream[:, j * 1024:(j + 1) * 1024],
                  in0=stream[:, j * 1024:(j + 1) * 1024],
                  in1=sbc[:, 0:1024], op=ALU.mult)
            pb.op("vector", "tensor_tensor", inc="lnsb",
                  out=stream[:, j * 1024:(j + 1) * 1024],
                  in0=stream[:, j * 1024:(j + 1) * 1024],
                  in1=sbc[:, 1024:2048], op=ALU.add)

        stream_to_fm([("lnsb", None)], xfmA)
        for tch in range(4):
            ts0 = tch * 512
            for ft in range(32):
                par = ft % 2
                g = []
                if pb.val("fcmm") >= 16:
                    g.append(("fcmm", pb.val("fcmm") - 8))
                if ft < 2:
                    g.append(("pjev", None))
                    if tch == 0:
                        g.append(("womm", None))
                pb.op("gpsimd", "dma_start", inc="d_fc%d" % par, winc=16,
                      wait=g or None,
                      out=wstr[:, par * 2048:par * 2048 + 1024],
                      in_=l1_w1[blk, ft])
                pb.wait("tensor", "d_fc%d" % par)
                pb.wait("tensor", "lntpe")
                for k in range(8):
                    mmw = None
                    if k == 0:
                        mmw = [("fcev", None), ("pjmm", None), ("pjev", None)]
                    pb.op("tensor", "matmul", psum[par][:, 0:512],
                          wstr[:, par * 2048 + k * 128:
                               par * 2048 + (k + 1) * 128],
                          xfmA[:, k * 2048 + ts0: k * 2048 + ts0 + 512],
                          start=(k == 0), stop=(k == 7), wait=mmw, inc="fcmm")
                pb.op("scalar", "activation",
                      wait=[("fcmm", None), ("pjmm", None), ("d_l1c", None)],
                      inc="fcev",
                      out=h_fm2[:, ft * 512:(ft + 1) * 512],
                      in_=psum[par][:, 0:512], func=AF.Gelu_apprx_tanh,
                      bias=b1l[:, ft:ft + 1], scale=1.0)
            for wave in range(2):
                for k in range(32):
                    par = k % 2
                    g = []
                    if pb.val("pjmm") >= 8:
                        g.append(("pjmm", pb.val("pjmm") - 4))
                    pb.op("gpsimd", "dma_start", inc="d_pj%d" % par, winc=16,
                          wait=g or None,
                          out=wstr[:, par * 2048 + 1024:par * 2048 + 2048],
                          in_=l1_w2[blk, k * 128:(k + 1) * 128, :])
                    pb.wait("tensor", "d_pj%d" % par)
                    for tt in range(2):
                        for eh in range(2):
                            mmw = None
                            if k == 0:
                                mmw = [("fcev", None), ("pjev", None),
                                       ("c2v", None), ("d_l1cg", None)]
                            pb.op("tensor", "matmul",
                                  psum[tt * 2 + eh][:, 0:512],
                                  h_fm2[:, k * 512 + (wave * 2 + tt) * 128:
                                        k * 512 + (wave * 2 + tt + 1) * 128],
                                  wstr[:, par * 2048 + 1024 + eh * 512:
                                       par * 2048 + 1024 + eh * 512 + 512],
                                  start=(k == 0), stop=False, wait=mmw,
                                  inc="pjmm")
                for tt in range(2):
                    for eh in range(2):
                        pb.op("tensor", "matmul", psum[tt * 2 + eh][:, 0:512],
                              ones_row[:],
                              b2r_b[:, eh * 512:eh * 512 + 512],
                              start=False, stop=True, inc="pjmm")
                for tt in range(2):
                    jj = 4 * tch + wave * 2 + tt
                    for eh in range(2):
                        slot = (tt * 2 + eh) % 2
                        gw = [("pjmm", None)]
                        if pb.val("pjev") >= 2:
                            gw.append(("pjev", pb.val("pjev") - 1))
                        pb.op("scalar", "activation", wait=gw, inc="pjs",
                              out=sq32[:, slot * 512:slot * 512 + 512],
                              in_=psum[tt * 2 + eh][:, 0:512],
                              func=AF.Gelu_apprx_tanh)
                        pb.op("vector", "tensor_tensor", wait=[("pjs", None)],
                              inc="pjev",
                              out=stream[:, jj * 1024 + eh * 512:
                                         jj * 1024 + eh * 512 + 512],
                              in0=stream[:, jj * 1024 + eh * 512:
                                         jj * 1024 + eh * 512 + 512],
                              in1=sq32[:, slot * 512:slot * 512 + 512],
                              op=ALU.add)

        pb.dma("gpsimd", sbc[:, 0:1024],
               l1_lnsb[blk, 2].broadcast_to([128, 1024]), "d_sbc",
               wait=[("lnsb", None)])
        pb.dma("gpsimd", sbc[:, 1024:2048],
               l1_lnsb[blk, 3].broadcast_to([128, 1024]), "d_sbc")
        stream_ln([("pjev", None), ("d_dbg", None)], inplace=True)
        for j in range(16):
            pb.op("vector", "tensor_tensor", wait=[("d_sbc", None)],
                  inc="lnsb", out=stream[:, j * 1024:(j + 1) * 1024],
                  in0=stream[:, j * 1024:(j + 1) * 1024],
                  in1=sbc[:, 0:1024], op=ALU.mult)
            pb.op("vector", "tensor_tensor", inc="lnsb",
                  out=stream[:, j * 1024:(j + 1) * 1024],
                  in0=stream[:, j * 1024:(j + 1) * 1024],
                  in1=sbc[:, 1024:2048], op=ALU.add)

    for j in range(16):
        pb.dma("gpsimd", sq32[:], xl1_hbm.ap()[j * 128:(j + 1) * 128, :],
               "d_so3", wait=[("res3", None), ("lnsb", None)])
        pb.op("vector", "tensor_tensor", wait=[("lnsb", None), ("d_so3", None)],
              inc="res3", out=stream[:, j * 1024:(j + 1) * 1024],
              in0=stream[:, j * 1024:(j + 1) * 1024], in1=sq32[:], op=ALU.add)
    stream_ln([("res3", None), ("d_dbg", None)], inplace=True)
    if "after_l1attn" in dbg:
        for j in range(16):
            pb.dma("sync", dbg["after_l1attn"][j * 128:(j + 1) * 128, :],
                   stream[:, j * 1024:(j + 1) * 1024], "d_dbg",
                   wait=[("lnap", None)])
    dstack.close()


    estack = contextlib.ExitStack()

    def ebt(name, shape, dt=F32):
        return estack.enter_context(nc.sbuf_tensor(name, list(shape), dt))

    wbig = ebt("wbig", [128, 24 * 1024], BF16)
    wt32 = ebt("wt32", [128, 3072])
    wt32b = ebt("wt32b", [128, 3072])
    g32 = ebt("g32", [128, 3072])
    rmax = ebt("rmax", [128, 96])
    rinv = ebt("rinv", [128, 96])
    r7 = ebt("r7", [128, 96])
    s7 = ebt("s7", [128, 96])
    wqb = ebt("wqb", [128, 3072], BF16)
    eye32_sb = ebt("eye32_sb", [128, 128])
    gcol = ebt("gcol", [128, 16])

    wupdq_hbm = nc.dram_tensor("wupdq_hbm", [1024, 3072], BF16)
    wgdq_hbm = nc.dram_tensor("wgdq_hbm", [1024, 3072], BF16)
    wddq_hbm = nc.dram_tensor("wddq_hbm", [3072, 1024], BF16)
    up_hbm = nc.dram_tensor("up_hbm", [TOWN, 3072], F32)
    gate_hbm = nc.dram_tensor("gate_hbm", [TOWN, 3072], F32)
    g2_hbm = nc.dram_tensor("g2_hbm", [TOWN, 3072], F32)
    upq_hbm = nc.dram_tensor("upq_hbm", [TOWN, 3072], F32)
    qs_part = nc.dram_tensor("qs_part", [1, 4], F32)
    qs_sum = nc.dram_tensor("qs_sum", [1, 4], F32)

    o_strip = xfmA
    pb.dma("sync", eye32_sb[:], eye128f, "d_ey", wait=[("lnap", None)])
    pb.op("vector", "memset", gcol[:, 0:8], 0.0, wait=[("lnap", None)],
          inc="einit")
    pb.dma("sync", qs_part.ap()[0:1, 0:4], gcol[0:1, 0:4], "d_qsp",
           wait=[("einit", None)])

    def dequant_weight(src_ap, ntiles, nfree, dst_hbm, tag):
        nblk = nfree // 32
        for pt in range(ntiles):
            pb.dma("gpsimd", wt32[:, 0:nfree],
                   src_ap[pt * 128:(pt + 1) * 128, :], "d_wt",
                   wait=[("dqv", None), ("lnap", None)])
            pb.op("vector", "tensor_reduce", wait=[("d_wt", None)], inc="dqv",
                  out=rmax[:, 0:nblk],
                  in_=wt32[:, 0:nfree].rearrange("p (b t) -> p b t", t=32),
                  axis=AXX, op=ALU.max, apply_absolute_value=True)
            pb.op("vector", "reciprocal", inc="dqv", out=rinv[:, 0:nblk],
                  in_=rmax[:, 0:nblk])
            pb.op("vector", "tensor_scalar", inc="dqv", out=r7[:, 0:nblk],
                  in0=rinv[:, 0:nblk], scalar1=7.0, scalar2=0.0,
                  op0=ALU.mult, op1=ALU.add)
            pb.op("vector", "tensor_scalar", inc="dqv", out=s7[:, 0:nblk],
                  in0=rmax[:, 0:nblk], scalar1=1.0 / 7.0, scalar2=0.0,
                  op0=ALU.mult, op1=ALU.add)
            pb.op("vector", "tensor_tensor", inc="dqv", out=wt32b[:, 0:nfree]
                  .rearrange("p (b t) -> p b t", t=32),
                  in0=wt32[:, 0:nfree].rearrange("p (b t) -> p b t", t=32),
                  in1=r7[:, 0:nblk].rearrange("p (b t) -> p b t", t=1)
                  .broadcast_to([128, nblk, 32]), op=ALU.mult)
            pb.op("vector", "tensor_scalar", inc="dqv", out=wt32b[:, 0:nfree],
                  in0=wt32b[:, 0:nfree], scalar1=MAGIC, scalar2=MAGIC,
                  op0=ALU.add, op1=ALU.subtract)
            pb.op("vector", "tensor_tensor", inc="dqv", out=wqb[:, 0:nfree]
                  .rearrange("p (b t) -> p b t", t=32),
                  in0=wt32b[:, 0:nfree].rearrange("p (b t) -> p b t", t=32),
                  in1=s7[:, 0:nblk].rearrange("p (b t) -> p b t", t=1)
                  .broadcast_to([128, nblk, 32]), op=ALU.mult)
            for fc in range(nfree // 128):
                pk = psum[4 + (fc % 2)]
                tp_wait = [("dqv", None)]
                if pb.val("dqtp") >= 2:
                    tp_wait.append(("dqe", pb.val("dqe") - 1))
                pb.op("tensor", "transpose", pk[:, 0:128],
                      wqb[:, fc * 128:(fc + 1) * 128], eye_sb[:],
                      wait=tp_wait, inc="dqtp")
                pb.op("vector", "tensor_copy",
                      wait=[("dqtp", None), ("d_dq", None)],
                      inc="dqe", out=xnb16[:, 0:128], in_=pk[:, 0:128])
                pb.dma("sync", dst_hbm.ap()[fc * 128:(fc + 1) * 128,
                                            pt * 128:(pt + 1) * 128],
                       xnb16[:, 0:128], "d_dq", wait=[("dqe", None)])

    dequant_weight(wup_in, 24, 1024, wupdq_hbm, "up")
    dequant_weight(wg_in, 24, 1024, wgdq_hbm, "g")
    dequant_weight(wd_in, 8, 3072, wddq_hbm, "d")

    for j in range(16):
        pb.op("vector", "tensor_reduce",
              wait=[("lnap", None), ("d_dbg", None), ("d_qsp", None)],
              inc="gxv", out=gcol[:, 0:1],
              in_=stream[:, j * 1024:(j + 1) * 1024], axis=AXX, op=ALU.max,
              apply_absolute_value=True)
        if j == 0:
            pb.op("vector", "tensor_copy", inc="gxv", out=gcol[:, 1:2],
                  in_=gcol[:, 0:1])
        else:
            pb.op("vector", "tensor_tensor", inc="gxv", out=gcol[:, 1:2],
                  in0=gcol[:, 1:2], in1=gcol[:, 0:1], op=ALU.max)

    def part_max_to_slot(src_col, slot, tag):
        pb.op("tensor", "transpose", psum[2][0:1, 0:128], src_col, eye32_sb[:],
              wait=[(tag, None), ("pmx", None), ("d_ey", None)], inc="pmt")
        pb.op("vector", "tensor_copy",
              wait=[("pmt", None), ("d_ug", None), ("d_p1o", None),
                    ("d_p2o", None), ("d_wt", None)],
              inc="pmx",
              out=wt32[0:1, 0:128], in_=psum[2][0:1, 0:128])
        pb.op("vector", "tensor_reduce", inc="pmx", out=wt32[0:1, 128:129],
              in_=wt32[0:1, 0:128], axis=AXX, op=ALU.max)
        pb.dma("sync", qs_part.ap()[0:1, slot:slot + 1], wt32[0:1, 128:129],
               "d_qsp", wait=[("pmx", None), ("cc_q", None)])

    def ar_max():
        pb.wait("gpsimd", "d_qsp")
        pb.wait("gpsimd", "d_qsb")
        pb.op("gpsimd", "collective_compute", "AllReduce", ALU.max,
              replica_groups=ALLG, ins=[qs_part.ap().opt()],
              outs=[qs_sum.ap().opt()], inc="cc_q")

    part_max_to_slot(gcol[:, 1:2], 0, "gxv")
    ar_max()
    pb.dma("gpsimd", gcol[:, 2:3], qs_sum.ap()[0:1, 0:1].broadcast_to([128, 1]),
           "d_qsb", wait=[("cc_q", None)])
    pb.op("vector", "reciprocal", wait=[("d_qsb", None)], inc="qsv",
          out=gcol[:, 3:4], in_=gcol[:, 2:3])
    pb.op("vector", "tensor_scalar", inc="qsv", out=gcol[:, 3:4],
          in0=gcol[:, 3:4], scalar1=32767.0, scalar2=0.0, op0=ALU.mult,
          op1=ALU.add)
    pb.op("vector", "tensor_scalar", inc="qsv", out=gcol[:, 4:5],
          in0=gcol[:, 2:3], scalar1=1.0 / 32767.0, scalar2=0.0, op0=ALU.mult,
          op1=ALU.add)

    for j in range(16):
        pb.op("vector", "tensor_scalar", wait=[("qsv", None)], inc="xqv",
              out=sq32[:], in0=stream[:, j * 1024:(j + 1) * 1024],
              scalar1=gcol[:, 3:4], scalar2=MAGIC, op0=ALU.mult, op1=ALU.add)
        pb.op("vector", "tensor_scalar", inc="xqv", out=sq32[:],
              in0=sq32[:], scalar1=MAGIC, scalar2=0.0, op0=ALU.subtract,
              op1=ALU.add)
        pb.op("vector", "tensor_copy", inc="xqv", out=xnb16[:], in_=sq32[:],
              wait=[("d_dq", None)])
        for k in range(8):
            pk = psum[4 + (k % 2)]
            tp_wait = [("xqv", None)]
            if pb.val("lntpe") >= 2:
                tp_wait.append(("lntpe", pb.val("lntpe") - 1))
            pb.op("tensor", "transpose", pk[:, 0:128],
                  xnb16[:, k * 128:(k + 1) * 128], eye_sb[:],
                  wait=tp_wait, inc="lntpp")
            pb.op("vector", "tensor_copy", wait=[("lntpp", None)],
                  inc="lntpe",
                  out=xfmA[:, k * 2048 + j * 128: k * 2048 + (j + 1) * 128],
                  in_=pk[:, 0:128])

    def upgate(dq_hbm, out_hbm, maxslot, tag):
        pb.op("gpsimd", "dma_start", inc="d_wb", winc=16,
              out=wbig[:].rearrange("p (c e) -> p c e", c=8),
              in_=dq_hbm.ap().rearrange("(c p) e -> p c e", p=128),
              wait=[("upmm", None), ("gamm", None), ("lnap", None)])
        pb.wait("tensor", "d_wb")
        pb.op("vector", "memset", gcol[:, 5:6], 0.0, inc=tag + "m")
        for j in range(16):
            for ih in range(6):
                pe = psum[ih % 2]
                for k in range(8):
                    mmw = None
                    if k == 0:
                        mmw = [("lntpe", None), (tag + "e", None)]
                    pb.op("tensor", "matmul", pe[:, 0:512],
                          xfmA[:, k * 2048 + j * 128: k * 2048 + (j + 1) * 128],
                          wbig[:, k * 3072 + ih * 512: k * 3072 + ih * 512 + 512],
                          start=(k == 0), stop=(k == 7), wait=mmw,
                          inc=tag + "mm")
                pb.op("vector", "tensor_scalar",
                      wait=[(tag + "mm", None), ("d_ug", None),
                            ("d_qsp", None)],
                      inc=tag + "e", out=wt32[:, 0:512], in0=pe[:, 0:512],
                      scalar1=gcol[:, 4:5], scalar2=0.0, op0=ALU.mult,
                      op1=ALU.add)
                pb.op("vector", "tensor_reduce", inc=tag + "e",
                      out=gcol[:, 6:7], in_=wt32[:, 0:512], axis=AXX,
                      op=ALU.max, apply_absolute_value=True)
                pb.op("vector", "tensor_tensor", inc=tag + "m",
                      out=gcol[:, 5:6], in0=gcol[:, 5:6], in1=gcol[:, 6:7],
                      op=ALU.max)
                pb.dma("sync", out_hbm.ap()[j * 128:(j + 1) * 128,
                                            ih * 512:ih * 512 + 512],
                       wt32[:, 0:512], "d_ug", wait=[(tag + "e", None)])
        part_max_to_slot(gcol[:, 5:6], maxslot, tag + "m")

    upgate(wupdq_hbm, up_hbm, 1, "up")
    upgate(wgdq_hbm, gate_hbm, 2, "ga")
    ar_max()
    for slot, rc, sc in ((1, 8, 9), (2, 10, 11)):
        pb.dma("gpsimd", gcol[:, 7:8],
               qs_sum.ap()[0:1, slot:slot + 1].broadcast_to([128, 1]), "d_qsb",
               wait=[("cc_q", None), ("qsv", None)])
        pb.op("vector", "reciprocal", wait=[("d_qsb", None)], inc="qsv",
              out=gcol[:, rc:rc + 1], in_=gcol[:, 7:8])
        pb.op("vector", "tensor_scalar", inc="qsv", out=gcol[:, rc:rc + 1],
              in0=gcol[:, rc:rc + 1], scalar1=32767.0, scalar2=0.0,
              op0=ALU.mult, op1=ALU.add)
        pb.op("vector", "tensor_scalar", inc="qsv", out=gcol[:, sc:sc + 1],
              in0=gcol[:, 7:8], scalar1=1.0 / 32767.0, scalar2=0.0,
              op0=ALU.mult, op1=ALU.add)

    pb.op("vector", "memset", gcol[:, 6:7], 0.0, wait=[("qsv", None)],
          inc="p1m")
    for j in range(16):
        pb.dma("gpsimd", wt32[:], up_hbm.ap()[j * 128:(j + 1) * 128, :],
               "d_p1", wait=[("p1v", None), ("d_p1o", None), ("d_ug", None),
                             ("d_qsp", None)])
        pb.dma("gpsimd", g32[:], gate_hbm.ap()[j * 128:(j + 1) * 128, :],
               "d_p1")
        pb.op("vector", "tensor_scalar", wait=[("d_p1", None)], inc="p1v",
              out=wt32[:], in0=wt32[:], scalar1=gcol[:, 8:9], scalar2=MAGIC,
              op0=ALU.mult, op1=ALU.add)
        pb.op("vector", "tensor_scalar", inc="p1v", out=wt32[:], in0=wt32[:],
              scalar1=MAGIC, scalar2=gcol[:, 9:10], op0=ALU.subtract,
              op1=ALU.mult)
        pb.op("vector", "tensor_scalar", inc="p1v", out=g32[:], in0=g32[:],
              scalar1=gcol[:, 10:11], scalar2=MAGIC, op0=ALU.mult,
              op1=ALU.add)
        pb.op("vector", "tensor_scalar", inc="p1v", out=g32[:], in0=g32[:],
              scalar1=MAGIC, scalar2=gcol[:, 11:12], op0=ALU.subtract,
              op1=ALU.mult)
        pb.op("scalar", "activation", wait=[("p1v", None)], inc="p1s",
              out=wt32b[:], in_=g32[:], func=AF.Sigmoid)
        pb.op("vector", "tensor_scalar", wait=[("p1s", None)], inc="p1v",
              out=wt32b[:], in0=wt32b[:], scalar1=65536.0, scalar2=MAGIC,
              op0=ALU.mult, op1=ALU.add)
        pb.op("vector", "tensor_scalar", inc="p1v", out=wt32b[:],
              in0=wt32b[:], scalar1=MAGIC, scalar2=65535.0,
              op0=ALU.subtract, op1=ALU.min)
        pb.op("vector", "tensor_scalar", inc="p1v", out=wt32b[:],
              in0=wt32b[:], scalar1=1.0 / 65536.0, scalar2=0.0,
              op0=ALU.mult, op1=ALU.add)
        pb.op("vector", "tensor_tensor", inc="p1v", out=g32[:], in0=g32[:],
              in1=wt32b[:], op=ALU.mult)
        pb.op("vector", "tensor_reduce", inc="p1v", out=gcol[:, 7:8],
              in_=g32[:], axis=AXX, op=ALU.max, apply_absolute_value=True)
        pb.op("vector", "tensor_tensor", inc="p1m", out=gcol[:, 6:7],
              in0=gcol[:, 6:7], in1=gcol[:, 7:8], op=ALU.max)
        pb.dma("sync", g2_hbm.ap()[j * 128:(j + 1) * 128, :], g32[:], "d_p1o",
               wait=[("p1v", None)])
        pb.dma("sync", upq_hbm.ap()[j * 128:(j + 1) * 128, :], wt32[:],
               "d_p1o")
    part_max_to_slot(gcol[:, 6:7], 3, "p1m")
    ar_max()
    pb.dma("gpsimd", gcol[:, 7:8],
           qs_sum.ap()[0:1, 3:4].broadcast_to([128, 1]), "d_qsb",
           wait=[("cc_q", None), ("qsv", None)])
    pb.op("vector", "reciprocal", wait=[("d_qsb", None)], inc="qsv",
          out=gcol[:, 12:13], in_=gcol[:, 7:8])
    pb.op("vector", "tensor_scalar", inc="qsv", out=gcol[:, 12:13],
          in0=gcol[:, 12:13], scalar1=32767.0, scalar2=0.0, op0=ALU.mult,
          op1=ALU.add)
    pb.op("vector", "tensor_scalar", inc="qsv", out=gcol[:, 13:14],
          in0=gcol[:, 7:8], scalar1=1.0 / 32767.0, scalar2=0.0,
          op0=ALU.mult, op1=ALU.add)

    pb.op("vector", "memset", gcol[:, 5:6], 0.0, wait=[("qsv", None)],
          inc="p2m")
    for j in range(16):
        pb.dma("gpsimd", g32[:], g2_hbm.ap()[j * 128:(j + 1) * 128, :],
               "d_p2", wait=[("p2v", None), ("d_p2o", None), ("d_p1o", None),
                             ("d_qsp", None)])
        pb.dma("gpsimd", wt32[:], upq_hbm.ap()[j * 128:(j + 1) * 128, :],
               "d_p2")
        pb.op("vector", "tensor_scalar", wait=[("d_p2", None), ("d_ug", None)],
              inc="p2v", out=g32[:], in0=g32[:], scalar1=gcol[:, 12:13],
              scalar2=MAGIC, op0=ALU.mult, op1=ALU.add)
        pb.op("vector", "tensor_scalar", inc="p2v", out=g32[:], in0=g32[:],
              scalar1=MAGIC, scalar2=gcol[:, 13:14], op0=ALU.subtract,
              op1=ALU.mult)
        pb.op("vector", "tensor_tensor", inc="p2v", out=g32[:], in0=g32[:],
              in1=wt32[:], op=ALU.mult)
        pb.op("vector", "tensor_reduce", inc="p2v", out=gcol[:, 7:8],
              in_=g32[:], axis=AXX, op=ALU.max, apply_absolute_value=True)
        pb.op("vector", "tensor_tensor", inc="p2m", out=gcol[:, 5:6],
              in0=gcol[:, 5:6], in1=gcol[:, 7:8], op=ALU.max)
        pb.dma("sync", up_hbm.ap()[j * 128:(j + 1) * 128, :], g32[:], "d_p2o",
               wait=[("p2v", None)])
    part_max_to_slot(gcol[:, 5:6], 0, "p2m")
    ar_max()
    pb.dma("gpsimd", gcol[:, 7:8],
           qs_sum.ap()[0:1, 0:1].broadcast_to([128, 1]), "d_qsb",
           wait=[("cc_q", None), ("qsv", None)])
    pb.op("vector", "reciprocal", wait=[("d_qsb", None)], inc="qsv",
          out=gcol[:, 14:15], in_=gcol[:, 7:8])
    pb.op("vector", "tensor_scalar", inc="qsv", out=gcol[:, 14:15],
          in0=gcol[:, 14:15], scalar1=32767.0, scalar2=0.0, op0=ALU.mult,
          op1=ALU.add)
    pb.op("vector", "tensor_scalar", inc="qsv", out=gcol[:, 15:16],
          in0=gcol[:, 7:8], scalar1=1.0 / 32767.0, scalar2=0.0,
          op0=ALU.mult, op1=ALU.add)

    pb.op("gpsimd", "dma_start", inc="d_wd", winc=16,
          out=wbig[:].rearrange("p (c e) -> p c e", c=24),
          in_=wddq_hbm.ap().rearrange("(c p) e -> p c e", p=128),
          wait=[("upmm", None), ("gamm", None)])
    pb.wait("tensor", "d_wd")
    for j in range(16):
        pb.dma("gpsimd", g32[:], up_hbm.ap()[j * 128:(j + 1) * 128, :],
               "d_p3", wait=[("p3v", None), ("d_p2o", None), ("d_qsp", None)])
        pb.op("vector", "tensor_scalar", wait=[("d_p3", None), ("p2v", None)],
              inc="p3v", out=g32[:], in0=g32[:], scalar1=gcol[:, 14:15],
              scalar2=MAGIC, op0=ALU.mult, op1=ALU.add)
        pb.op("vector", "tensor_scalar", inc="p3v", out=g32[:], in0=g32[:],
              scalar1=MAGIC, scalar2=gcol[:, 15:16], op0=ALU.subtract,
              op1=ALU.mult)
        pb.op("vector", "tensor_copy", inc="p3v", out=o_strip[:, 0:3072]
              if False else wqb[:, 0:3072], in_=g32[:])
        for fc in range(24):
            pk = psum[4 + (fc % 2)]
            tp_wait = [("p3v", None)]
            if pb.val("p3tp") >= 2:
                tp_wait.append(("p3e", pb.val("p3e") - 1))
            pb.op("tensor", "transpose", pk[:, 0:128],
                  wqb[:, fc * 128:(fc + 1) * 128], eye_sb[:],
                  wait=tp_wait, inc="p3tp")
            pb.op("vector", "tensor_copy",
                  wait=[("p3tp", None), ("upmm", None), ("gamm", None)],
                  inc="p3e",
                  out=o_strip[:, fc * 128:(fc + 1) * 128], in_=pk[:, 0:128])
        for eh in range(2):
            pe = psum[eh]
            for k in range(24):
                mmw = None
                if k == 0:
                    mmw = [("p3e", None), ("dnev", None)]
                pb.op("tensor", "matmul", pe[:, 0:512],
                      o_strip[:, k * 128:(k + 1) * 128],
                      wbig[:, k * 1024 + eh * 512: k * 1024 + eh * 512 + 512],
                      start=(k == 0), stop=(k == 23), wait=mmw, inc="dnmm")
            pb.op("vector", "tensor_tensor", wait=[("dnmm", None)], inc="dnev",
                  out=stream[:, j * 1024 + eh * 512: j * 1024 + eh * 512 + 512],
                  in0=stream[:, j * 1024 + eh * 512: j * 1024 + eh * 512 + 512],
                  in1=pe[:, 0:512], op=ALU.add)

    stream_ln([("dnev", None), ("d_dbg", None)], inplace=True)
    estack.close()

    for j in range(16):
        pb.dma("gpsimd", out[j * 128:(j + 1) * 128, :],
               stream[:, j * 1024:(j + 1) * 1024], "d_out",
               wait=[("lnap", None)])
    pb.wait("gpsimd", "d_out")
    if "after_mixer" in dbg:
        pb.wait("gpsimd", "d_dbg")

    pb.replay(stack.enter_context(nc.Block()))
    stack.close()
    return nc



def _prep_core_inputs(c, I):
    h, b = c % 2, c // 2
    f32 = np.float32
    perm = np.concatenate([np.arange(h * DH, (h + 1) * DH),
                           np.arange((1 - h) * DH, (2 - h) * DH)])
    own = perm[:DH]
    ipw = np.asarray(I["L0_attn_in_proj_w"], f32)
    xr_w = ipw[:DI][perm]
    z_w = ipw[DI:][own]
    winT_full = np.concatenate([xr_w, z_w], 0).T
    winT = np.ascontiguousarray(
        winT_full.reshape(8, 128, 24, 128).transpose(2, 1, 0, 3)
        .reshape(24, 128, 1024))
    conv_w = np.asarray(I["L0_attn_conv_w"], f32)[perm, 0, :]
    conv_w_cols = np.ascontiguousarray(
        conv_w.reshape(16, 128, 4).transpose(1, 0, 2).reshape(128, 64))
    conv_b_cols = np.ascontiguousarray(
        np.asarray(I["L0_attn_conv_b"], f32)[perm].reshape(16, 128).T)
    ln1w_col = np.ascontiguousarray(
        np.asarray(I["L0_attn_ln1_w"], f32).reshape(8, 128).T)
    xpw = np.asarray(I["L0_attn_x_proj_w"], f32)[:, perm]
    xpw_pad = np.zeros((112, DI), f32)
    xpw_pad[0:80] = xpw[0:80]
    xpw_pad[96:112] = xpw[80:96]
    xprojT = np.ascontiguousarray(xpw_pad.T)
    dtwT = np.ascontiguousarray(np.asarray(I["L0_attn_dt_proj_w"], f32)[own].T)
    dtb_col = np.ascontiguousarray(
        np.asarray(I["L0_attn_dt_proj_b"], f32)[own].reshape(8, 128).T)
    alog_scan = np.ascontiguousarray(
        np.asarray(I["L0_attn_A_log"], f32)[own].reshape(128, 128).T)
    dp_col = np.ascontiguousarray(
        np.asarray(I["L0_attn_Dp"], f32)[own].reshape(8, 128).T)
    woutT = np.ascontiguousarray(
        np.asarray(I["L0_attn_out_proj_w"], f32)[:, own].T)

    bf = ml_dtypes.bfloat16
    sel16 = np.zeros((16, 128), bf)
    for k in range(16):
        sel16[k, np.arange(k, 128, 16)] = 1
    sely = np.zeros((128, 16 * 128), bf)
    for s in range(16):
        for p in range(128):
            sely[p, s * 128 + s * 8 + p // 16] = 1
    eye128 = np.eye(128, dtype=bf)

    fcT = np.asarray(I["L0_attn_fc_w"], f32).T
    fcT_tiled = np.ascontiguousarray(
        fcT.reshape(8, 128, 32, 128).transpose(2, 1, 0, 3).reshape(32, 128, 1024))
    projT = np.ascontiguousarray(np.asarray(I["L0_attn_proj_w"], f32).T)
    w8T = np.ascontiguousarray(
        np.asarray(I["L0_ffn_weight"]).T.astype(ml_dtypes.bfloat16))
    ws_in = np.asarray(I["L0_ffn_weight_scale"], f32).reshape(1, 1)
    bias_row_in = np.ascontiguousarray(
        np.asarray(I["L0_ffn_bias"], f32).reshape(1, 1024))
    ln2w_col_in = np.ascontiguousarray(
        np.asarray(I["L0_attn_ln2_w"], f32).reshape(8, 128).T)

    def tile_w(w):
        return w.reshape(8, 128, 8, 128).transpose(2, 1, 0, 3).reshape(8, 128, 1024)

    l1_qkvo = np.zeros((4, 4, 8, 128, 1024), f32)
    for i in range(4):
        l1_qkvo[i, 0] = tile_w(np.asarray(I["L1_attn_Wq"], f32)[i])
        l1_qkvo[i, 1] = tile_w(np.asarray(I["L1_attn_Wk"], f32)[i])
        l1_qkvo[i, 2] = tile_w(np.asarray(I["L1_attn_Wv"], f32)[i])
        l1_qkvo[i, 3] = tile_w(np.asarray(I["L1_attn_Wo"], f32)[i])
    l1_w1 = np.zeros((4, 32, 128, 1024), f32)
    for i in range(4):
        w1 = np.asarray(I["L1_attn_W1"], f32)[i]
        l1_w1[i] = w1.reshape(8, 128, 32, 128).transpose(2, 1, 0, 3) \
            .reshape(32, 128, 1024)
    l1_w2 = np.ascontiguousarray(np.asarray(I["L1_attn_W2"], f32))
    l1_bqkvo = np.zeros((4, 4, 128, 8), f32)
    for i in range(4):
        for wi, nm in enumerate(("bq", "bk", "bv", "bo")):
            l1_bqkvo[i, wi] = np.asarray(I["L1_attn_" + nm], f32)[i] \
                .reshape(8, 128).T
    l1_b1c = np.zeros((4, 128, 32), f32)
    for i in range(4):
        l1_b1c[i] = np.asarray(I["L1_attn_b1"], f32)[i].reshape(32, 128).T
    l1_b2r = np.asarray(I["L1_attn_b2"], f32).reshape(4, 1, 1024)
    l1_lnsb = np.stack([
        np.stack([np.asarray(I["L1_attn_ln1_s"], f32)[i].reshape(1, 1024),
                  np.asarray(I["L1_attn_ln1_b"], f32)[i].reshape(1, 1024),
                  np.asarray(I["L1_attn_ln2_s"], f32)[i].reshape(1, 1024),
                  np.asarray(I["L1_attn_ln2_b"], f32)[i].reshape(1, 1024)])
        for i in range(4)])

    x = np.asarray(I["L0_attn_x"], f32)
    return {
        "eye128f": np.eye(128, dtype=f32),
        "wup_in": np.ascontiguousarray(np.asarray(I["L1_ffn_w_gate"], f32)) if False
        else np.ascontiguousarray(np.asarray(I["L1_ffn_w_up"], f32)),
        "wg_in": np.ascontiguousarray(np.asarray(I["L1_ffn_w_gate"], f32)),
        "wd_in": np.ascontiguousarray(np.asarray(I["L1_ffn_w_down"], f32)),
        "l1_qkvo": l1_qkvo, "l1_w1": l1_w1, "l1_w2": l1_w2,
        "l1_bqkvo": l1_bqkvo, "l1_b1c": l1_b1c, "l1_b2r": l1_b2r,
        "l1_lnsb": np.ascontiguousarray(l1_lnsb),
        "fcT_tiled": fcT_tiled, "projT": projT, "w8T": w8T, "ws_in": ws_in,
        "bias_row_in": bias_row_in, "ln2w_col_in": ln2w_col_in,
        "x_batch": np.ascontiguousarray(x[b]),
        "x_own": np.ascontiguousarray(x[b, h * TOWN:(h + 1) * TOWN]),
        "winT": winT, "conv_w_cols": conv_w_cols, "conv_b_cols": conv_b_cols,
        "ln1w_col": ln1w_col, "xprojT": xprojT, "dtwT": dtwT,
        "dtb_col": dtb_col, "alog_scan": alog_scan, "dp_col": dp_col,
        "woutT": woutT, "sel16": sel16, "sely": sely, "eye128": eye128,
    }


_DEBUG = ()


def kernel(**inputs):
    nc = build_program(debug=_DEBUG)
    in_maps = [_prep_core_inputs(c, inputs) for c in range(NC)]
    res = run_bass_kernel_spmd(nc, in_maps, core_ids=list(range(NC)))
    outs = [np.asarray(res.results[c]["out"]) for c in range(NC)]
    full = np.stack(outs).reshape(OUT_SHAPE)
    if _DEBUG:
        kernel._dbg = [{k: np.asarray(v) for k, v in res.results[c].items()}
                       for c in range(NC)]
    return full


# version 03
# speedup vs baseline: 1.2093x; 1.0176x over previous
import contextlib

import numpy as np
import ml_dtypes
import concourse.bass as bass
import concourse.mybir as mybir
from concourse.bass_utils import run_bass_kernel_spmd

F32 = mybir.dt.float32
BF16 = mybir.dt.bfloat16
FP8 = mybir.dt.float8e4
AF = mybir.ActivationFunctionType
ALU = mybir.AluOpType
AXX = mybir.AxisListType.X

MAGIC = 3 * 2.0 ** 22

D = 1024
DI = 2048
DH = 1024
NST = 16
DTR = 64
S = 4096
TOWN = 2048
NC = 8
TC = 1024
NCHUNK = S // TC
PAIRS = [[2 * k, 2 * k + 1] for k in range(NC // 2)]
ALLG = [list(range(NC))]
OUT_SHAPE = (4, 4096, 1024)

ENGINES = ("sync", "vector", "scalar", "tensor", "gpsimd")


class PB:

    def __init__(self, nc):
        self.nc = nc
        self.q = {e: [] for e in ENGINES}
        self.cnt = {}
        self.sems = {}
        self.alias = {}
        self.autodrain = True
        self._stack = contextlib.ExitStack()

    def sem(self, name):
        if name not in self.sems:
            self.sems[name] = self._stack.enter_context(self.nc.semaphore(name))
            self.cnt[name] = 0
        return self.sems[name]

    KEEP = {"lntpe", "ipmm", "fcmm", "pjmm", "qkmm", "womm", "dqe", "pjev",
            "p3e", "cc_mix", "cc_kv", "cc_q", "sc_v", "sc_s", "scn", "ymm"}
    CANON = {"vector": "cv", "scalar": "cs", "tensor": "ct", "gpsimd": "cg"}

    def op(self, engine, method, *args, inc=None, winc=1, wait=None, **kw):
        if (inc is not None and method != "dma_start"
                and method != "collective_compute" and inc not in self.KEEP):
            canon = self.CANON[engine]
            old = self.alias.get(inc)
            assert old in (None, canon), (inc, old, canon)
            self.alias[inc] = canon
            inc = canon
        waits = []
        if wait:
            if not isinstance(wait, list):
                wait = [wait]
            for w in wait:
                if isinstance(w, str):
                    wname, wval = w, None
                else:
                    wname, wval = w
                wname = self.alias.get(wname, wname)
                self.sem(wname)
                wv = self.cnt[wname] if wval is None else wval
                if wv > 0:
                    waits.append((self.sem(wname), wv))
        sem = None
        if inc is not None:
            sem = self.sem(inc)
            self.cnt[inc] += winc

        drain = kw.pop("drain", False)
        if (self.autodrain and engine in ("vector", "scalar", "gpsimd")
                and method not in ("dma_start", "collective_compute")):
            drain = True

        def fn(eng, _m=method, _a=args, _k=kw, _s=sem, _w=waits, _wi=winc,
               _d=drain):
            for (sh, wv) in _w:
                eng.wait_ge(sh, wv)
            if _d:
                eng.drain()
            r = getattr(eng, _m)(*_a, **_k)
            if _s is not None:
                r.then_inc(_s, _wi)
        self.q[engine].append(fn)

    def dma(self, engine, out, in_, inc, wait=None, **kw):
        self.op(engine, "dma_start", inc=inc, winc=16, wait=wait,
                out=out, in_=in_, **kw)

    def wait(self, engine, name, value=None):
        name = self.alias.get(name, name)
        self.sem(name)
        v = self.cnt[name] if value is None else value
        if v <= 0:
            return
        sem = self.sem(name)
        self.q[engine].append(lambda eng, _s=sem, _v=v: eng.wait_ge(_s, _v))

    def val(self, name):
        name = self.alias.get(name, name)
        self.sem(name)
        return self.cnt[name]

    def replay(self, block):
        pb = self

        @block.sync
        def _(eng):
            for fn in pb.q["sync"]:
                fn(eng)

        @block.vector
        def _(eng):
            for fn in pb.q["vector"]:
                fn(eng)

        @block.scalar
        def _(eng):
            for fn in pb.q["scalar"]:
                fn(eng)

        @block.tensor
        def _(eng):
            for fn in pb.q["tensor"]:
                fn(eng)

        @block.gpsimd
        def _(eng):
            for fn in pb.q["gpsimd"]:
                fn(eng)


def build_program(debug=()):
    nc = bass.Bass("TRN2", target_bir_lowering=False, num_devices=NC)
    stack = contextlib.ExitStack()

    def din(name, shape, dt=F32):
        return nc.dram_tensor(name, list(shape), dt, kind="ExternalInput").ap()

    def sbt(name, shape, dt=F32):
        return stack.enter_context(nc.sbuf_tensor(name, list(shape), dt))

    x_batch = din("x_batch", [S, D])
    x_own = din("x_own", [TOWN, D])
    winT = din("winT", [24, 128, 1024])
    conv_w_cols = din("conv_w_cols", [128, 64])
    conv_b_cols = din("conv_b_cols", [128, 16])
    ln1w_col = din("ln1w_col", [128, 8])
    ln2w_col_in = din("ln2w_col_in", [128, 8])
    xprojT = din("xprojT", [DI, 112])
    dtwT = din("dtwT", [DTR, DH])
    dtb_col = din("dtb_col", [128, 8])
    alog_scan = din("alog_scan", [128, 128])
    dp_col = din("dp_col", [128, 8])
    woutT = din("woutT", [DH, D])
    fcT_tiled = din("fcT_tiled", [32, 128, 1024])
    l1_qkvo = din("l1_qkvo", [4, 4, 8, 128, 1024])
    l1_w1 = din("l1_w1", [4, 32, 128, 1024])
    l1_w2 = din("l1_w2", [4, 4096, 1024])
    l1_bqkvo = din("l1_bqkvo", [4, 4, 128, 8])
    l1_b1c = din("l1_b1c", [4, 128, 32])
    l1_b2r = din("l1_b2r", [4, 1, 1024])
    l1_lnsb = din("l1_lnsb", [4, 4, 1, 1024])
    projT = din("projT", [4096, 1024])
    w8T = din("w8T", [D, D], BF16)
    ws_in = din("ws_in", [1, 1])
    bias_row_in = din("bias_row_in", [1, D])
    sel16 = din("sel16", [16, 128], BF16)
    sely = din("sely", [128, 16 * 128], BF16)
    eye128 = din("eye128", [128, 128], BF16)
    eye128f = din("eye128f", [128, 128])
    wup_in = din("wup_in", [3072, 1024])
    wg_in = din("wg_in", [3072, 1024])
    wd_in = din("wd_in", [1024, 3072])

    out = nc.dram_tensor("out", [TOWN, D], F32, kind="ExternalOutput").ap()
    dbg = {}
    for name, shape in debug:
        dbg[name] = nc.dram_tensor("dbg_" + name, list(shape), F32,
                                   kind="ExternalOutput").ap()

    dt_hbm = nc.dram_tensor("dt_hbm", [DH, 1, TC], F32)
    w_hbm = nc.dram_tensor("w_hbm", [DH, 1, TC], F32)
    mix_part = nc.dram_tensor("mix_part", [2 * TOWN, D], F32)
    mix_sum = nc.dram_tensor("mix_sum", [TOWN, D], F32)

    pb = PB(nc)
    pb.autodrain = False

    xnb16 = sbt("xnb16", [128, 1024], BF16)
    sq32 = sbt("sq32", [128, 1024])
    stat = sbt("stat", [128, 16])
    ccol = sbt("ccol", [128, 128])
    kcol = sbt("kcol", [128, 8])
    eye_sb = sbt("eye_sb", [128, 128], BF16)
    mstack = contextlib.ExitStack()

    def mbt(name, shape, dt=F32):
        return mstack.enter_context(nc.sbuf_tensor(name, list(shape), dt))

    xtile = mbt("xtile", [128, 1024])
    sp32a = mbt("sp32a", [128, 1024])
    sp32b = mbt("sp32b", [128, 1024])
    ev32 = mbt("ev32", [128, 1024])
    xn_fm = mbt("xn_fm", [128, 8 * TC], BF16)
    wib = mbt("wib", [128, 2 * 1024], BF16)
    xr_rot = mbt("xr_rot", [128, TC + 4])
    halo = mbt("halo", [128, 64])
    cacc = mbt("cacc", [128, TC])
    xc_own = mbt("xc_own", [128, 8 * TC], BF16)
    xc_rot = mbt("xc_rot", [128, TC], BF16)
    z_b = mbt("z_b", [128, 8 * TC], BF16)
    xprojT_sb = mbt("xprojT_sb", [128, 16 * 112], BF16)
    dtw_sb = mbt("dtw_sb", [64, DH], BF16)
    wout_sb = mbt("wout_sb", [128, 8 * 1024], BF16)
    dcb_dtr = mbt("dcb_dtr", [64, TC], BF16)
    dcb_bc = mbt("dcb_bc", [16, TC], BF16)
    dcb_cc = mbt("dcb_cc", [16, TC], BF16)
    dt32 = mbt("dt32", [128, TC])
    w32 = mbt("w32", [128, TC])
    bcrep = mbt("bcrep", [128, TC], BF16)
    ccrep = mbt("ccrep", [128, TC], BF16)
    dtrep = mbt("dtrep", [128, 2 * TC], BF16)
    wrep = mbt("wrep", [128, 2 * TC], BF16)
    dA_b = mbt("dA_b", [128, 2 * TC], BF16)
    dBx_b = mbt("dBx_b", [128, 2 * TC], BF16)
    h32 = mbt("h32", [128, 2 * TC])
    hC_b = mbt("hC_b", [128, 2 * TC], BF16)
    hcarry = mbt("hcarry", [128, 128])
    yg_b = mbt("yg_b", [128, 8 * TC], BF16)
    zsil = mbt("zsil", [128, 512])
    acol = mbt("acol", [128, 128])
    sely_sb = mbt("sely_sb", [128, 16 * 128], BF16)
    sel16_sb = mbt("sel16_sb", [16, 128], BF16)

    psum = [stack.enter_context(nc.psum_tensor(
                f"ps{i}", [128, 512], BF16 if i in (4, 5) else F32))
            for i in range(8)]

    LN1W, CONVW, CONVB, DTB, DP = 0, 16, 80, 96, 104

    pb.dma("sync", ccol[:, LN1W:LN1W + 8], ln1w_col, "cld")
    pb.dma("sync", ccol[:, CONVW:CONVW + 64], conv_w_cols, "cld")
    pb.dma("sync", ccol[:, CONVB:CONVB + 16], conv_b_cols, "cld")
    pb.dma("sync", ccol[:, DTB:DTB + 8], dtb_col, "cld")
    pb.dma("sync", ccol[:, DP:DP + 8], dp_col, "cld")
    pb.dma("sync", sel16_sb[:], sel16, "cld")
    pb.dma("sync", sely_sb[:], sely, "cld")
    pb.dma("sync", eye_sb[:], eye128, "cld")
    pb.dma("sync", acol[:], alog_scan, "cld")
    pb.op("gpsimd", "dma_start", inc="cldg", winc=16,
          out=xprojT_sb[:].rearrange("p (c r) -> p c r", c=16),
          in_=xprojT.rearrange("(c p) r -> p c r", p=128))
    pb.dma("gpsimd", dtw_sb[:], dtwT, "cldg")
    pb.op("gpsimd", "dma_start", inc="cldg", winc=16,
          out=wout_sb[:].rearrange("p (c e) -> p c e", c=8),
          in_=woutT.rearrange("(c p) e -> p c e", p=128))

    pb.op("scalar", "activation", wait=[("cld", None)], inc="aprep",
          out=acol[:], in_=acol[:], func=AF.Exp)
    pb.op("vector", "tensor_scalar", wait=[("aprep", None)], inc="aprepv",
          out=acol[:], in0=acol[:], scalar1=-1.0, scalar2=0.0,
          op0=ALU.mult, op1=ALU.add)
    apv_val = pb.val("aprepv")
    pb.op("vector", "memset", kcol[:, 0:1], 1e-5, inc="minit")
    pb.op("vector", "memset", kcol[:, 1:2], 1.0, inc="minit")
    pb.op("vector", "memset", kcol[:, 2:3], 1e-6, inc="minit")
    pb.op("vector", "memset", halo[:], 0.0, inc="minit")
    pb.op("vector", "memset", hcarry[:], 0.0, inc="minit")
    pb.wait("vector", "cld")
    pb.wait("scalar", "cld")
    pb.wait("tensor", "cld")
    pb.wait("tensor", "cldg")

    for ci in range(NCHUNK):
        t0 = ci * TC

        for j in range(TC // 128):
            pb.dma("sync", xtile[:], x_batch[t0 + j * 128: t0 + (j + 1) * 128, :],
                   "d_xld", wait=[("lnap", None)])
            pb.op("scalar", "activation",
                  wait=[("d_xld", None), ("st_v", None)], inc="st_s",
                  out=sq32[:], in_=xtile[:], func=AF.Square,
                  accum_out=stat[:, 0:1])
            pb.op("vector", "tensor_reduce",
                  wait=[("d_xld", None), ("st_s", None)],
                  inc="st_v", out=stat[:, 1:2], in_=xtile[:], axis=AXX,
                  op=ALU.add)
            pb.op("vector", "tensor_scalar", inc="st_v", out=stat[:, 2:3],
                  in0=stat[:, 1:2], scalar1=1.0 / D, scalar2=0.0,
                  op0=ALU.mult, op1=ALU.add, drain=True)
            pb.op("vector", "tensor_scalar", inc="st_v", out=stat[:, 3:4],
                  in0=stat[:, 0:1], scalar1=1.0 / D, scalar2=0.0,
                  op0=ALU.mult, op1=ALU.add)
            pb.op("vector", "tensor_tensor", inc="st_v", out=stat[:, 4:5],
                  in0=stat[:, 2:3], in1=stat[:, 2:3], op=ALU.mult, drain=True)
            pb.op("vector", "tensor_tensor", inc="st_v", out=stat[:, 5:6],
                  in0=stat[:, 3:4], in1=stat[:, 4:5], op=ALU.subtract,
                  drain=True)
            pb.op("scalar", "activation",
                  wait=[("st_v", None), ("minit", None)], inc="st_s2",
                  out=stat[:, 6:7], in_=stat[:, 5:6], func=AF.Sqrt,
                  bias=kcol[:, 0:1], scale=1.0)
            pb.op("vector", "reciprocal", wait=[("st_s2", None)], inc="st_v",
                  out=stat[:, 7:8], in_=stat[:, 6:7])
            pb.op("vector", "tensor_scalar", inc="lnap",
                  out=xnb16[:], in0=xtile[:], scalar1=stat[:, 2:3],
                  scalar2=stat[:, 7:8], op0=ALU.subtract, op1=ALU.mult,
                  drain=True)
            for k in range(8):
                pk = psum[4 + (k % 2)]
                tp_wait = [("lnap", None)]
                if pb.val("lntpe") >= 2:
                    tp_wait.append(("lntpe", pb.val("lntpe") - 1))
                pb.op("tensor", "transpose", pk[:, 0:128],
                      xnb16[:, k * 128:(k + 1) * 128], eye_sb[:],
                      wait=tp_wait, inc="lntpp")
                pb.op("vector", "tensor_scalar", wait=[("lntpp", None)],
                      inc="lntpe",
                      out=xn_fm[:, k * TC + j * 128: k * TC + (j + 1) * 128],
                      in0=pk[:, 0:128], scalar1=ccol[:, LN1W + k:LN1W + k + 1],
                      scalar2=0.0, op0=ALU.mult, op1=ALU.add)

        for et in range(24):
            wb = wib[:, (et % 2) * 1024:(et % 2) * 1024 + 1024]
            wsem = "d_wib%d" % (et % 2)
            dma_wait = None
            if pb.val("ipmm") >= 32:
                dma_wait = [("ipmm", pb.val("ipmm") - 16)]
            pb.op("gpsimd", "dma_start", inc=wsem, winc=16, wait=dma_wait,
                  out=wb, in_=winT[et])
            pb.wait("tensor", wsem)
            pb.wait("tensor", "lntpe")
            for th in range(2):
                pe = psum[th] if th == 0 else psum[6 + (et % 2)]
                if th == 0:
                    pe = psum[et % 2]
                for k in range(8):
                    mm_wait = [("scn", None)] if False else None
                    if k == 0:
                        mm_wait = [("ipev", None), ("cnv", None),
                                   ("xpmm", None)]
                    pb.op("tensor", "matmul", pe[:, 0:512],
                          wb[:, k * 128:(k + 1) * 128],
                          xn_fm[:, k * TC + th * 512: k * TC + th * 512 + 512],
                          start=(k == 0), stop=(k == 7),
                          wait=mm_wait, inc="ipmm")
            if et < 16:
                pb.op("vector", "tensor_copy", wait=[("ipmm", None)],
                      inc="ipev", out=xr_rot[:, 4:4 + 512],
                      in_=psum[et % 2][:, 0:512])
                pb.op("vector", "tensor_copy", inc="ipev",
                      out=xr_rot[:, 4 + 512:4 + 1024],
                      in_=psum[6 + (et % 2)][:, 0:512])
                pb.op("vector", "tensor_copy", inc="cnv",
                      out=xr_rot[:, 0:4], in_=halo[:, et * 4:(et + 1) * 4])
                pb.op("vector", "tensor_scalar", inc="cnv", out=cacc[:],
                      drain=True, in0=xr_rot[:, 1:1 + TC],
                      scalar1=ccol[:, CONVW + et * 4:CONVW + et * 4 + 1],
                      scalar2=0.0, op0=ALU.mult, op1=ALU.add)
                for tap in range(1, 4):
                    pb.op("vector", "scalar_tensor_tensor", inc="cnv",
                          drain=True, out=cacc[:],
                          in0=xr_rot[:, tap + 1:tap + 1 + TC],
                          scalar=ccol[:, CONVW + et * 4 + tap:
                                       CONVW + et * 4 + tap + 1],
                          in1=cacc[:], op0=ALU.mult, op1=ALU.add)
                pb.op("vector", "tensor_copy", inc="cnv",
                      out=halo[:, et * 4:(et + 1) * 4],
                      in_=xr_rot[:, TC:TC + 4])
                if et < 8:
                    xc_dst = xc_own[:, et * TC:(et + 1) * TC]
                else:
                    xc_dst = xc_rot[:]
                pb.op("scalar", "activation",
                      wait=[("cnv", None), ("xpmm", None)], inc="cnv_s",
                      out=xc_dst, in_=cacc[:], func=AF.Silu,
                      bias=ccol[:, CONVB + et:CONVB + et + 1], scale=1.0)
                for th in range(2):
                    pb.op("tensor", "matmul", psum[2 + th][0:112, 0:512],
                          xprojT_sb[:, et * 112:(et + 1) * 112],
                          xc_dst[:, th * 512:th * 512 + 512],
                          start=(et == 0), stop=(et == 15),
                          wait=[("cnv_s", None)], inc="xpmm")
            else:
                zi = (et - 16) * TC
                pb.op("vector", "tensor_copy", wait=[("ipmm", None)],
                      inc="ipev", out=z_b[:, zi:zi + 512],
                      in_=psum[et % 2][:, 0:512])
                pb.op("vector", "tensor_copy", inc="ipev",
                      out=z_b[:, zi + 512:zi + 1024],
                      in_=psum[6 + (et % 2)][:, 0:512])

        for th in range(2):
            pb.op("vector", "tensor_copy", wait=[("xpmm", None)], inc="xde",
                  out=dcb_dtr[:, th * 512:th * 512 + 512],
                  in_=psum[2 + th][0:64, 0:512])
            pb.op("vector", "tensor_copy", inc="xde",
                  out=dcb_bc[:, th * 512:th * 512 + 512],
                  in_=psum[2 + th][64:80, 0:512])
            pb.op("vector", "tensor_copy", inc="xde",
                  out=dcb_cc[:, th * 512:th * 512 + 512],
                  in_=psum[2 + th][96:112, 0:512])

        for th in range(2):
            pb.op("tensor", "matmul", psum[2][:, 0:512], sel16_sb[:],
                  dcb_bc[:, th * 512:th * 512 + 512], start=True, stop=True,
                  wait=[("xde", None), ("bcre", None)], inc="bcr")
            pb.op("vector", "tensor_copy", wait=[("bcr", None)], inc="bcre",
                  out=bcrep[:, th * 512:th * 512 + 512], in_=psum[2][:, 0:512])
            pb.op("tensor", "matmul", psum[3][:, 0:512], sel16_sb[:],
                  dcb_cc[:, th * 512:th * 512 + 512], start=True, stop=True,
                  wait=[("bcre", None)], inc="bcr")
            pb.op("vector", "tensor_copy", wait=[("bcr", None)], inc="bcre",
                  out=ccrep[:, th * 512:th * 512 + 512], in_=psum[3][:, 0:512])

        for dtile in range(8):
            for th in range(2):
                mm_wait = [("xde", None), ("bcre", None), ("dtv", None)]
                pb.op("tensor", "matmul", psum[2 + th][:, 0:512],
                      dtw_sb[:, dtile * 128:(dtile + 1) * 128],
                      dcb_dtr[:, th * 512:th * 512 + 512],
                      start=True, stop=True, wait=mm_wait, inc="dtmm")
            for th in range(2):
                ps = psum[2 + th]
                sl = slice(th * 512, th * 512 + 512)
                pb.op("vector", "tensor_scalar",
                      wait=[("dtmm", None), ("d_dtw", None)],
                      inc="dtv", out=sp32a[:, sl], in0=ps[:, 0:512],
                      scalar1=ccol[:, DTB + dtile:DTB + dtile + 1],
                      scalar2=0.0, op0=ALU.add, op1=ALU.add)
                pb.op("vector", "tensor_scalar", inc="dtv", out=sp32b[:, sl],
                      in0=sp32a[:, sl], scalar1=-1.0, scalar2=0.0,
                      op0=ALU.mult, op1=ALU.add, drain=True)
                pb.op("vector", "tensor_tensor", inc="dtv", out=sp32b[:, sl],
                      in0=sp32a[:, sl], in1=sp32b[:, sl], op=ALU.max,
                      drain=True)
                pb.op("scalar", "activation", wait=[("dtv", None)], inc="dts",
                      out=sp32b[:, sl], in_=sp32b[:, sl], func=AF.Exp,
                      scale=-1.0)
                pb.op("scalar", "activation", inc="dts", out=sp32b[:, sl],
                      in_=sp32b[:, sl], func=AF.Ln, bias=kcol[:, 1:2],
                      scale=1.0, drain=True)
                pb.op("vector", "scalar_tensor_tensor", wait=[("dts", None)],
                      inc="dtev", out=dt32[:, sl], in0=sp32a[:, sl],
                      scalar=0.0, in1=sp32b[:, sl], op0=ALU.max, op1=ALU.add,
                      drain=True)
            pb.op("vector", "tensor_tensor", inc="dtev", out=w32[:],
                  in0=dt32[:], in1=xc_own[:, dtile * TC:(dtile + 1) * TC],
                  op=ALU.mult, drain=True)
            dtw_wait = [("dtev", None)]
            if ci > 0 and dtile == 0:
                dtw_wait.append(("d_rep0", None))
                dtw_wait.append(("d_rep1", None))
            pb.dma("sync", dt_hbm.ap()[dtile * 128:(dtile + 1) * 128, 0, :],
                   dt32[:], "d_dtw", wait=dtw_wait)
            pb.dma("sync", w_hbm.ap()[dtile * 128:(dtile + 1) * 128, 0, :],
                   w32[:], "d_dtw")

        for g in range(8):
            for i16 in range(16):
                s = g * 16 + i16
                rot = s % 2
                src_dt = dt_hbm.ap()[s * 8:(s + 1) * 8, :, :] \
                    .broadcast_to([8, 16, TC])
                src_w = w_hbm.ap()[s * 8:(s + 1) * 8, :, :] \
                    .broadcast_to([8, 16, TC])
                dst_dt = dtrep[:, rot * TC:rot * TC + TC]
                dst_w = wrep[:, rot * TC:rot * TC + TC]
                sN = ci * 128 + s
                repsem = "d_rep%d" % rot
                wdep = [("d_dtw", None)]
                if pb.val("sc_s") >= 2:
                    wdep.append(("sc_s", pb.val("sc_s") - 1))
                    wdep.append(("sc_v", pb.val("sc_v") - 5))
                pb.op("gpsimd", "dma_start", inc=repsem, winc=16,
                      out=dst_dt, in_=src_dt, wait=wdep)
                pb.op("gpsimd", "dma_start", inc=repsem, winc=16,
                      out=dst_w, in_=src_w)
                expw = [(repsem, None), ("aprepv", apv_val)]
                if pb.val("sc_v") >= 5:
                    expw.append(("sc_v", pb.val("sc_v") - 4))
                pb.op("scalar", "activation", wait=expw, inc="sc_s",
                      out=dA_b[:, rot * TC:rot * TC + TC],
                      in_=dtrep[:, rot * TC:rot * TC + TC],
                      func=AF.Exp, scale=acol[:, s:s + 1])
                pb.op("vector", "tensor_tensor",
                      wait=[(repsem, None)],
                      inc="sc_v", out=dBx_b[:, rot * TC:rot * TC + TC],
                      in0=wrep[:, rot * TC:rot * TC + TC], in1=bcrep[:],
                      op=ALU.mult)
                pb.op("vector", "tensor_tensor_scan", wait=[("sc_s", None)],
                      inc="sc_v", out=h32[:, rot * TC:rot * TC + TC],
                      data0=dA_b[:, rot * TC:rot * TC + TC],
                      data1=dBx_b[:, rot * TC:rot * TC + TC],
                      initial=hcarry[:, s:s + 1], op0=ALU.mult, op1=ALU.add,
                      drain=True)
                pb.op("vector", "tensor_copy", inc="sc_v",
                      out=hcarry[:, s:s + 1],
                      in_=h32[:, rot * TC + TC - 1:rot * TC + TC],
                      drain=True)
                hcw = []
                if pb.val("ymm") >= 4:
                    hcw.append(("ymm", pb.val("ymm") - 2))
                pb.op("vector", "tensor_tensor", inc="scn",
                      out=hC_b[:, rot * TC:rot * TC + TC],
                      in0=h32[:, rot * TC:rot * TC + TC], in1=ccrep[:],
                      op=ALU.mult, wait=hcw)
                for th in range(2):
                    ym_wait = [("scn", None)]
                    if i16 == 0:
                        ym_wait.append(("ygv", None))
                    pb.op("tensor", "matmul", psum[6 + th][:, 0:512],
                          sely_sb[:, i16 * 128:(i16 + 1) * 128],
                          hC_b[:, rot * TC + th * 512:rot * TC + th * 512 + 512],
                          start=(i16 == 0), stop=(i16 == 15),
                          wait=ym_wait, inc="ymm")
            for th in range(2):
                ps = psum[6 + th]
                sl = slice(g * TC + th * 512, g * TC + th * 512 + 512)
                pb.op("vector", "scalar_tensor_tensor", wait=[("ymm", None)],
                      inc="ygv0", drain=True, out=sp32a[:, 0:512],
                      in0=xc_own[:, sl], scalar=ccol[:, DP + g:DP + g + 1],
                      in1=ps[:, 0:512], op0=ALU.mult, op1=ALU.add)
                pb.op("scalar", "activation",
                      wait=[("ygv", None), ("ygv0", None)], inc="ygs",
                      out=zsil[:], in_=z_b[:, sl], func=AF.Silu)
                pb.op("vector", "tensor_tensor", wait=[("ygs", None)],
                      inc="ygv", out=yg_b[:, sl], in0=sp32a[:, 0:512],
                      in1=zsil[:], op=ALU.mult, drain=True)

        for tt in range(TC // 128):
            for th in range(2):
                for k in range(8):
                    mm_wait = [("ygv", None)]
                    if k == 0:
                        mm_wait.append(("opev", None))
                    pb.op("tensor", "matmul", psum[th][:, 0:512],
                          yg_b[:, k * TC + tt * 128: k * TC + (tt + 1) * 128],
                          wout_sb[:, k * 1024 + th * 512:
                                  k * 1024 + th * 512 + 512],
                          start=(k == 0), stop=(k == 7),
                          wait=mm_wait, inc="opmm")
                pb.op("vector", "tensor_copy",
                      wait=[("opmm", None), ("d_mp", None)], inc="opev",
                      out=ev32[:, th * 512:th * 512 + 512],
                      in_=psum[th][:, 0:512])
            pb.dma("sync", mix_part.ap()[t0 + tt * 128: t0 + (tt + 1) * 128, :],
                   ev32[:, 0:1024], "d_mp", wait=[("opev", None)])

    pb.wait("gpsimd", "d_mp")
    pb.op("gpsimd", "collective_compute", "ReduceScatter", ALU.add,
          replica_groups=PAIRS, ins=[mix_part.ap().opt()],
          outs=[mix_sum.ap().opt()], inc="cc_mix")

    mstack.close()
    stream = sbt("stream", [128, 16 * 1024])
    for j in range(16):
        pb.dma("sync", stream[:, j * 1024:(j + 1) * 1024],
               x_own[j * 128:(j + 1) * 128, :], "d_so",
               wait=[("opev", None), ("d_mp", None)])
    for j in range(16):
        pb.dma("gpsimd", sq32[:],
               mix_sum.ap()[j * 128:(j + 1) * 128, :], "d_sm",
               wait=[("cc_mix", None), ("res0", None)])
        pb.op("vector", "tensor_tensor",
              wait=[("d_so", None), ("d_sm", None)],
              inc="res0", out=stream[:, j * 1024:(j + 1) * 1024],
              in0=stream[:, j * 1024:(j + 1) * 1024], in1=sq32[:], op=ALU.add,
              drain=True)

    if "after_mixer" in dbg:
        for j in range(16):
            pb.dma("sync", dbg["after_mixer"][j * 128:(j + 1) * 128, :],
                   stream[:, j * 1024:(j + 1) * 1024], "d_dbg",
                   wait=[("res0", None)])


    bstack = contextlib.ExitStack()

    def bbt(name, shape, dt=F32):
        return bstack.enter_context(nc.sbuf_tensor(name, list(shape), dt))

    xfmA = sbt("xfmA", [128, 8 * 2048], BF16)
    wstr = sbt("wstr", [128, 2 * 2048], BF16)
    h_fm = bbt("h_fm", [128, 32 * 512], BF16)
    ln2w2_col = bbt("ln2w2_col", [128, 8])
    pb.dma("sync", ln2w2_col[:], ln2w_col_in, "d_ln2w")

    def stream_ln(ready, dst_fm=None, wcol=None, inplace=False):
        for j in range(16):
            xt = stream[:, j * 1024:(j + 1) * 1024]
            pb.op("vector", "bn_stats", wait=ready + [("st_s2", None)],
                  inc="st_v", out=stat[:, 0:6], in_=xt[:, 0:512])
            pb.op("vector", "bn_stats", inc="st_v", out=stat[:, 6:12],
                  in_=xt[:, 512:1024])
            pb.op("vector", "bn_aggr", inc="st_v", out=stat[:, 12:14],
                  in_=stat[:, 0:12], drain=True)
            pb.op("scalar", "activation", wait=[("st_v", None)], inc="st_s2",
                  out=stat[:, 14:15], in_=stat[:, 13:14], func=AF.Sqrt,
                  bias=kcol[:, 0:1], scale=1.0)
            pb.op("vector", "reciprocal", wait=[("st_s2", None)], inc="st_v",
                  out=stat[:, 15:16], in_=stat[:, 14:15])
            if inplace:
                pb.op("vector", "tensor_scalar", inc="lnap", out=xt, in0=xt,
                      scalar1=stat[:, 12:13], scalar2=stat[:, 15:16],
                      op0=ALU.subtract, op1=ALU.mult, drain=True)
            else:
                pb.op("vector", "tensor_scalar", inc="lnap", out=xnb16[:],
                      in0=xt, scalar1=stat[:, 12:13], scalar2=stat[:, 15:16],
                      op0=ALU.subtract, op1=ALU.mult, drain=True)
                for k in range(8):
                    pk = psum[4 + (k % 2)]
                    tp_wait = [("lnap", None)]
                    if pb.val("lntpe") >= 2:
                        tp_wait.append(("lntpe", pb.val("lntpe") - 1))
                    pb.op("tensor", "transpose", pk[:, 0:128],
                          xnb16[:, k * 128:(k + 1) * 128], eye_sb[:],
                          wait=tp_wait, inc="lntpp")
                    if wcol is None:
                        pb.op("vector", "tensor_copy", wait=[("lntpp", None)],
                              inc="lntpe",
                              out=dst_fm[:, k * 2048 + j * 128:
                                         k * 2048 + (j + 1) * 128],
                              in_=pk[:, 0:128])
                    else:
                        pb.op("vector", "tensor_scalar", wait=[("lntpp", None)],
                              inc="lntpe",
                              out=dst_fm[:, k * 2048 + j * 128:
                                         k * 2048 + (j + 1) * 128],
                              in0=pk[:, 0:128], scalar1=wcol[:, k:k + 1],
                              scalar2=0.0, op0=ALU.mult, op1=ALU.add)

    pb.wait("vector", "d_ln2w")
    stream_ln([("res0", None), ("d_dbg", None)], dst_fm=xfmA, wcol=ln2w2_col)

    for tch in range(4):
        ts0 = tch * 512
        for ft in range(32):
            par = ft % 2
            g = []
            if pb.val("fcmm") >= 16:
                g.append(("fcmm", pb.val("fcmm") - 8))
            if tch > 0 and ft < 2:
                g.append(("pjev", None))
            pb.op("gpsimd", "dma_start", inc="d_fc%d" % par, winc=16,
                  wait=g or None, out=wstr[:, par * 2048:par * 2048 + 1024],
                  in_=fcT_tiled[ft])
            pb.wait("tensor", "d_fc%d" % par)
            pb.wait("tensor", "lntpe")
            for k in range(8):
                mmw = None
                if k == 0:
                    mmw = [("fcev", None), ("pjmm", None), ("pjev", None)]
                pb.op("tensor", "matmul", psum[par][:, 0:512],
                      wstr[:, par * 2048 + k * 128:par * 2048 + (k + 1) * 128],
                      xfmA[:, k * 2048 + ts0: k * 2048 + ts0 + 512],
                      start=(k == 0), stop=(k == 7), wait=mmw, inc="fcmm")
            pb.op("scalar", "activation",
                  wait=[("fcmm", None), ("pjmm", None)], inc="fcev",
                  out=h_fm[:, ft * 512:(ft + 1) * 512],
                  in_=psum[par][:, 0:512], func=AF.Silu)
        for wave in range(2):
            for k in range(32):
                par = k % 2
                g = []
                if pb.val("pjmm") >= 8:
                    g.append(("pjmm", pb.val("pjmm") - 4))
                pb.op("gpsimd", "dma_start", inc="d_pj%d" % par, winc=16,
                      wait=g or None,
                      out=wstr[:, par * 2048 + 1024:par * 2048 + 2048],
                      in_=projT[k * 128:(k + 1) * 128, :])
                pb.wait("tensor", "d_pj%d" % par)
                for tt in range(2):
                    for eh in range(2):
                        mmw = None
                        if k == 0:
                            mmw = [("fcev", None), ("pjev", None)]
                        pb.op("tensor", "matmul", psum[tt * 2 + eh][:, 0:512],
                              h_fm[:, k * 512 + (wave * 2 + tt) * 128:
                                   k * 512 + (wave * 2 + tt + 1) * 128],
                              wstr[:, par * 2048 + 1024 + eh * 512:
                                   par * 2048 + 1024 + eh * 512 + 512],
                              start=(k == 0), stop=(k == 31),
                              wait=mmw, inc="pjmm")
            for tt in range(2):
                jj = 4 * tch + wave * 2 + tt
                for eh in range(2):
                    pb.op("vector", "tensor_tensor", wait=[("pjmm", None)],
                          inc="pjev",
                          out=stream[:, jj * 1024 + eh * 512:
                                     jj * 1024 + eh * 512 + 512],
                          in0=stream[:, jj * 1024 + eh * 512:
                                     jj * 1024 + eh * 512 + 512],
                          in1=psum[tt * 2 + eh][:, 0:512], op=ALU.add,
                          drain=True)

    for j in range(16):
        pb.dma("gpsimd", sq32[:], x_own[j * 128:(j + 1) * 128, :], "d_so2",
               wait=[("res2", None), ("pjev", None)])
        pb.op("vector", "tensor_tensor", wait=[("pjev", None), ("d_so2", None)],
              inc="res2", out=stream[:, j * 1024:(j + 1) * 1024],
              in0=stream[:, j * 1024:(j + 1) * 1024], in1=sq32[:], op=ALU.add,
              drain=True)
    stream_ln([("res2", None), ("d_dbg", None)], inplace=True)
    if "after_l0attn" in dbg:
        for j in range(16):
            pb.dma("sync", dbg["after_l0attn"][j * 128:(j + 1) * 128, :],
                   stream[:, j * 1024:(j + 1) * 1024], "d_dbg",
                   wait=[("lnap", None)])

    ws_col = bbt("ws_col", [128, 1])
    bias_row = bbt("bias_row", [1, 1024])
    bias_row_b = bbt("bias_row_b", [1, 1024], BF16)
    w8sb = bbt("w8sb", [128, 8 * 1024], BF16)
    wq_b = bbt("wq_b", [128, 8 * 1024], BF16)
    q8t = bbt("q8t", [128, 1024], FP8)
    sclC = bbt("sclC", [128, 16])
    rcol_b = bbt("rcol_b", [128, 1], BF16)
    rrow_b = bbt("rrow_b", [1, 128], BF16)

    pb.dma("sync", ws_col[:], ws_in.broadcast_to([128, 1]), "d_c2")
    pb.dma("sync", bias_row[:], bias_row_in, "d_c2")
    pb.op("gpsimd", "dma_start", inc="d_c2g", winc=16,
          out=w8sb[:].rearrange("p (c e) -> p c e", c=8),
          in_=w8T.rearrange("(c p) e -> p c e", p=128))
    pb.op("vector", "tensor_copy", wait=[("d_c2", None)], inc="c2v",
          out=bias_row_b[:], in_=bias_row[:])
    pb.op("vector", "tensor_scalar", wait=[("d_c2", None), ("d_c2g", None)],
          inc="c2v",
          out=wq_b[:], in0=w8sb[:], scalar1=ws_col[:, 0:1], scalar2=0.0,
          op0=ALU.mult, op1=ALU.add)
    for j in range(16):
        xt = stream[:, j * 1024:(j + 1) * 1024]
        pb.op("vector", "tensor_reduce",
              wait=[("lnap", None), ("d_dbg", None)], inc="q8v",
              out=stat[:, 9:10], in_=xt, axis=AXX, op=ALU.max,
              apply_absolute_value=True)
        pb.op("vector", "reciprocal", inc="q8v", out=stat[:, 10:11],
              in_=stat[:, 9:10], drain=True)
        pb.op("vector", "tensor_scalar", inc="q8v", out=sq32[:], in0=xt,
              scalar1=stat[:, 10:11], scalar2=224.0, op0=ALU.mult,
              op1=ALU.mult, drain=True)
        pb.op("vector", "tensor_copy", inc="q8v", out=q8t[:], in_=sq32[:],
              drain=True)
        pb.op("vector", "tensor_copy", inc="q8v", out=xnb16[:], in_=q8t[:],
              drain=True)
        pb.op("vector", "tensor_scalar", inc="q8v", out=sclC[:, j:j + 1],
              in0=stat[:, 9:10], scalar1=1.0 / 224.0, scalar2=0.0,
              op0=ALU.mult, op1=ALU.add, drain=True)
        for k in range(8):
            pk = psum[4 + (k % 2)]
            tp_wait = [("q8v", None)]
            if pb.val("lntpe") >= 2:
                tp_wait.append(("lntpe", pb.val("lntpe") - 1))
            pb.op("tensor", "transpose", pk[:, 0:128],
                  xnb16[:, k * 128:(k + 1) * 128], eye_sb[:],
                  wait=tp_wait, inc="lntpp")
            pb.op("vector", "tensor_copy", wait=[("lntpp", None)],
                  inc="lntpe",
                  out=xfmA[:, k * 2048 + j * 128: k * 2048 + (j + 1) * 128],
                  in_=pk[:, 0:128])

    for j in range(16):
        pb.op("vector", "reciprocal", wait=[("q8v", None)], inc="r1v",
              out=stat[:, 11:12], in_=sclC[:, j:j + 1], drain=True)
        pb.op("vector", "tensor_copy", inc="r1v", out=rcol_b[:],
              in_=stat[:, 11:12], drain=True)
        pb.op("tensor", "transpose", psum[5][0:1, 0:128], rcol_b[:],
              eye_sb[:], wait=[("r1v", None), ("lntpe", None)], inc="r1t")
        pb.op("vector", "tensor_copy", wait=[("r1t", None)], inc="r1e",
              out=rrow_b[:], in_=psum[5][0:1, 0:128])
        for eh in range(2):
            pe = psum[eh]
            for k in range(8):
                pb.op("tensor", "matmul", pe[:, 0:512],
                      xfmA[:, k * 2048 + j * 128: k * 2048 + (j + 1) * 128],
                      wq_b[:, k * 1024 + eh * 512: k * 1024 + eh * 512 + 512],
                      start=(k == 0), stop=False,
                      wait=([("q8ev", None)] if k == 0 else None),
                      inc="q8mm")
            pb.op("tensor", "matmul", pe[:, 0:512], rrow_b[:],
                  bias_row_b[:, eh * 512:eh * 512 + 512], start=False,
                  stop=True, wait=[("r1e", None)], inc="q8mm")
            pb.op("vector", "scalar_tensor_tensor", wait=[("q8mm", None)],
                  inc="q8ev", drain=True,
                  out=stream[:, j * 1024 + eh * 512: j * 1024 + eh * 512 + 512],
                  in0=pe[:, 0:512], scalar=sclC[:, j:j + 1],
                  in1=stream[:, j * 1024 + eh * 512: j * 1024 + eh * 512 + 512],
                  op0=ALU.mult, op1=ALU.add)

    stream_ln([("q8ev", None), ("d_dbg", None)], inplace=True)
    if "after_l0" in dbg:
        for j in range(16):
            pb.dma("sync", dbg["after_l0"][j * 128:(j + 1) * 128, :],
                   stream[:, j * 1024:(j + 1) * 1024], "d_dbg",
                   wait=[("lnap", None)])
    bstack.close()


    pb.autodrain = True
    dstack = contextlib.ExitStack()

    def dbt(name, shape, dt=F32):
        return dstack.enter_context(nc.sbuf_tensor(name, list(shape), dt))

    xfmB = dbt("xfmB", [128, 8 * 2048], BF16)
    Kt = dbt("Kt", [128, 2048])
    vt = dbt("vt", [128, 2048])
    kvcols = dbt("kvcols", [128, 48])
    sbc = dbt("sbc", [128, 2 * 1024])
    bql = dbt("bql", [128, 8 * 4])
    b1l = dbt("b1l", [128, 32])
    b2r_b = dbt("b2r_b", [1, 1024], BF16)
    ones_row = dbt("ones_row", [1, 128], BF16)
    h_fm2 = dbt("h_fm2", [128, 32 * 512], BF16)

    kv_part = nc.dram_tensor("kv_part", [128, 16], F32)
    kv_sum = nc.dram_tensor("kv_sum", [128, 16], F32)
    xl1_hbm = nc.dram_tensor("xl1_hbm", [TOWN, D], F32)

    pb.op("vector", "memset", ones_row[:], 1.0, inc="c2v")
    for j in range(16):
        pb.dma("sync", xl1_hbm.ap()[j * 128:(j + 1) * 128, :],
               stream[:, j * 1024:(j + 1) * 1024], "d_xl1",
               wait=[("lnap", None)])

    def stream_to_fm(ready, dst_fm):
        for j in range(16):
            pb.op("vector", "tensor_copy", wait=ready + [("lntpp", None)],
                  inc="lnap", out=xnb16[:], drain=True,
                  in_=stream[:, j * 1024:(j + 1) * 1024])
            for k in range(8):
                pk = psum[4 + (k % 2)]
                tp_wait = [("lnap", None)]
                if pb.val("lntpe") >= 2:
                    tp_wait.append(("lntpe", pb.val("lntpe") - 1))
                pb.op("tensor", "transpose", pk[:, 0:128],
                      xnb16[:, k * 128:(k + 1) * 128], eye_sb[:],
                      wait=tp_wait, inc="lntpp")
                pb.op("vector", "tensor_copy", wait=[("lntpp", None)],
                      inc="lntpe",
                      out=dst_fm[:, k * 2048 + j * 128:
                                 k * 2048 + (j + 1) * 128],
                      in_=pk[:, 0:128])

    for blk in range(4):
        pb.dma("sync", bql[:], l1_bqkvo[blk].rearrange("w p c -> p w c"),
               "d_l1c", wait=[("qkev", None), ("wos", None), ("fcev", None),
                              ("q8ev", None), ("lnap", None)])
        pb.dma("sync", b1l[:], l1_b1c[blk], "d_l1c")
        pb.dma("gpsimd", b2r_b[:], l1_b2r[blk], "d_l1cg",
               wait=[("pjmm", None), ("q8ev", None), ("lnap", None)])

        stream_to_fm([("lnap", None), ("d_xl1", None), ("d_dbg", None)], xfmA)

        for at in range(8):
            for w in range(3):
                par = (at * 3 + w) % 2
                g = []
                if pb.val("qkmm") >= 64:
                    g.append(("qkmm", pb.val("qkmm") - 32))
                if at == 0 and w == 0:
                    g.append(("fcmm", None))
                pb.op("gpsimd", "dma_start", inc="d_qk%d" % par, winc=16,
                      wait=g or None,
                      out=wstr[:, par * 2048:par * 2048 + 1024],
                      in_=l1_qkvo[blk, w, at])
                pb.wait("tensor", "d_qk%d" % par)
                pb.wait("tensor", "lntpe")
                for tch in range(4):
                    pe = psum[par]
                    for k in range(8):
                        mmw = None
                        if k == 0:
                            mmw = [("qkev", None), ("pjev", None)]
                        pb.op("tensor", "matmul", pe[:, 0:512],
                              wstr[:, par * 2048 + k * 128:
                                   par * 2048 + (k + 1) * 128],
                              xfmA[:, k * 2048 + tch * 512:
                                   k * 2048 + tch * 512 + 512],
                              start=(k == 0), stop=(k == 7),
                              wait=mmw, inc="qkmm")
                    bcol = bql[:, w * 8 + at:w * 8 + at + 1]
                    sl = slice(tch * 512, tch * 512 + 512)
                    if w < 2:
                        dst = Kt if w == 1 else vt
                        pb.op("vector", "tensor_scalar",
                              wait=[("qkmm", None), ("d_l1c", None)],
                              inc="qkv1", out=sq32[:, 0:512], in0=pe[:, 0:512],
                              scalar1=bcol, scalar2=0.0, op0=ALU.add,
                              op1=ALU.min)
                        pb.op("scalar", "activation", wait=[("qkv1", None)],
                              inc="qks1", out=sq32[:, 512:1024],
                              in_=sq32[:, 0:512], func=AF.Exp)
                        pb.op("vector", "tensor_scalar", inc="qkv1",
                              out=dst[:, sl], in0=pe[:, 0:512],
                              scalar1=bcol, scalar2=0.0, op0=ALU.add,
                              op1=ALU.max)
                        if w == 0:
                            pb.op("vector", "tensor_tensor",
                                  wait=[("qks1", None)], inc="qkev",
                                  out=h_fm2[:, at * 2048 + tch * 512:
                                            at * 2048 + tch * 512 + 512],
                                  in0=dst[:, sl], in1=sq32[:, 512:1024],
                                  op=ALU.add)
                        else:
                            pb.op("vector", "scalar_tensor_tensor",
                                  wait=[("qks1", None)], inc="qkev",
                                  out=Kt[:, sl], in0=Kt[:, sl], scalar=1.0,
                                  in1=sq32[:, 512:1024], op0=ALU.mult,
                                  op1=ALU.add)
                    else:
                        pb.op("vector", "tensor_scalar",
                              wait=[("qkmm", None), ("d_l1c", None)],
                              inc="qkev", out=vt[:, sl], in0=pe[:, 0:512],
                              scalar1=bcol, scalar2=0.0, op0=ALU.add,
                              op1=ALU.add)
            pb.op("vector", "tensor_reduce",
                  wait=[("qkev", None), ("d_kvp", None)], inc="kvv",
                  out=kvcols[:, 8 + at:8 + at + 1], in_=Kt[:], axis=AXX,
                  op=ALU.add)
            pb.op("vector", "scalar_tensor_tensor", inc="kvv",
                  out=Kt[:], in0=Kt[:], scalar=1.0, in1=vt[:],
                  op0=ALU.mult, op1=ALU.mult,
                  accum_out=kvcols[:, at:at + 1])

        pb.dma("sync", kv_part.ap()[:, 0:16], kvcols[:, 0:16], "d_kvp",
               wait=[("kvv", None)])
        pb.wait("gpsimd", "d_kvp")
        pb.op("gpsimd", "collective_compute", "AllReduce", ALU.add,
              replica_groups=PAIRS, ins=[kv_part.ap().opt()],
              outs=[kv_sum.ap().opt()], inc="cc_kv")
        pb.dma("gpsimd", kvcols[:, 16:32], kv_sum.ap()[:, 0:16], "d_kvs",
               wait=[("cc_kv", None)])

        for at in range(8):
            qsl = slice(at * 2048, (at + 1) * 2048)
            pb.op("vector", "tensor_scalar", wait=[("d_kvs", None)], inc="vv",
                  out=Kt[:], in0=h_fm2[:, qsl],
                  scalar1=kvcols[:, 24 + at:24 + at + 1], scalar2=1e-6,
                  op0=ALU.mult, op1=ALU.add)
            pb.op("vector", "reciprocal", inc="vv", out=Kt[:], in_=Kt[:])
            pb.op("vector", "tensor_scalar", inc="vv", out=vt[:],
                  in0=h_fm2[:, qsl],
                  scalar1=kvcols[:, 16 + at:16 + at + 1], scalar2=0.0,
                  op0=ALU.mult, op1=ALU.add)
            pb.op("vector", "tensor_tensor", inc="vv", out=xfmB[:, qsl],
                  in0=vt[:], in1=Kt[:], op=ALU.mult)

        for et in range(8):
            par = et % 2
            g = []
            if pb.val("womm") >= 64:
                g.append(("womm", pb.val("womm") - 32))
            pb.op("gpsimd", "dma_start", inc="d_wo%d" % par, winc=16,
                  wait=g or None, out=wstr[:, par * 2048:par * 2048 + 1024],
                  in_=l1_qkvo[blk, 3, et])
            pb.wait("tensor", "d_wo%d" % par)
            for tch in range(4):
                pe = psum[2 + par]
                for k in range(8):
                    mmw = [("vv", None), ("woev", None)] if k == 0 else None
                    pb.op("tensor", "matmul", pe[:, 0:512],
                          wstr[:, par * 2048 + k * 128:
                               par * 2048 + (k + 1) * 128],
                          xfmB[:, k * 2048 + tch * 512:
                               k * 2048 + tch * 512 + 512],
                          start=(k == 0), stop=(k == 7),
                          wait=mmw, inc="womm")
                bcol = bql[:, 24 + et:24 + et + 1]
                pb.op("scalar", "activation",
                      wait=[("womm", None), ("d_l1c", None)],
                      inc="wos", out=sq32[:, 0:512], in_=pe[:, 0:512],
                      func=AF.Gelu_apprx_tanh, bias=bcol, scale=1.0)
                pb.op("scalar", "activation", inc="woev",
                      out=h_fm2[:, et * 2048 + tch * 512:
                                et * 2048 + tch * 512 + 512],
                      in_=sq32[:, 0:512], func=AF.Gelu_apprx_tanh)

        for j in range(16):
            for k in range(8):
                pk = psum[4 + (k % 2)]
                tp_wait = [("woev", None)]
                if pb.val("lntpe") >= 2:
                    tp_wait.append(("lntpe", pb.val("lntpe") - 1))
                pb.op("tensor", "transpose", pk[:, 0:128],
                      h_fm2[:, k * 2048 + j * 128: k * 2048 + (j + 1) * 128],
                      eye_sb[:], wait=tp_wait, inc="lntpp")
                pb.op("vector", "tensor_tensor", wait=[("lntpp", None)],
                      inc="lntpe",
                      out=stream[:, j * 1024 + k * 128: j * 1024 + (k + 1) * 128],
                      in0=stream[:, j * 1024 + k * 128:
                                 j * 1024 + (k + 1) * 128],
                      in1=pk[:, 0:128], op=ALU.add)

        pb.dma("gpsimd", sbc[:, 0:1024],
               l1_lnsb[blk, 0].broadcast_to([128, 1024]), "d_sbc",
               wait=[("lnsb", None), ("q8ev", None), ("lnap", None)])
        pb.dma("gpsimd", sbc[:, 1024:2048],
               l1_lnsb[blk, 1].broadcast_to([128, 1024]), "d_sbc")
        stream_ln([("lntpe", None), ("d_dbg", None)], inplace=True)
        for j in range(16):
            pb.op("vector", "tensor_tensor", wait=[("d_sbc", None)],
                  inc="lnsb", out=stream[:, j * 1024:(j + 1) * 1024],
                  in0=stream[:, j * 1024:(j + 1) * 1024],
                  in1=sbc[:, 0:1024], op=ALU.mult)
            pb.op("vector", "tensor_tensor", inc="lnsb",
                  out=stream[:, j * 1024:(j + 1) * 1024],
                  in0=stream[:, j * 1024:(j + 1) * 1024],
                  in1=sbc[:, 1024:2048], op=ALU.add)

        stream_to_fm([("lnsb", None)], xfmA)
        for tch in range(4):
            ts0 = tch * 512
            for ft in range(32):
                par = ft % 2
                g = []
                if pb.val("fcmm") >= 16:
                    g.append(("fcmm", pb.val("fcmm") - 8))
                if ft < 2:
                    g.append(("pjev", None))
                    if tch == 0:
                        g.append(("womm", None))
                pb.op("gpsimd", "dma_start", inc="d_fc%d" % par, winc=16,
                      wait=g or None,
                      out=wstr[:, par * 2048:par * 2048 + 1024],
                      in_=l1_w1[blk, ft])
                pb.wait("tensor", "d_fc%d" % par)
                pb.wait("tensor", "lntpe")
                for k in range(8):
                    mmw = None
                    if k == 0:
                        mmw = [("fcev", None), ("pjmm", None), ("pjev", None)]
                    pb.op("tensor", "matmul", psum[par][:, 0:512],
                          wstr[:, par * 2048 + k * 128:
                               par * 2048 + (k + 1) * 128],
                          xfmA[:, k * 2048 + ts0: k * 2048 + ts0 + 512],
                          start=(k == 0), stop=(k == 7), wait=mmw, inc="fcmm")
                pb.op("scalar", "activation",
                      wait=[("fcmm", None), ("pjmm", None), ("d_l1c", None)],
                      inc="fcev",
                      out=h_fm2[:, ft * 512:(ft + 1) * 512],
                      in_=psum[par][:, 0:512], func=AF.Gelu_apprx_tanh,
                      bias=b1l[:, ft:ft + 1], scale=1.0)
            for wave in range(2):
                for k in range(32):
                    par = k % 2
                    g = []
                    if pb.val("pjmm") >= 8:
                        g.append(("pjmm", pb.val("pjmm") - 4))
                    pb.op("gpsimd", "dma_start", inc="d_pj%d" % par, winc=16,
                          wait=g or None,
                          out=wstr[:, par * 2048 + 1024:par * 2048 + 2048],
                          in_=l1_w2[blk, k * 128:(k + 1) * 128, :])
                    pb.wait("tensor", "d_pj%d" % par)
                    for tt in range(2):
                        for eh in range(2):
                            mmw = None
                            if k == 0:
                                mmw = [("fcev", None), ("pjev", None),
                                       ("c2v", None), ("d_l1cg", None)]
                            pb.op("tensor", "matmul",
                                  psum[tt * 2 + eh][:, 0:512],
                                  h_fm2[:, k * 512 + (wave * 2 + tt) * 128:
                                        k * 512 + (wave * 2 + tt + 1) * 128],
                                  wstr[:, par * 2048 + 1024 + eh * 512:
                                       par * 2048 + 1024 + eh * 512 + 512],
                                  start=(k == 0), stop=False, wait=mmw,
                                  inc="pjmm")
                for tt in range(2):
                    for eh in range(2):
                        pb.op("tensor", "matmul", psum[tt * 2 + eh][:, 0:512],
                              ones_row[:],
                              b2r_b[:, eh * 512:eh * 512 + 512],
                              start=False, stop=True, inc="pjmm")
                for tt in range(2):
                    jj = 4 * tch + wave * 2 + tt
                    for eh in range(2):
                        slot = (tt * 2 + eh) % 2
                        gw = [("pjmm", None)]
                        if pb.val("pjev") >= 2:
                            gw.append(("pjev", pb.val("pjev") - 1))
                        pb.op("scalar", "activation", wait=gw, inc="pjs",
                              out=sq32[:, slot * 512:slot * 512 + 512],
                              in_=psum[tt * 2 + eh][:, 0:512],
                              func=AF.Gelu_apprx_tanh)
                        pb.op("vector", "tensor_tensor", wait=[("pjs", None)],
                              inc="pjev",
                              out=stream[:, jj * 1024 + eh * 512:
                                         jj * 1024 + eh * 512 + 512],
                              in0=stream[:, jj * 1024 + eh * 512:
                                         jj * 1024 + eh * 512 + 512],
                              in1=sq32[:, slot * 512:slot * 512 + 512],
                              op=ALU.add)

        pb.dma("gpsimd", sbc[:, 0:1024],
               l1_lnsb[blk, 2].broadcast_to([128, 1024]), "d_sbc",
               wait=[("lnsb", None)])
        pb.dma("gpsimd", sbc[:, 1024:2048],
               l1_lnsb[blk, 3].broadcast_to([128, 1024]), "d_sbc")
        stream_ln([("pjev", None), ("d_dbg", None)], inplace=True)
        for j in range(16):
            pb.op("vector", "tensor_tensor", wait=[("d_sbc", None)],
                  inc="lnsb", out=stream[:, j * 1024:(j + 1) * 1024],
                  in0=stream[:, j * 1024:(j + 1) * 1024],
                  in1=sbc[:, 0:1024], op=ALU.mult)
            pb.op("vector", "tensor_tensor", inc="lnsb",
                  out=stream[:, j * 1024:(j + 1) * 1024],
                  in0=stream[:, j * 1024:(j + 1) * 1024],
                  in1=sbc[:, 1024:2048], op=ALU.add)

    for j in range(16):
        pb.dma("gpsimd", sq32[:], xl1_hbm.ap()[j * 128:(j + 1) * 128, :],
               "d_so3", wait=[("res3", None), ("lnsb", None)])
        pb.op("vector", "tensor_tensor", wait=[("lnsb", None), ("d_so3", None)],
              inc="res3", out=stream[:, j * 1024:(j + 1) * 1024],
              in0=stream[:, j * 1024:(j + 1) * 1024], in1=sq32[:], op=ALU.add)
    stream_ln([("res3", None), ("d_dbg", None)], inplace=True)
    if "after_l1attn" in dbg:
        for j in range(16):
            pb.dma("sync", dbg["after_l1attn"][j * 128:(j + 1) * 128, :],
                   stream[:, j * 1024:(j + 1) * 1024], "d_dbg",
                   wait=[("lnap", None)])
    dstack.close()


    estack = contextlib.ExitStack()

    def ebt(name, shape, dt=F32):
        return estack.enter_context(nc.sbuf_tensor(name, list(shape), dt))

    wbig = ebt("wbig", [128, 24 * 1024], BF16)
    wt32 = ebt("wt32", [128, 3072])
    wt32b = ebt("wt32b", [128, 3072])
    g32 = ebt("g32", [128, 3072])
    rmax = ebt("rmax", [128, 96])
    rinv = ebt("rinv", [128, 96])
    r7 = ebt("r7", [128, 96])
    s7 = ebt("s7", [128, 96])
    wqb = ebt("wqb", [128, 3072], BF16)
    eye32_sb = ebt("eye32_sb", [128, 128])
    gcol = ebt("gcol", [128, 16])

    wupdq_hbm = nc.dram_tensor("wupdq_hbm", [1024, 3072], BF16)
    wgdq_hbm = nc.dram_tensor("wgdq_hbm", [1024, 3072], BF16)
    wddq_hbm = nc.dram_tensor("wddq_hbm", [3072, 1024], BF16)
    up_hbm = nc.dram_tensor("up_hbm", [TOWN, 3072], F32)
    gate_hbm = nc.dram_tensor("gate_hbm", [TOWN, 3072], F32)
    g2_hbm = nc.dram_tensor("g2_hbm", [TOWN, 3072], F32)
    upq_hbm = nc.dram_tensor("upq_hbm", [TOWN, 3072], F32)
    qs_part = nc.dram_tensor("qs_part", [1, 4], F32)
    qs_sum = nc.dram_tensor("qs_sum", [1, 4], F32)

    o_strip = xfmA
    pb.dma("sync", eye32_sb[:], eye128f, "d_ey", wait=[("lnap", None)])
    pb.op("vector", "memset", gcol[:, 0:8], 0.0, wait=[("lnap", None)],
          inc="einit")
    pb.dma("sync", qs_part.ap()[0:1, 0:4], gcol[0:1, 0:4], "d_qsp",
           wait=[("einit", None)])

    def dequant_weight(src_ap, ntiles, nfree, dst_hbm, tag):
        nblk = nfree // 32
        for pt in range(ntiles):
            pb.dma("gpsimd", wt32[:, 0:nfree],
                   src_ap[pt * 128:(pt + 1) * 128, :], "d_wt",
                   wait=[("dqv", None), ("lnap", None)])
            pb.op("vector", "tensor_reduce", wait=[("d_wt", None)], inc="dqv",
                  out=rmax[:, 0:nblk],
                  in_=wt32[:, 0:nfree].rearrange("p (b t) -> p b t", t=32),
                  axis=AXX, op=ALU.max, apply_absolute_value=True)
            pb.op("vector", "reciprocal", inc="dqv", out=rinv[:, 0:nblk],
                  in_=rmax[:, 0:nblk])
            pb.op("vector", "tensor_scalar", inc="dqv", out=r7[:, 0:nblk],
                  in0=rinv[:, 0:nblk], scalar1=7.0, scalar2=0.0,
                  op0=ALU.mult, op1=ALU.add)
            pb.op("vector", "tensor_scalar", inc="dqv", out=s7[:, 0:nblk],
                  in0=rmax[:, 0:nblk], scalar1=1.0 / 7.0, scalar2=0.0,
                  op0=ALU.mult, op1=ALU.add)
            pb.op("vector", "tensor_tensor", inc="dqv", out=wt32b[:, 0:nfree]
                  .rearrange("p (b t) -> p b t", t=32),
                  in0=wt32[:, 0:nfree].rearrange("p (b t) -> p b t", t=32),
                  in1=r7[:, 0:nblk].rearrange("p (b t) -> p b t", t=1)
                  .broadcast_to([128, nblk, 32]), op=ALU.mult)
            pb.op("vector", "tensor_scalar", inc="dqv", out=wt32b[:, 0:nfree],
                  in0=wt32b[:, 0:nfree], scalar1=MAGIC, scalar2=MAGIC,
                  op0=ALU.add, op1=ALU.subtract)
            pb.op("vector", "tensor_tensor", inc="dqv", out=wqb[:, 0:nfree]
                  .rearrange("p (b t) -> p b t", t=32),
                  in0=wt32b[:, 0:nfree].rearrange("p (b t) -> p b t", t=32),
                  in1=s7[:, 0:nblk].rearrange("p (b t) -> p b t", t=1)
                  .broadcast_to([128, nblk, 32]), op=ALU.mult)
            for fc in range(nfree // 128):
                pk = psum[4 + (fc % 2)]
                tp_wait = [("dqv", None)]
                if pb.val("dqtp") >= 2:
                    tp_wait.append(("dqe", pb.val("dqe") - 1))
                pb.op("tensor", "transpose", pk[:, 0:128],
                      wqb[:, fc * 128:(fc + 1) * 128], eye_sb[:],
                      wait=tp_wait, inc="dqtp")
                pb.op("vector", "tensor_copy",
                      wait=[("dqtp", None), ("d_dq", None)],
                      inc="dqe", out=xnb16[:, 0:128], in_=pk[:, 0:128])
                pb.dma("sync", dst_hbm.ap()[fc * 128:(fc + 1) * 128,
                                            pt * 128:(pt + 1) * 128],
                       xnb16[:, 0:128], "d_dq", wait=[("dqe", None)])

    dequant_weight(wup_in, 24, 1024, wupdq_hbm, "up")
    dequant_weight(wg_in, 24, 1024, wgdq_hbm, "g")
    dequant_weight(wd_in, 8, 3072, wddq_hbm, "d")

    for j in range(16):
        pb.op("vector", "tensor_reduce",
              wait=[("lnap", None), ("d_dbg", None), ("d_qsp", None)],
              inc="gxv", out=gcol[:, 0:1],
              in_=stream[:, j * 1024:(j + 1) * 1024], axis=AXX, op=ALU.max,
              apply_absolute_value=True)
        if j == 0:
            pb.op("vector", "tensor_copy", inc="gxv", out=gcol[:, 1:2],
                  in_=gcol[:, 0:1])
        else:
            pb.op("vector", "tensor_tensor", inc="gxv", out=gcol[:, 1:2],
                  in0=gcol[:, 1:2], in1=gcol[:, 0:1], op=ALU.max)

    def part_max_to_slot(src_col, slot, tag):
        pb.op("tensor", "transpose", psum[2][0:1, 0:128], src_col, eye32_sb[:],
              wait=[(tag, None), ("pmx", None), ("d_ey", None)], inc="pmt")
        pb.op("vector", "tensor_copy",
              wait=[("pmt", None), ("d_ug", None), ("d_p1o", None),
                    ("d_p2o", None), ("d_wt", None)],
              inc="pmx",
              out=wt32[0:1, 0:128], in_=psum[2][0:1, 0:128])
        pb.op("vector", "tensor_reduce", inc="pmx", out=wt32[0:1, 128:129],
              in_=wt32[0:1, 0:128], axis=AXX, op=ALU.max)
        pb.dma("sync", qs_part.ap()[0:1, slot:slot + 1], wt32[0:1, 128:129],
               "d_qsp", wait=[("pmx", None), ("cc_q", None)])

    def ar_max():
        pb.wait("gpsimd", "d_qsp")
        pb.wait("gpsimd", "d_qsb")
        pb.op("gpsimd", "collective_compute", "AllReduce", ALU.max,
              replica_groups=ALLG, ins=[qs_part.ap().opt()],
              outs=[qs_sum.ap().opt()], inc="cc_q")

    part_max_to_slot(gcol[:, 1:2], 0, "gxv")
    ar_max()
    pb.dma("gpsimd", gcol[:, 2:3], qs_sum.ap()[0:1, 0:1].broadcast_to([128, 1]),
           "d_qsb", wait=[("cc_q", None)])
    pb.op("vector", "reciprocal", wait=[("d_qsb", None)], inc="qsv",
          out=gcol[:, 3:4], in_=gcol[:, 2:3])
    pb.op("vector", "tensor_scalar", inc="qsv", out=gcol[:, 3:4],
          in0=gcol[:, 3:4], scalar1=32767.0, scalar2=0.0, op0=ALU.mult,
          op1=ALU.add)
    pb.op("vector", "tensor_scalar", inc="qsv", out=gcol[:, 4:5],
          in0=gcol[:, 2:3], scalar1=1.0 / 32767.0, scalar2=0.0, op0=ALU.mult,
          op1=ALU.add)

    for j in range(16):
        pb.op("vector", "tensor_scalar", wait=[("qsv", None)], inc="xqv",
              out=sq32[:], in0=stream[:, j * 1024:(j + 1) * 1024],
              scalar1=gcol[:, 3:4], scalar2=MAGIC, op0=ALU.mult, op1=ALU.add)
        pb.op("vector", "tensor_scalar", inc="xqv", out=sq32[:],
              in0=sq32[:], scalar1=MAGIC, scalar2=0.0, op0=ALU.subtract,
              op1=ALU.add)
        pb.op("vector", "tensor_copy", inc="xqv", out=xnb16[:], in_=sq32[:],
              wait=[("d_dq", None)])
        for k in range(8):
            pk = psum[4 + (k % 2)]
            tp_wait = [("xqv", None)]
            if pb.val("lntpe") >= 2:
                tp_wait.append(("lntpe", pb.val("lntpe") - 1))
            pb.op("tensor", "transpose", pk[:, 0:128],
                  xnb16[:, k * 128:(k + 1) * 128], eye_sb[:],
                  wait=tp_wait, inc="lntpp")
            pb.op("vector", "tensor_copy", wait=[("lntpp", None)],
                  inc="lntpe",
                  out=xfmA[:, k * 2048 + j * 128: k * 2048 + (j + 1) * 128],
                  in_=pk[:, 0:128])

    def upgate(dq_hbm, out_hbm, maxslot, tag):
        pb.op("gpsimd", "dma_start", inc="d_wb", winc=16,
              out=wbig[:].rearrange("p (c e) -> p c e", c=8),
              in_=dq_hbm.ap().rearrange("(c p) e -> p c e", p=128),
              wait=[("upmm", None), ("gamm", None), ("lnap", None)])
        pb.wait("tensor", "d_wb")
        pb.op("vector", "memset", gcol[:, 5:6], 0.0, inc=tag + "m")
        for j in range(16):
            for ih in range(6):
                pe = psum[ih % 2]
                for k in range(8):
                    mmw = None
                    if k == 0:
                        mmw = [("lntpe", None), (tag + "e", None)]
                    pb.op("tensor", "matmul", pe[:, 0:512],
                          xfmA[:, k * 2048 + j * 128: k * 2048 + (j + 1) * 128],
                          wbig[:, k * 3072 + ih * 512: k * 3072 + ih * 512 + 512],
                          start=(k == 0), stop=(k == 7), wait=mmw,
                          inc=tag + "mm")
                pb.op("vector", "tensor_scalar",
                      wait=[(tag + "mm", None), ("d_ug", None),
                            ("d_qsp", None)],
                      inc=tag + "e", out=wt32[:, 0:512], in0=pe[:, 0:512],
                      scalar1=gcol[:, 4:5], scalar2=0.0, op0=ALU.mult,
                      op1=ALU.add)
                pb.op("vector", "tensor_reduce", inc=tag + "e",
                      out=gcol[:, 6:7], in_=wt32[:, 0:512], axis=AXX,
                      op=ALU.max, apply_absolute_value=True)
                pb.op("vector", "tensor_tensor", inc=tag + "m",
                      out=gcol[:, 5:6], in0=gcol[:, 5:6], in1=gcol[:, 6:7],
                      op=ALU.max)
                pb.dma("sync", out_hbm.ap()[j * 128:(j + 1) * 128,
                                            ih * 512:ih * 512 + 512],
                       wt32[:, 0:512], "d_ug", wait=[(tag + "e", None)])
        part_max_to_slot(gcol[:, 5:6], maxslot, tag + "m")

    upgate(wupdq_hbm, up_hbm, 1, "up")
    upgate(wgdq_hbm, gate_hbm, 2, "ga")
    ar_max()
    for slot, rc, sc in ((1, 8, 9), (2, 10, 11)):
        pb.dma("gpsimd", gcol[:, 7:8],
               qs_sum.ap()[0:1, slot:slot + 1].broadcast_to([128, 1]), "d_qsb",
               wait=[("cc_q", None), ("qsv", None)])
        pb.op("vector", "reciprocal", wait=[("d_qsb", None)], inc="qsv",
              out=gcol[:, rc:rc + 1], in_=gcol[:, 7:8])
        pb.op("vector", "tensor_scalar", inc="qsv", out=gcol[:, rc:rc + 1],
              in0=gcol[:, rc:rc + 1], scalar1=32767.0, scalar2=0.0,
              op0=ALU.mult, op1=ALU.add)
        pb.op("vector", "tensor_scalar", inc="qsv", out=gcol[:, sc:sc + 1],
              in0=gcol[:, 7:8], scalar1=1.0 / 32767.0, scalar2=0.0,
              op0=ALU.mult, op1=ALU.add)

    pb.op("vector", "memset", gcol[:, 6:7], 0.0, wait=[("qsv", None)],
          inc="p1m")
    for j in range(16):
        pb.dma("gpsimd", wt32[:], up_hbm.ap()[j * 128:(j + 1) * 128, :],
               "d_p1", wait=[("p1v", None), ("d_p1o", None), ("d_ug", None),
                             ("d_qsp", None)])
        pb.dma("gpsimd", g32[:], gate_hbm.ap()[j * 128:(j + 1) * 128, :],
               "d_p1")
        pb.op("vector", "tensor_scalar", wait=[("d_p1", None)], inc="p1v",
              out=wt32[:], in0=wt32[:], scalar1=gcol[:, 8:9], scalar2=MAGIC,
              op0=ALU.mult, op1=ALU.add)
        pb.op("vector", "tensor_scalar", inc="p1v", out=wt32[:], in0=wt32[:],
              scalar1=MAGIC, scalar2=gcol[:, 9:10], op0=ALU.subtract,
              op1=ALU.mult)
        pb.op("vector", "tensor_scalar", inc="p1v", out=g32[:], in0=g32[:],
              scalar1=gcol[:, 10:11], scalar2=MAGIC, op0=ALU.mult,
              op1=ALU.add)
        pb.op("vector", "tensor_scalar", inc="p1v", out=g32[:], in0=g32[:],
              scalar1=MAGIC, scalar2=gcol[:, 11:12], op0=ALU.subtract,
              op1=ALU.mult)
        pb.op("scalar", "activation", wait=[("p1v", None)], inc="p1s",
              out=wt32b[:], in_=g32[:], func=AF.Sigmoid)
        pb.op("vector", "tensor_scalar", wait=[("p1s", None)], inc="p1v",
              out=wt32b[:], in0=wt32b[:], scalar1=65536.0, scalar2=MAGIC,
              op0=ALU.mult, op1=ALU.add)
        pb.op("vector", "tensor_scalar", inc="p1v", out=wt32b[:],
              in0=wt32b[:], scalar1=MAGIC, scalar2=65535.0,
              op0=ALU.subtract, op1=ALU.min)
        pb.op("vector", "tensor_scalar", inc="p1v", out=wt32b[:],
              in0=wt32b[:], scalar1=1.0 / 65536.0, scalar2=0.0,
              op0=ALU.mult, op1=ALU.add)
        pb.op("vector", "tensor_tensor", inc="p1v", out=g32[:], in0=g32[:],
              in1=wt32b[:], op=ALU.mult)
        pb.op("vector", "tensor_reduce", inc="p1v", out=gcol[:, 7:8],
              in_=g32[:], axis=AXX, op=ALU.max, apply_absolute_value=True)
        pb.op("vector", "tensor_tensor", inc="p1m", out=gcol[:, 6:7],
              in0=gcol[:, 6:7], in1=gcol[:, 7:8], op=ALU.max)
        pb.dma("sync", g2_hbm.ap()[j * 128:(j + 1) * 128, :], g32[:], "d_p1o",
               wait=[("p1v", None)])
        pb.dma("sync", upq_hbm.ap()[j * 128:(j + 1) * 128, :], wt32[:],
               "d_p1o")
    part_max_to_slot(gcol[:, 6:7], 3, "p1m")
    ar_max()
    pb.dma("gpsimd", gcol[:, 7:8],
           qs_sum.ap()[0:1, 3:4].broadcast_to([128, 1]), "d_qsb",
           wait=[("cc_q", None), ("qsv", None)])
    pb.op("vector", "reciprocal", wait=[("d_qsb", None)], inc="qsv",
          out=gcol[:, 12:13], in_=gcol[:, 7:8])
    pb.op("vector", "tensor_scalar", inc="qsv", out=gcol[:, 12:13],
          in0=gcol[:, 12:13], scalar1=32767.0, scalar2=0.0, op0=ALU.mult,
          op1=ALU.add)
    pb.op("vector", "tensor_scalar", inc="qsv", out=gcol[:, 13:14],
          in0=gcol[:, 7:8], scalar1=1.0 / 32767.0, scalar2=0.0,
          op0=ALU.mult, op1=ALU.add)

    pb.op("vector", "memset", gcol[:, 5:6], 0.0, wait=[("qsv", None)],
          inc="p2m")
    for j in range(16):
        pb.dma("gpsimd", g32[:], g2_hbm.ap()[j * 128:(j + 1) * 128, :],
               "d_p2", wait=[("p2v", None), ("d_p2o", None), ("d_p1o", None),
                             ("d_qsp", None)])
        pb.dma("gpsimd", wt32[:], upq_hbm.ap()[j * 128:(j + 1) * 128, :],
               "d_p2")
        pb.op("vector", "tensor_scalar", wait=[("d_p2", None), ("d_ug", None)],
              inc="p2v", out=g32[:], in0=g32[:], scalar1=gcol[:, 12:13],
              scalar2=MAGIC, op0=ALU.mult, op1=ALU.add)
        pb.op("vector", "tensor_scalar", inc="p2v", out=g32[:], in0=g32[:],
              scalar1=MAGIC, scalar2=gcol[:, 13:14], op0=ALU.subtract,
              op1=ALU.mult)
        pb.op("vector", "tensor_tensor", inc="p2v", out=g32[:], in0=g32[:],
              in1=wt32[:], op=ALU.mult)
        pb.op("vector", "tensor_reduce", inc="p2v", out=gcol[:, 7:8],
              in_=g32[:], axis=AXX, op=ALU.max, apply_absolute_value=True)
        pb.op("vector", "tensor_tensor", inc="p2m", out=gcol[:, 5:6],
              in0=gcol[:, 5:6], in1=gcol[:, 7:8], op=ALU.max)
        pb.dma("sync", up_hbm.ap()[j * 128:(j + 1) * 128, :], g32[:], "d_p2o",
               wait=[("p2v", None)])
    part_max_to_slot(gcol[:, 5:6], 0, "p2m")
    ar_max()
    pb.dma("gpsimd", gcol[:, 7:8],
           qs_sum.ap()[0:1, 0:1].broadcast_to([128, 1]), "d_qsb",
           wait=[("cc_q", None), ("qsv", None)])
    pb.op("vector", "reciprocal", wait=[("d_qsb", None)], inc="qsv",
          out=gcol[:, 14:15], in_=gcol[:, 7:8])
    pb.op("vector", "tensor_scalar", inc="qsv", out=gcol[:, 14:15],
          in0=gcol[:, 14:15], scalar1=32767.0, scalar2=0.0, op0=ALU.mult,
          op1=ALU.add)
    pb.op("vector", "tensor_scalar", inc="qsv", out=gcol[:, 15:16],
          in0=gcol[:, 7:8], scalar1=1.0 / 32767.0, scalar2=0.0,
          op0=ALU.mult, op1=ALU.add)

    pb.op("gpsimd", "dma_start", inc="d_wd", winc=16,
          out=wbig[:].rearrange("p (c e) -> p c e", c=24),
          in_=wddq_hbm.ap().rearrange("(c p) e -> p c e", p=128),
          wait=[("upmm", None), ("gamm", None)])
    pb.wait("tensor", "d_wd")
    for j in range(16):
        pb.dma("gpsimd", g32[:], up_hbm.ap()[j * 128:(j + 1) * 128, :],
               "d_p3", wait=[("p3v", None), ("d_p2o", None), ("d_qsp", None)])
        pb.op("vector", "tensor_scalar", wait=[("d_p3", None), ("p2v", None)],
              inc="p3v", out=g32[:], in0=g32[:], scalar1=gcol[:, 14:15],
              scalar2=MAGIC, op0=ALU.mult, op1=ALU.add)
        pb.op("vector", "tensor_scalar", inc="p3v", out=g32[:], in0=g32[:],
              scalar1=MAGIC, scalar2=gcol[:, 15:16], op0=ALU.subtract,
              op1=ALU.mult)
        pb.op("vector", "tensor_copy", inc="p3v", out=o_strip[:, 0:3072]
              if False else wqb[:, 0:3072], in_=g32[:])
        for fc in range(24):
            pk = psum[4 + (fc % 2)]
            tp_wait = [("p3v", None)]
            if pb.val("p3tp") >= 2:
                tp_wait.append(("p3e", pb.val("p3e") - 1))
            pb.op("tensor", "transpose", pk[:, 0:128],
                  wqb[:, fc * 128:(fc + 1) * 128], eye_sb[:],
                  wait=tp_wait, inc="p3tp")
            pb.op("vector", "tensor_copy",
                  wait=[("p3tp", None), ("upmm", None), ("gamm", None)],
                  inc="p3e",
                  out=o_strip[:, fc * 128:(fc + 1) * 128], in_=pk[:, 0:128])
        for eh in range(2):
            pe = psum[eh]
            for k in range(24):
                mmw = None
                if k == 0:
                    mmw = [("p3e", None), ("dnev", None)]
                pb.op("tensor", "matmul", pe[:, 0:512],
                      o_strip[:, k * 128:(k + 1) * 128],
                      wbig[:, k * 1024 + eh * 512: k * 1024 + eh * 512 + 512],
                      start=(k == 0), stop=(k == 23), wait=mmw, inc="dnmm")
            pb.op("vector", "tensor_tensor", wait=[("dnmm", None)], inc="dnev",
                  out=stream[:, j * 1024 + eh * 512: j * 1024 + eh * 512 + 512],
                  in0=stream[:, j * 1024 + eh * 512: j * 1024 + eh * 512 + 512],
                  in1=pe[:, 0:512], op=ALU.add)

    stream_ln([("dnev", None), ("d_dbg", None)], inplace=True)
    estack.close()

    for j in range(16):
        pb.dma("gpsimd", out[j * 128:(j + 1) * 128, :],
               stream[:, j * 1024:(j + 1) * 1024], "d_out",
               wait=[("lnap", None)])
    pb.wait("gpsimd", "d_out")
    if "after_mixer" in dbg:
        pb.wait("gpsimd", "d_dbg")

    pb.replay(stack.enter_context(nc.Block()))
    stack.close()
    return nc



def _prep_core_inputs(c, I):
    h, b = c % 2, c // 2
    f32 = np.float32
    perm = np.concatenate([np.arange(h * DH, (h + 1) * DH),
                           np.arange((1 - h) * DH, (2 - h) * DH)])
    own = perm[:DH]
    ipw = np.asarray(I["L0_attn_in_proj_w"], f32)
    xr_w = ipw[:DI][perm]
    z_w = ipw[DI:][own]
    winT_full = np.concatenate([xr_w, z_w], 0).T
    winT = np.ascontiguousarray(
        winT_full.reshape(8, 128, 24, 128).transpose(2, 1, 0, 3)
        .reshape(24, 128, 1024))
    conv_w = np.asarray(I["L0_attn_conv_w"], f32)[perm, 0, :]
    conv_w_cols = np.ascontiguousarray(
        conv_w.reshape(16, 128, 4).transpose(1, 0, 2).reshape(128, 64))
    conv_b_cols = np.ascontiguousarray(
        np.asarray(I["L0_attn_conv_b"], f32)[perm].reshape(16, 128).T)
    ln1w_col = np.ascontiguousarray(
        np.asarray(I["L0_attn_ln1_w"], f32).reshape(8, 128).T)
    xpw = np.asarray(I["L0_attn_x_proj_w"], f32)[:, perm]
    xpw_pad = np.zeros((112, DI), f32)
    xpw_pad[0:80] = xpw[0:80]
    xpw_pad[96:112] = xpw[80:96]
    xprojT = np.ascontiguousarray(xpw_pad.T)
    dtwT = np.ascontiguousarray(np.asarray(I["L0_attn_dt_proj_w"], f32)[own].T)
    dtb_col = np.ascontiguousarray(
        np.asarray(I["L0_attn_dt_proj_b"], f32)[own].reshape(8, 128).T)
    alog_scan = np.ascontiguousarray(
        np.asarray(I["L0_attn_A_log"], f32)[own].reshape(128, 128).T)
    dp_col = np.ascontiguousarray(
        np.asarray(I["L0_attn_Dp"], f32)[own].reshape(8, 128).T)
    woutT = np.ascontiguousarray(
        np.asarray(I["L0_attn_out_proj_w"], f32)[:, own].T)

    bf = ml_dtypes.bfloat16
    sel16 = np.zeros((16, 128), bf)
    for k in range(16):
        sel16[k, np.arange(k, 128, 16)] = 1
    sely = np.zeros((128, 16 * 128), bf)
    for s in range(16):
        for p in range(128):
            sely[p, s * 128 + s * 8 + p // 16] = 1
    eye128 = np.eye(128, dtype=bf)

    fcT = np.asarray(I["L0_attn_fc_w"], f32).T
    fcT_tiled = np.ascontiguousarray(
        fcT.reshape(8, 128, 32, 128).transpose(2, 1, 0, 3).reshape(32, 128, 1024))
    projT = np.ascontiguousarray(np.asarray(I["L0_attn_proj_w"], f32).T)
    w8T = np.ascontiguousarray(
        np.asarray(I["L0_ffn_weight"]).T.astype(ml_dtypes.bfloat16))
    ws_in = np.asarray(I["L0_ffn_weight_scale"], f32).reshape(1, 1)
    bias_row_in = np.ascontiguousarray(
        np.asarray(I["L0_ffn_bias"], f32).reshape(1, 1024))
    ln2w_col_in = np.ascontiguousarray(
        np.asarray(I["L0_attn_ln2_w"], f32).reshape(8, 128).T)

    def tile_w(w):
        return w.reshape(8, 128, 8, 128).transpose(2, 1, 0, 3).reshape(8, 128, 1024)

    l1_qkvo = np.zeros((4, 4, 8, 128, 1024), f32)
    for i in range(4):
        l1_qkvo[i, 0] = tile_w(np.asarray(I["L1_attn_Wq"], f32)[i])
        l1_qkvo[i, 1] = tile_w(np.asarray(I["L1_attn_Wk"], f32)[i])
        l1_qkvo[i, 2] = tile_w(np.asarray(I["L1_attn_Wv"], f32)[i])
        l1_qkvo[i, 3] = tile_w(np.asarray(I["L1_attn_Wo"], f32)[i])
    l1_w1 = np.zeros((4, 32, 128, 1024), f32)
    for i in range(4):
        w1 = np.asarray(I["L1_attn_W1"], f32)[i]
        l1_w1[i] = w1.reshape(8, 128, 32, 128).transpose(2, 1, 0, 3) \
            .reshape(32, 128, 1024)
    l1_w2 = np.ascontiguousarray(np.asarray(I["L1_attn_W2"], f32))
    l1_bqkvo = np.zeros((4, 4, 128, 8), f32)
    for i in range(4):
        for wi, nm in enumerate(("bq", "bk", "bv", "bo")):
            l1_bqkvo[i, wi] = np.asarray(I["L1_attn_" + nm], f32)[i] \
                .reshape(8, 128).T
    l1_b1c = np.zeros((4, 128, 32), f32)
    for i in range(4):
        l1_b1c[i] = np.asarray(I["L1_attn_b1"], f32)[i].reshape(32, 128).T
    l1_b2r = np.asarray(I["L1_attn_b2"], f32).reshape(4, 1, 1024)
    l1_lnsb = np.stack([
        np.stack([np.asarray(I["L1_attn_ln1_s"], f32)[i].reshape(1, 1024),
                  np.asarray(I["L1_attn_ln1_b"], f32)[i].reshape(1, 1024),
                  np.asarray(I["L1_attn_ln2_s"], f32)[i].reshape(1, 1024),
                  np.asarray(I["L1_attn_ln2_b"], f32)[i].reshape(1, 1024)])
        for i in range(4)])

    x = np.asarray(I["L0_attn_x"], f32)
    return {
        "eye128f": np.eye(128, dtype=f32),
        "wup_in": np.ascontiguousarray(np.asarray(I["L1_ffn_w_gate"], f32)) if False
        else np.ascontiguousarray(np.asarray(I["L1_ffn_w_up"], f32)),
        "wg_in": np.ascontiguousarray(np.asarray(I["L1_ffn_w_gate"], f32)),
        "wd_in": np.ascontiguousarray(np.asarray(I["L1_ffn_w_down"], f32)),
        "l1_qkvo": l1_qkvo, "l1_w1": l1_w1, "l1_w2": l1_w2,
        "l1_bqkvo": l1_bqkvo, "l1_b1c": l1_b1c, "l1_b2r": l1_b2r,
        "l1_lnsb": np.ascontiguousarray(l1_lnsb),
        "fcT_tiled": fcT_tiled, "projT": projT, "w8T": w8T, "ws_in": ws_in,
        "bias_row_in": bias_row_in, "ln2w_col_in": ln2w_col_in,
        "x_batch": np.ascontiguousarray(x[b]),
        "x_own": np.ascontiguousarray(x[b, h * TOWN:(h + 1) * TOWN]),
        "winT": winT, "conv_w_cols": conv_w_cols, "conv_b_cols": conv_b_cols,
        "ln1w_col": ln1w_col, "xprojT": xprojT, "dtwT": dtwT,
        "dtb_col": dtb_col, "alog_scan": alog_scan, "dp_col": dp_col,
        "woutT": woutT, "sel16": sel16, "sely": sely, "eye128": eye128,
    }


_DEBUG = ()


def kernel(**inputs):
    nc = build_program(debug=_DEBUG)
    in_maps = [_prep_core_inputs(c, inputs) for c in range(NC)]
    res = run_bass_kernel_spmd(nc, in_maps, core_ids=list(range(NC)))
    outs = [np.asarray(res.results[c]["out"]) for c in range(NC)]
    full = np.stack(outs).reshape(OUT_SHAPE)
    if _DEBUG:
        kernel._dbg = [{k: np.asarray(v) for k, v in res.results[c].items()}
                       for c in range(NC)]
    return full


# version 04 (final)
# speedup vs baseline: 1.2258x; 1.0136x over previous
import contextlib

import numpy as np
import ml_dtypes
import concourse.bass as bass
import concourse.mybir as mybir
from concourse.bass_utils import run_bass_kernel_spmd

F32 = mybir.dt.float32
BF16 = mybir.dt.bfloat16
FP8 = mybir.dt.float8e4
AF = mybir.ActivationFunctionType
ALU = mybir.AluOpType
AXX = mybir.AxisListType.X

MAGIC = 3 * 2.0 ** 22

D = 1024
DI = 2048
DH = 1024
NST = 16
DTR = 64
S = 4096
TOWN = 2048
NC = 8
TC = 1024
NCHUNK = S // TC
PAIRS = [[2 * k, 2 * k + 1] for k in range(NC // 2)]
ALLG = [list(range(NC))]
OUT_SHAPE = (4, 4096, 1024)

ENGINES = ("sync", "vector", "scalar", "tensor", "gpsimd")


class PB:

    def __init__(self, nc):
        self.nc = nc
        self.q = {e: [] for e in ENGINES}
        self.cnt = {}
        self.sems = {}
        self.alias = {}
        self.autodrain = True
        self._stack = contextlib.ExitStack()

    def sem(self, name):
        if name not in self.sems:
            self.sems[name] = self._stack.enter_context(self.nc.semaphore(name))
            self.cnt[name] = 0
        return self.sems[name]

    KEEP = {"lntpe", "ipmm", "fcmm", "pjmm", "qkmm", "womm", "dqe", "pjev",
            "p3e", "cc_mix", "cc_kv", "cc_q", "sc_v", "sc_s", "scn", "ymm"}
    CANON = {"vector": "cv", "scalar": "cs", "tensor": "ct", "gpsimd": "cg"}

    def op(self, engine, method, *args, inc=None, winc=1, wait=None, **kw):
        if (inc is not None and method != "dma_start"
                and method != "collective_compute" and inc not in self.KEEP):
            canon = self.CANON[engine]
            old = self.alias.get(inc)
            assert old in (None, canon), (inc, old, canon)
            self.alias[inc] = canon
            inc = canon
        waits = []
        if wait:
            if not isinstance(wait, list):
                wait = [wait]
            for w in wait:
                if isinstance(w, str):
                    wname, wval = w, None
                else:
                    wname, wval = w
                wname = self.alias.get(wname, wname)
                self.sem(wname)
                wv = self.cnt[wname] if wval is None else wval
                if wv > 0:
                    waits.append((self.sem(wname), wv))
        sem = None
        if inc is not None:
            sem = self.sem(inc)
            self.cnt[inc] += winc

        drain = kw.pop("drain", False)
        if (self.autodrain and engine in ("vector", "scalar", "gpsimd")
                and method not in ("dma_start", "collective_compute")):
            drain = True

        def fn(eng, _m=method, _a=args, _k=kw, _s=sem, _w=waits, _wi=winc,
               _d=drain):
            for (sh, wv) in _w:
                eng.wait_ge(sh, wv)
            if _d:
                eng.drain()
            r = getattr(eng, _m)(*_a, **_k)
            if _s is not None:
                r.then_inc(_s, _wi)
        self.q[engine].append(fn)

    def dma(self, engine, out, in_, inc, wait=None, **kw):
        self.op(engine, "dma_start", inc=inc, winc=16, wait=wait,
                out=out, in_=in_, **kw)

    def wait(self, engine, name, value=None):
        name = self.alias.get(name, name)
        self.sem(name)
        v = self.cnt[name] if value is None else value
        if v <= 0:
            return
        sem = self.sem(name)
        self.q[engine].append(lambda eng, _s=sem, _v=v: eng.wait_ge(_s, _v))

    def val(self, name):
        name = self.alias.get(name, name)
        self.sem(name)
        return self.cnt[name]

    def replay(self, block):
        pb = self

        @block.sync
        def _(eng):
            for fn in pb.q["sync"]:
                fn(eng)

        @block.vector
        def _(eng):
            for fn in pb.q["vector"]:
                fn(eng)

        @block.scalar
        def _(eng):
            for fn in pb.q["scalar"]:
                fn(eng)

        @block.tensor
        def _(eng):
            for fn in pb.q["tensor"]:
                fn(eng)

        @block.gpsimd
        def _(eng):
            for fn in pb.q["gpsimd"]:
                fn(eng)


def build_program(debug=()):
    nc = bass.Bass("TRN2", target_bir_lowering=False, num_devices=NC)
    stack = contextlib.ExitStack()

    def din(name, shape, dt=F32):
        return nc.dram_tensor(name, list(shape), dt, kind="ExternalInput").ap()

    def sbt(name, shape, dt=F32):
        return stack.enter_context(nc.sbuf_tensor(name, list(shape), dt))

    x_batch = din("x_batch", [S, D])
    x_own = din("x_own", [TOWN, D])
    winT = din("winT", [24, 128, 1024])
    conv_w_cols = din("conv_w_cols", [128, 64])
    conv_b_cols = din("conv_b_cols", [128, 16])
    ln1w_col = din("ln1w_col", [128, 8])
    ln2w_col_in = din("ln2w_col_in", [128, 8])
    xprojT = din("xprojT", [DI, 112])
    dtwT = din("dtwT", [DTR, DH])
    dtb_col = din("dtb_col", [128, 8])
    alog_scan = din("alog_scan", [128, 128])
    dp_col = din("dp_col", [128, 8])
    woutT = din("woutT", [DH, D])
    fcT_tiled = din("fcT_tiled", [32, 128, 1024])
    l1_qkvo = din("l1_qkvo", [4, 4, 8, 128, 1024])
    l1_w1 = din("l1_w1", [4, 32, 128, 1024])
    l1_w2 = din("l1_w2", [4, 4096, 1024])
    l1_bqkvo = din("l1_bqkvo", [4, 4, 128, 8])
    l1_b1c = din("l1_b1c", [4, 128, 32])
    l1_b2r = din("l1_b2r", [4, 1, 1024])
    l1_lnsb = din("l1_lnsb", [4, 4, 1, 1024])
    projT = din("projT", [4096, 1024])
    w8T = din("w8T", [D, D], BF16)
    ws_in = din("ws_in", [1, 1])
    bias_row_in = din("bias_row_in", [1, D])
    sel16 = din("sel16", [16, 128], BF16)
    sely = din("sely", [128, 16 * 128], BF16)
    eye128 = din("eye128", [128, 128], BF16)
    eye128f = din("eye128f", [128, 128])
    wup_in = din("wup_in", [3072, 1024])
    wg_in = din("wg_in", [3072, 1024])
    wd_in = din("wd_in", [1024, 3072])

    out = nc.dram_tensor("out", [TOWN, D], F32, kind="ExternalOutput").ap()
    dbg = {}
    for name, shape in debug:
        dbg[name] = nc.dram_tensor("dbg_" + name, list(shape), F32,
                                   kind="ExternalOutput").ap()

    dt_hbm = nc.dram_tensor("dt_hbm", [DH, 1, TC], F32)
    w_hbm = nc.dram_tensor("w_hbm", [DH, 1, TC], F32)
    mix_part = nc.dram_tensor("mix_part", [2 * TOWN, D], F32)
    mix_sum = nc.dram_tensor("mix_sum", [TOWN, D], F32)

    pb = PB(nc)
    pb.autodrain = False

    xnb16 = sbt("xnb16", [128, 1024], BF16)
    sq32 = sbt("sq32", [128, 1024])
    stat = sbt("stat", [128, 16])
    ccol = sbt("ccol", [128, 128])
    kcol = sbt("kcol", [128, 8])
    eye_sb = sbt("eye_sb", [128, 128], BF16)
    mstack = contextlib.ExitStack()

    def mbt(name, shape, dt=F32):
        return mstack.enter_context(nc.sbuf_tensor(name, list(shape), dt))

    xtile = mbt("xtile", [128, 1024])
    sp32a = mbt("sp32a", [128, 1024])
    sp32b = mbt("sp32b", [128, 1024])
    ev32 = mbt("ev32", [128, 1024])
    xn_fm = mbt("xn_fm", [128, 8 * TC], BF16)
    wib = mbt("wib", [128, 2 * 1024], BF16)
    xr_rot = mbt("xr_rot", [128, TC + 4])
    halo = mbt("halo", [128, 64])
    cacc = mbt("cacc", [128, TC])
    xc_own = mbt("xc_own", [128, 8 * TC], BF16)
    xc_rot = mbt("xc_rot", [128, TC], BF16)
    z_b = mbt("z_b", [128, 8 * TC], BF16)
    xprojT_sb = mbt("xprojT_sb", [128, 16 * 112], BF16)
    dtw_sb = mbt("dtw_sb", [64, DH], BF16)
    wout_sb = mbt("wout_sb", [128, 8 * 1024], BF16)
    dcb_dtr = mbt("dcb_dtr", [64, TC], BF16)
    dcb_bc = mbt("dcb_bc", [16, TC], BF16)
    dcb_cc = mbt("dcb_cc", [16, TC], BF16)
    dt32 = mbt("dt32", [128, TC])
    w32 = mbt("w32", [128, TC])
    bcrep = mbt("bcrep", [128, TC], BF16)
    ccrep = mbt("ccrep", [128, TC], BF16)
    dtrep = mbt("dtrep", [128, 2 * TC], BF16)
    wrep = mbt("wrep", [128, 2 * TC], BF16)
    dA_b = mbt("dA_b", [128, 2 * TC], BF16)
    dBx_b = mbt("dBx_b", [128, 2 * TC], BF16)
    h32 = mbt("h32", [128, 2 * TC])
    hC_b = mbt("hC_b", [128, 2 * TC], BF16)
    hcarry = mbt("hcarry", [128, 128])
    yg_b = mbt("yg_b", [128, 8 * TC], BF16)
    zsil = mbt("zsil", [128, 512])
    acol = mbt("acol", [128, 128])
    sely_sb = mbt("sely_sb", [128, 16 * 128], BF16)
    sel16_sb = mbt("sel16_sb", [16, 128], BF16)

    psum = [stack.enter_context(nc.psum_tensor(
                f"ps{i}", [128, 512], BF16 if i in (4, 5) else F32))
            for i in range(8)]

    LN1W, CONVW, CONVB, DTB, DP = 0, 16, 80, 96, 104

    pb.dma("sync", ccol[:, LN1W:LN1W + 8], ln1w_col, "cld")
    pb.dma("sync", ccol[:, CONVW:CONVW + 64], conv_w_cols, "cld")
    pb.dma("sync", ccol[:, CONVB:CONVB + 16], conv_b_cols, "cld")
    pb.dma("sync", ccol[:, DTB:DTB + 8], dtb_col, "cld")
    pb.dma("sync", ccol[:, DP:DP + 8], dp_col, "cld")
    pb.dma("sync", sel16_sb[:], sel16, "cld")
    pb.dma("sync", sely_sb[:], sely, "cld")
    pb.dma("sync", eye_sb[:], eye128, "cld")
    pb.dma("sync", acol[:], alog_scan, "cld")
    pb.op("gpsimd", "dma_start", inc="cldg", winc=16,
          out=xprojT_sb[:].rearrange("p (c r) -> p c r", c=16),
          in_=xprojT.rearrange("(c p) r -> p c r", p=128))
    pb.dma("gpsimd", dtw_sb[:], dtwT, "cldg")
    pb.op("gpsimd", "dma_start", inc="cldg", winc=16,
          out=wout_sb[:].rearrange("p (c e) -> p c e", c=8),
          in_=woutT.rearrange("(c p) e -> p c e", p=128))

    pb.op("scalar", "activation", wait=[("cld", None)], inc="aprep",
          out=acol[:], in_=acol[:], func=AF.Exp)
    pb.op("vector", "tensor_scalar", wait=[("aprep", None)], inc="aprepv",
          out=acol[:], in0=acol[:], scalar1=-1.0, scalar2=0.0,
          op0=ALU.mult, op1=ALU.add)
    apv_val = pb.val("aprepv")
    pb.op("vector", "memset", kcol[:, 0:1], 1e-5, inc="minit")
    pb.op("vector", "memset", kcol[:, 1:2], 1.0, inc="minit")
    pb.op("vector", "memset", kcol[:, 2:3], 1e-6, inc="minit")
    pb.op("vector", "memset", halo[:], 0.0, inc="minit")
    pb.op("vector", "memset", hcarry[:], 0.0, inc="minit")
    pb.wait("vector", "cld")
    pb.wait("scalar", "cld")
    pb.wait("tensor", "cld")
    pb.wait("tensor", "cldg")

    for ci in range(NCHUNK):
        t0 = ci * TC

        for j in range(TC // 128):
            pb.dma("sync", xtile[:], x_batch[t0 + j * 128: t0 + (j + 1) * 128, :],
                   "d_xld", wait=[("lnap", None)])
            pb.op("scalar", "activation",
                  wait=[("d_xld", None), ("st_v", None)], inc="st_s",
                  out=sq32[:], in_=xtile[:], func=AF.Square,
                  accum_out=stat[:, 0:1])
            pb.op("vector", "tensor_reduce",
                  wait=[("d_xld", None), ("st_s", None)],
                  inc="st_v", out=stat[:, 1:2], in_=xtile[:], axis=AXX,
                  op=ALU.add)
            pb.op("vector", "tensor_scalar", inc="st_v", out=stat[:, 2:3],
                  in0=stat[:, 1:2], scalar1=1.0 / D, scalar2=0.0,
                  op0=ALU.mult, op1=ALU.add, drain=True)
            pb.op("vector", "tensor_scalar", inc="st_v", out=stat[:, 3:4],
                  in0=stat[:, 0:1], scalar1=1.0 / D, scalar2=0.0,
                  op0=ALU.mult, op1=ALU.add)
            pb.op("vector", "tensor_tensor", inc="st_v", out=stat[:, 4:5],
                  in0=stat[:, 2:3], in1=stat[:, 2:3], op=ALU.mult, drain=True)
            pb.op("vector", "tensor_tensor", inc="st_v", out=stat[:, 5:6],
                  in0=stat[:, 3:4], in1=stat[:, 4:5], op=ALU.subtract,
                  drain=True)
            pb.op("scalar", "activation",
                  wait=[("st_v", None), ("minit", None)], inc="st_s2",
                  out=stat[:, 6:7], in_=stat[:, 5:6], func=AF.Sqrt,
                  bias=kcol[:, 0:1], scale=1.0)
            pb.op("vector", "reciprocal", wait=[("st_s2", None)], inc="st_v",
                  out=stat[:, 7:8], in_=stat[:, 6:7])
            pb.op("vector", "tensor_scalar", inc="lnap",
                  out=xnb16[:], in0=xtile[:], scalar1=stat[:, 2:3],
                  scalar2=stat[:, 7:8], op0=ALU.subtract, op1=ALU.mult,
                  drain=True)
            for k in range(8):
                pk = psum[4 + (k % 2)]
                tp_wait = [("lnap", None)]
                if pb.val("lntpe") >= 2:
                    tp_wait.append(("lntpe", pb.val("lntpe") - 1))
                pb.op("tensor", "transpose", pk[:, 0:128],
                      xnb16[:, k * 128:(k + 1) * 128], eye_sb[:],
                      wait=tp_wait, inc="lntpp")
                pb.op("vector", "tensor_scalar", wait=[("lntpp", None)],
                      inc="lntpe",
                      out=xn_fm[:, k * TC + j * 128: k * TC + (j + 1) * 128],
                      in0=pk[:, 0:128], scalar1=ccol[:, LN1W + k:LN1W + k + 1],
                      scalar2=0.0, op0=ALU.mult, op1=ALU.add)

        for et in range(24):
            wb = wib[:, (et % 2) * 1024:(et % 2) * 1024 + 1024]
            wsem = "d_wib%d" % (et % 2)
            dma_wait = None
            if pb.val("ipmm") >= 32:
                dma_wait = [("ipmm", pb.val("ipmm") - 16)]
            pb.op("gpsimd", "dma_start", inc=wsem, winc=16, wait=dma_wait,
                  out=wb, in_=winT[et])
            pb.wait("tensor", wsem)
            pb.wait("tensor", "lntpe")
            for th in range(2):
                pe = psum[th] if th == 0 else psum[6 + (et % 2)]
                if th == 0:
                    pe = psum[et % 2]
                for k in range(8):
                    mm_wait = [("scn", None)] if False else None
                    if k == 0:
                        mm_wait = [("ipev", None), ("cnv", None),
                                   ("xpmm", None)]
                    pb.op("tensor", "matmul", pe[:, 0:512],
                          wb[:, k * 128:(k + 1) * 128],
                          xn_fm[:, k * TC + th * 512: k * TC + th * 512 + 512],
                          start=(k == 0), stop=(k == 7),
                          wait=mm_wait, inc="ipmm")
            if et < 16:
                pb.op("vector", "tensor_copy", wait=[("ipmm", None)],
                      inc="ipev", out=xr_rot[:, 4:4 + 512],
                      in_=psum[et % 2][:, 0:512])
                pb.op("vector", "tensor_copy", inc="ipev",
                      out=xr_rot[:, 4 + 512:4 + 1024],
                      in_=psum[6 + (et % 2)][:, 0:512])
                pb.op("vector", "tensor_copy", inc="cnv",
                      out=xr_rot[:, 0:4], in_=halo[:, et * 4:(et + 1) * 4])
                pb.op("vector", "tensor_scalar", inc="cnv", out=cacc[:],
                      drain=True, in0=xr_rot[:, 1:1 + TC],
                      scalar1=ccol[:, CONVW + et * 4:CONVW + et * 4 + 1],
                      scalar2=0.0, op0=ALU.mult, op1=ALU.add)
                for tap in range(1, 4):
                    pb.op("vector", "scalar_tensor_tensor", inc="cnv",
                          drain=True, out=cacc[:],
                          in0=xr_rot[:, tap + 1:tap + 1 + TC],
                          scalar=ccol[:, CONVW + et * 4 + tap:
                                       CONVW + et * 4 + tap + 1],
                          in1=cacc[:], op0=ALU.mult, op1=ALU.add)
                pb.op("vector", "tensor_copy", inc="cnv",
                      out=halo[:, et * 4:(et + 1) * 4],
                      in_=xr_rot[:, TC:TC + 4])
                if et < 8:
                    xc_dst = xc_own[:, et * TC:(et + 1) * TC]
                else:
                    xc_dst = xc_rot[:]
                pb.op("scalar", "activation",
                      wait=[("cnv", None), ("xpmm", None)], inc="cnv_s",
                      out=xc_dst, in_=cacc[:], func=AF.Silu,
                      bias=ccol[:, CONVB + et:CONVB + et + 1], scale=1.0)
                for th in range(2):
                    pb.op("tensor", "matmul", psum[2 + th][0:112, 0:512],
                          xprojT_sb[:, et * 112:(et + 1) * 112],
                          xc_dst[:, th * 512:th * 512 + 512],
                          start=(et == 0), stop=(et == 15),
                          wait=[("cnv_s", None)], inc="xpmm")
            else:
                zi = (et - 16) * TC
                pb.op("vector", "tensor_copy", wait=[("ipmm", None)],
                      inc="ipev", out=z_b[:, zi:zi + 512],
                      in_=psum[et % 2][:, 0:512])
                pb.op("vector", "tensor_copy", inc="ipev",
                      out=z_b[:, zi + 512:zi + 1024],
                      in_=psum[6 + (et % 2)][:, 0:512])

        for th in range(2):
            pb.op("vector", "tensor_copy", wait=[("xpmm", None)], inc="xde",
                  out=dcb_dtr[:, th * 512:th * 512 + 512],
                  in_=psum[2 + th][0:64, 0:512])
            pb.op("vector", "tensor_copy", inc="xde",
                  out=dcb_bc[:, th * 512:th * 512 + 512],
                  in_=psum[2 + th][64:80, 0:512])
            pb.op("vector", "tensor_copy", inc="xde",
                  out=dcb_cc[:, th * 512:th * 512 + 512],
                  in_=psum[2 + th][96:112, 0:512])

        for th in range(2):
            pb.op("tensor", "matmul", psum[2][:, 0:512], sel16_sb[:],
                  dcb_bc[:, th * 512:th * 512 + 512], start=True, stop=True,
                  wait=[("xde", None), ("bcre", None)], inc="bcr")
            pb.op("vector", "tensor_copy", wait=[("bcr", None)], inc="bcre",
                  out=bcrep[:, th * 512:th * 512 + 512], in_=psum[2][:, 0:512])
            pb.op("tensor", "matmul", psum[3][:, 0:512], sel16_sb[:],
                  dcb_cc[:, th * 512:th * 512 + 512], start=True, stop=True,
                  wait=[("bcre", None)], inc="bcr")
            pb.op("vector", "tensor_copy", wait=[("bcr", None)], inc="bcre",
                  out=ccrep[:, th * 512:th * 512 + 512], in_=psum[3][:, 0:512])

        for dtile in range(8):
            for th in range(2):
                mm_wait = [("xde", None), ("bcre", None), ("dtv", None)]
                pb.op("tensor", "matmul", psum[2 + th][:, 0:512],
                      dtw_sb[:, dtile * 128:(dtile + 1) * 128],
                      dcb_dtr[:, th * 512:th * 512 + 512],
                      start=True, stop=True, wait=mm_wait, inc="dtmm")
            for th in range(2):
                ps = psum[2 + th]
                sl = slice(th * 512, th * 512 + 512)
                pb.op("vector", "tensor_scalar",
                      wait=[("dtmm", None), ("d_dtw", None)],
                      inc="dtv", out=sp32a[:, sl], in0=ps[:, 0:512],
                      scalar1=ccol[:, DTB + dtile:DTB + dtile + 1],
                      scalar2=0.0, op0=ALU.add, op1=ALU.add)
                pb.op("vector", "tensor_scalar", inc="dtv", out=sp32b[:, sl],
                      in0=sp32a[:, sl], scalar1=-1.0, scalar2=0.0,
                      op0=ALU.mult, op1=ALU.add, drain=True)
                pb.op("vector", "tensor_tensor", inc="dtv", out=sp32b[:, sl],
                      in0=sp32a[:, sl], in1=sp32b[:, sl], op=ALU.max,
                      drain=True)
                pb.op("scalar", "activation", wait=[("dtv", None)], inc="dts",
                      out=sp32b[:, sl], in_=sp32b[:, sl], func=AF.Exp,
                      scale=-1.0)
                pb.op("scalar", "activation", inc="dts", out=sp32b[:, sl],
                      in_=sp32b[:, sl], func=AF.Ln, bias=kcol[:, 1:2],
                      scale=1.0, drain=True)
                pb.op("vector", "scalar_tensor_tensor", wait=[("dts", None)],
                      inc="dtev", out=dt32[:, sl], in0=sp32a[:, sl],
                      scalar=0.0, in1=sp32b[:, sl], op0=ALU.max, op1=ALU.add,
                      drain=True)
            pb.op("vector", "tensor_tensor", inc="dtev", out=w32[:],
                  in0=dt32[:], in1=xc_own[:, dtile * TC:(dtile + 1) * TC],
                  op=ALU.mult, drain=True)
            dtw_wait = [("dtev", None)]
            if ci > 0 and dtile == 0:
                dtw_wait.append(("d_rep0", None))
                dtw_wait.append(("d_rep1", None))
            pb.dma("sync", dt_hbm.ap()[dtile * 128:(dtile + 1) * 128, 0, :],
                   dt32[:], "d_dtw", wait=dtw_wait)
            pb.dma("sync", w_hbm.ap()[dtile * 128:(dtile + 1) * 128, 0, :],
                   w32[:], "d_dtw")

        for g in range(8):
            for i16 in range(16):
                s = g * 16 + i16
                rot = s % 2
                src_dt = dt_hbm.ap()[s * 8:(s + 1) * 8, :, :] \
                    .broadcast_to([8, 16, TC])
                src_w = w_hbm.ap()[s * 8:(s + 1) * 8, :, :] \
                    .broadcast_to([8, 16, TC])
                dst_dt = dtrep[:, rot * TC:rot * TC + TC]
                dst_w = wrep[:, rot * TC:rot * TC + TC]
                sN = ci * 128 + s
                repsem = "d_rep%d" % rot
                wdep = [("d_dtw", None)]
                if pb.val("sc_s") >= 2:
                    wdep.append(("sc_s", pb.val("sc_s") - 1))
                    wdep.append(("sc_v", pb.val("sc_v") - 5))
                pb.op("gpsimd", "dma_start", inc=repsem, winc=16,
                      out=dst_dt, in_=src_dt, wait=wdep)
                pb.op("gpsimd", "dma_start", inc=repsem, winc=16,
                      out=dst_w, in_=src_w)
                expw = [(repsem, None), ("aprepv", apv_val)]
                if pb.val("sc_v") >= 5:
                    expw.append(("sc_v", pb.val("sc_v") - 4))
                pb.op("scalar", "activation", wait=expw, inc="sc_s",
                      out=dA_b[:, rot * TC:rot * TC + TC],
                      in_=dtrep[:, rot * TC:rot * TC + TC],
                      func=AF.Exp, scale=acol[:, s:s + 1])
                pb.op("vector", "tensor_tensor",
                      wait=[(repsem, None)],
                      inc="sc_v", out=dBx_b[:, rot * TC:rot * TC + TC],
                      in0=wrep[:, rot * TC:rot * TC + TC], in1=bcrep[:],
                      op=ALU.mult)
                pb.op("vector", "tensor_tensor_scan", wait=[("sc_s", None)],
                      inc="sc_v", out=h32[:, rot * TC:rot * TC + TC],
                      data0=dA_b[:, rot * TC:rot * TC + TC],
                      data1=dBx_b[:, rot * TC:rot * TC + TC],
                      initial=hcarry[:, s:s + 1], op0=ALU.mult, op1=ALU.add,
                      drain=True)
                pb.op("vector", "tensor_copy", inc="sc_v",
                      out=hcarry[:, s:s + 1],
                      in_=h32[:, rot * TC + TC - 1:rot * TC + TC],
                      drain=True)
                hcw = []
                if pb.val("ymm") >= 4:
                    hcw.append(("ymm", pb.val("ymm") - 2))
                pb.op("vector", "tensor_tensor", inc="scn",
                      out=hC_b[:, rot * TC:rot * TC + TC],
                      in0=h32[:, rot * TC:rot * TC + TC], in1=ccrep[:],
                      op=ALU.mult, wait=hcw)
                for th in range(2):
                    ym_wait = [("scn", None)]
                    if i16 == 0:
                        ym_wait.append(("ygv", None))
                    pb.op("tensor", "matmul", psum[6 + th][:, 0:512],
                          sely_sb[:, i16 * 128:(i16 + 1) * 128],
                          hC_b[:, rot * TC + th * 512:rot * TC + th * 512 + 512],
                          start=(i16 == 0), stop=(i16 == 15),
                          wait=ym_wait, inc="ymm")
            for th in range(2):
                ps = psum[6 + th]
                sl = slice(g * TC + th * 512, g * TC + th * 512 + 512)
                pb.op("vector", "scalar_tensor_tensor", wait=[("ymm", None)],
                      inc="ygv0", drain=True, out=sp32a[:, 0:512],
                      in0=xc_own[:, sl], scalar=ccol[:, DP + g:DP + g + 1],
                      in1=ps[:, 0:512], op0=ALU.mult, op1=ALU.add)
                pb.op("scalar", "activation",
                      wait=[("ygv", None), ("ygv0", None)], inc="ygs",
                      out=zsil[:], in_=z_b[:, sl], func=AF.Silu)
                pb.op("vector", "tensor_tensor", wait=[("ygs", None)],
                      inc="ygv", out=yg_b[:, sl], in0=sp32a[:, 0:512],
                      in1=zsil[:], op=ALU.mult, drain=True)

        for tt in range(TC // 128):
            for th in range(2):
                for k in range(8):
                    mm_wait = [("ygv", None)]
                    if k == 0:
                        mm_wait.append(("opev", None))
                    pb.op("tensor", "matmul", psum[th][:, 0:512],
                          yg_b[:, k * TC + tt * 128: k * TC + (tt + 1) * 128],
                          wout_sb[:, k * 1024 + th * 512:
                                  k * 1024 + th * 512 + 512],
                          start=(k == 0), stop=(k == 7),
                          wait=mm_wait, inc="opmm")
                pb.op("vector", "tensor_copy",
                      wait=[("opmm", None), ("d_mp", None)], inc="opev",
                      out=ev32[:, th * 512:th * 512 + 512],
                      in_=psum[th][:, 0:512])
            pb.dma("sync", mix_part.ap()[t0 + tt * 128: t0 + (tt + 1) * 128, :],
                   ev32[:, 0:1024], "d_mp", wait=[("opev", None)])

    pb.wait("gpsimd", "d_mp")
    pb.op("gpsimd", "collective_compute", "ReduceScatter", ALU.add,
          replica_groups=PAIRS, ins=[mix_part.ap().opt()],
          outs=[mix_sum.ap().opt()], inc="cc_mix")

    mstack.close()
    stream = sbt("stream", [128, 16 * 1024])
    for j in range(16):
        pb.dma("sync", stream[:, j * 1024:(j + 1) * 1024],
               x_own[j * 128:(j + 1) * 128, :], "d_so",
               wait=[("opev", None), ("d_mp", None)])
    for j in range(16):
        pb.dma("gpsimd", sq32[:],
               mix_sum.ap()[j * 128:(j + 1) * 128, :], "d_sm",
               wait=[("cc_mix", None), ("res0", None)])
        pb.op("vector", "tensor_tensor",
              wait=[("d_so", None), ("d_sm", None)],
              inc="res0", out=stream[:, j * 1024:(j + 1) * 1024],
              in0=stream[:, j * 1024:(j + 1) * 1024], in1=sq32[:], op=ALU.add,
              drain=True)

    if "after_mixer" in dbg:
        for j in range(16):
            pb.dma("sync", dbg["after_mixer"][j * 128:(j + 1) * 128, :],
                   stream[:, j * 1024:(j + 1) * 1024], "d_dbg",
                   wait=[("res0", None)])


    bstack = contextlib.ExitStack()

    def bbt(name, shape, dt=F32):
        return bstack.enter_context(nc.sbuf_tensor(name, list(shape), dt))

    xfmA = sbt("xfmA", [128, 8 * 2048], BF16)
    wstr = sbt("wstr", [128, 2 * 2048], BF16)
    h_fm = bbt("h_fm", [128, 32 * 512], BF16)
    ln2w2_col = bbt("ln2w2_col", [128, 8])
    pb.dma("sync", ln2w2_col[:], ln2w_col_in, "d_ln2w")

    def stream_ln(ready, dst_fm=None, wcol=None, inplace=False):
        for j in range(16):
            xt = stream[:, j * 1024:(j + 1) * 1024]
            pb.op("vector", "bn_stats", wait=ready + [("st_s2", None)],
                  inc="st_v", out=stat[:, 0:6], in_=xt[:, 0:512])
            pb.op("vector", "bn_stats", inc="st_v", out=stat[:, 6:12],
                  in_=xt[:, 512:1024])
            pb.op("vector", "bn_aggr", inc="st_v", out=stat[:, 12:14],
                  in_=stat[:, 0:12], drain=True)
            pb.op("scalar", "activation", wait=[("st_v", None)], inc="st_s2",
                  out=stat[:, 14:15], in_=stat[:, 13:14], func=AF.Sqrt,
                  bias=kcol[:, 0:1], scale=1.0)
            pb.op("vector", "reciprocal", wait=[("st_s2", None)], inc="st_v",
                  out=stat[:, 15:16], in_=stat[:, 14:15])
            if inplace:
                pb.op("vector", "tensor_scalar", inc="lnap", out=xt, in0=xt,
                      scalar1=stat[:, 12:13], scalar2=stat[:, 15:16],
                      op0=ALU.subtract, op1=ALU.mult, drain=True)
            else:
                pb.op("vector", "tensor_scalar", inc="lnap", out=xnb16[:],
                      in0=xt, scalar1=stat[:, 12:13], scalar2=stat[:, 15:16],
                      op0=ALU.subtract, op1=ALU.mult, drain=True)
                for k in range(8):
                    pk = psum[4 + (k % 2)]
                    tp_wait = [("lnap", None)]
                    if pb.val("lntpe") >= 2:
                        tp_wait.append(("lntpe", pb.val("lntpe") - 1))
                    pb.op("tensor", "transpose", pk[:, 0:128],
                          xnb16[:, k * 128:(k + 1) * 128], eye_sb[:],
                          wait=tp_wait, inc="lntpp")
                    if wcol is None:
                        pb.op("vector", "tensor_copy", wait=[("lntpp", None)],
                              inc="lntpe",
                              out=dst_fm[:, k * 2048 + j * 128:
                                         k * 2048 + (j + 1) * 128],
                              in_=pk[:, 0:128])
                    else:
                        pb.op("vector", "tensor_scalar", wait=[("lntpp", None)],
                              inc="lntpe",
                              out=dst_fm[:, k * 2048 + j * 128:
                                         k * 2048 + (j + 1) * 128],
                              in0=pk[:, 0:128], scalar1=wcol[:, k:k + 1],
                              scalar2=0.0, op0=ALU.mult, op1=ALU.add)

    pb.wait("vector", "d_ln2w")
    stream_ln([("res0", None), ("d_dbg", None)], dst_fm=xfmA, wcol=ln2w2_col)

    for tch in range(4):
        ts0 = tch * 512
        for ft in range(32):
            par = ft % 2
            g = []
            if pb.val("fcmm") >= 16:
                g.append(("fcmm", pb.val("fcmm") - 8))
            if tch > 0 and ft < 2:
                g.append(("pjev", None))
            pb.op("gpsimd", "dma_start", inc="d_fc%d" % par, winc=16,
                  wait=g or None, out=wstr[:, par * 2048:par * 2048 + 1024],
                  in_=fcT_tiled[ft])
            pb.wait("tensor", "d_fc%d" % par)
            pb.wait("tensor", "lntpe")
            for k in range(8):
                mmw = None
                if k == 0:
                    mmw = [("fcev", None), ("pjmm", None), ("pjev", None)]
                pb.op("tensor", "matmul", psum[par][:, 0:512],
                      wstr[:, par * 2048 + k * 128:par * 2048 + (k + 1) * 128],
                      xfmA[:, k * 2048 + ts0: k * 2048 + ts0 + 512],
                      start=(k == 0), stop=(k == 7), wait=mmw, inc="fcmm")
            pb.op("scalar", "activation",
                  wait=[("fcmm", None), ("pjmm", None)], inc="fcev",
                  out=h_fm[:, ft * 512:(ft + 1) * 512],
                  in_=psum[par][:, 0:512], func=AF.Silu)
        for wave in range(2):
            for k in range(32):
                par = k % 2
                g = []
                if pb.val("pjmm") >= 8:
                    g.append(("pjmm", pb.val("pjmm") - 4))
                pb.op("gpsimd", "dma_start", inc="d_pj%d" % par, winc=16,
                      wait=g or None,
                      out=wstr[:, par * 2048 + 1024:par * 2048 + 2048],
                      in_=projT[k * 128:(k + 1) * 128, :])
                pb.wait("tensor", "d_pj%d" % par)
                for tt in range(2):
                    for eh in range(2):
                        mmw = None
                        if k == 0:
                            mmw = [("fcev", None), ("pjev", None)]
                        pb.op("tensor", "matmul", psum[tt * 2 + eh][:, 0:512],
                              h_fm[:, k * 512 + (wave * 2 + tt) * 128:
                                   k * 512 + (wave * 2 + tt + 1) * 128],
                              wstr[:, par * 2048 + 1024 + eh * 512:
                                   par * 2048 + 1024 + eh * 512 + 512],
                              start=(k == 0), stop=(k == 31),
                              wait=mmw, inc="pjmm")
            for tt in range(2):
                jj = 4 * tch + wave * 2 + tt
                for eh in range(2):
                    pb.op("vector", "tensor_tensor", wait=[("pjmm", None)],
                          inc="pjev",
                          out=stream[:, jj * 1024 + eh * 512:
                                     jj * 1024 + eh * 512 + 512],
                          in0=stream[:, jj * 1024 + eh * 512:
                                     jj * 1024 + eh * 512 + 512],
                          in1=psum[tt * 2 + eh][:, 0:512], op=ALU.add,
                          drain=True)

    for j in range(16):
        pb.dma("gpsimd", sq32[:], x_own[j * 128:(j + 1) * 128, :], "d_so2",
               wait=[("res2", None), ("pjev", None)])
        pb.op("vector", "tensor_tensor", wait=[("pjev", None), ("d_so2", None)],
              inc="res2", out=stream[:, j * 1024:(j + 1) * 1024],
              in0=stream[:, j * 1024:(j + 1) * 1024], in1=sq32[:], op=ALU.add,
              drain=True)
    stream_ln([("res2", None), ("d_dbg", None)], inplace=True)
    if "after_l0attn" in dbg:
        for j in range(16):
            pb.dma("sync", dbg["after_l0attn"][j * 128:(j + 1) * 128, :],
                   stream[:, j * 1024:(j + 1) * 1024], "d_dbg",
                   wait=[("lnap", None)])

    ws_col = bbt("ws_col", [128, 1])
    bias_row = bbt("bias_row", [1, 1024])
    bias_row_b = bbt("bias_row_b", [1, 1024], BF16)
    w8sb = bbt("w8sb", [128, 8 * 1024], BF16)
    wq_b = bbt("wq_b", [128, 8 * 1024], BF16)
    q8t = bbt("q8t", [128, 1024], FP8)
    sclC = bbt("sclC", [128, 16])
    rcol_b = bbt("rcol_b", [128, 1], BF16)
    rrow_b = bbt("rrow_b", [1, 128], BF16)

    pb.dma("sync", ws_col[:], ws_in.broadcast_to([128, 1]), "d_c2")
    pb.dma("sync", bias_row[:], bias_row_in, "d_c2")
    pb.op("gpsimd", "dma_start", inc="d_c2g", winc=16,
          out=w8sb[:].rearrange("p (c e) -> p c e", c=8),
          in_=w8T.rearrange("(c p) e -> p c e", p=128))
    pb.op("vector", "tensor_copy", wait=[("d_c2", None)], inc="c2v",
          out=bias_row_b[:], in_=bias_row[:])
    pb.op("vector", "tensor_scalar", wait=[("d_c2", None), ("d_c2g", None)],
          inc="c2v",
          out=wq_b[:], in0=w8sb[:], scalar1=ws_col[:, 0:1], scalar2=0.0,
          op0=ALU.mult, op1=ALU.add)
    for j in range(16):
        xt = stream[:, j * 1024:(j + 1) * 1024]
        pb.op("vector", "tensor_reduce",
              wait=[("lnap", None), ("d_dbg", None)], inc="q8v",
              out=stat[:, 9:10], in_=xt, axis=AXX, op=ALU.max,
              apply_absolute_value=True)
        pb.op("vector", "reciprocal", inc="q8v", out=stat[:, 10:11],
              in_=stat[:, 9:10], drain=True)
        pb.op("vector", "tensor_scalar", inc="q8v", out=sq32[:], in0=xt,
              scalar1=stat[:, 10:11], scalar2=224.0, op0=ALU.mult,
              op1=ALU.mult, drain=True)
        pb.op("vector", "tensor_copy", inc="q8v", out=q8t[:], in_=sq32[:],
              drain=True)
        pb.op("vector", "tensor_copy", inc="q8v", out=xnb16[:], in_=q8t[:],
              drain=True)
        pb.op("vector", "tensor_scalar", inc="q8v", out=sclC[:, j:j + 1],
              in0=stat[:, 9:10], scalar1=1.0 / 224.0, scalar2=0.0,
              op0=ALU.mult, op1=ALU.add, drain=True)
        for k in range(8):
            pk = psum[4 + (k % 2)]
            tp_wait = [("q8v", None)]
            if pb.val("lntpe") >= 2:
                tp_wait.append(("lntpe", pb.val("lntpe") - 1))
            pb.op("tensor", "transpose", pk[:, 0:128],
                  xnb16[:, k * 128:(k + 1) * 128], eye_sb[:],
                  wait=tp_wait, inc="lntpp")
            pb.op("vector", "tensor_copy", wait=[("lntpp", None)],
                  inc="lntpe",
                  out=xfmA[:, k * 2048 + j * 128: k * 2048 + (j + 1) * 128],
                  in_=pk[:, 0:128])

    for j in range(16):
        pb.op("vector", "reciprocal", wait=[("q8v", None)], inc="r1v",
              out=stat[:, 11:12], in_=sclC[:, j:j + 1], drain=True)
        pb.op("vector", "tensor_copy", inc="r1v", out=rcol_b[:],
              in_=stat[:, 11:12], drain=True)
        pb.op("tensor", "transpose", psum[5][0:1, 0:128], rcol_b[:],
              eye_sb[:], wait=[("r1v", None), ("lntpe", None)], inc="r1t")
        pb.op("vector", "tensor_copy", wait=[("r1t", None)], inc="r1e",
              out=rrow_b[:], in_=psum[5][0:1, 0:128])
        for eh in range(2):
            pe = psum[eh]
            for k in range(8):
                pb.op("tensor", "matmul", pe[:, 0:512],
                      xfmA[:, k * 2048 + j * 128: k * 2048 + (j + 1) * 128],
                      wq_b[:, k * 1024 + eh * 512: k * 1024 + eh * 512 + 512],
                      start=(k == 0), stop=False,
                      wait=([("q8ev", None)] if k == 0 else None),
                      inc="q8mm")
            pb.op("tensor", "matmul", pe[:, 0:512], rrow_b[:],
                  bias_row_b[:, eh * 512:eh * 512 + 512], start=False,
                  stop=True, wait=[("r1e", None)], inc="q8mm")
            pb.op("vector", "scalar_tensor_tensor", wait=[("q8mm", None)],
                  inc="q8ev", drain=True,
                  out=stream[:, j * 1024 + eh * 512: j * 1024 + eh * 512 + 512],
                  in0=pe[:, 0:512], scalar=sclC[:, j:j + 1],
                  in1=stream[:, j * 1024 + eh * 512: j * 1024 + eh * 512 + 512],
                  op0=ALU.mult, op1=ALU.add)

    stream_ln([("q8ev", None), ("d_dbg", None)], inplace=True)
    if "after_l0" in dbg:
        for j in range(16):
            pb.dma("sync", dbg["after_l0"][j * 128:(j + 1) * 128, :],
                   stream[:, j * 1024:(j + 1) * 1024], "d_dbg",
                   wait=[("lnap", None)])
    bstack.close()


    dstack = contextlib.ExitStack()

    def dbt(name, shape, dt=F32):
        return dstack.enter_context(nc.sbuf_tensor(name, list(shape), dt))

    xfmB = dbt("xfmB", [128, 8 * 2048], BF16)
    Kt = dbt("Kt", [128, 2048])
    vt = dbt("vt", [128, 2048])
    kvcols = dbt("kvcols", [128, 48])
    sbc = dbt("sbc", [128, 2 * 1024])
    bql = dbt("bql", [128, 8 * 4])
    b1l = dbt("b1l", [128, 32])
    b2r_b = dbt("b2r_b", [1, 1024], BF16)
    ones_row = dbt("ones_row", [1, 128], BF16)
    h_fm2 = dbt("h_fm2", [128, 32 * 512], BF16)

    kv_part = nc.dram_tensor("kv_part", [128, 16], F32)
    kv_sum = nc.dram_tensor("kv_sum", [128, 16], F32)
    xl1_hbm = nc.dram_tensor("xl1_hbm", [TOWN, D], F32)

    pb.op("vector", "memset", ones_row[:], 1.0, inc="c2v")
    for j in range(16):
        pb.dma("sync", xl1_hbm.ap()[j * 128:(j + 1) * 128, :],
               stream[:, j * 1024:(j + 1) * 1024], "d_xl1",
               wait=[("lnap", None)])

    def stream_to_fm(ready, dst_fm):
        for j in range(16):
            pb.op("vector", "tensor_copy", wait=ready + [("lntpp", None)],
                  inc="lnap", out=xnb16[:], drain=True,
                  in_=stream[:, j * 1024:(j + 1) * 1024])
            for k in range(8):
                pk = psum[4 + (k % 2)]
                tp_wait = [("lnap", None)]
                if pb.val("lntpe") >= 2:
                    tp_wait.append(("lntpe", pb.val("lntpe") - 1))
                pb.op("tensor", "transpose", pk[:, 0:128],
                      xnb16[:, k * 128:(k + 1) * 128], eye_sb[:],
                      wait=tp_wait, inc="lntpp")
                pb.op("vector", "tensor_copy", wait=[("lntpp", None)],
                      inc="lntpe",
                      out=dst_fm[:, k * 2048 + j * 128:
                                 k * 2048 + (j + 1) * 128],
                      in_=pk[:, 0:128])

    for blk in range(4):
        pb.dma("sync", bql[:], l1_bqkvo[blk].rearrange("w p c -> p w c"),
               "d_l1c", wait=[("qkev", None), ("wos", None), ("fcev", None),
                              ("q8ev", None), ("lnap", None)])
        pb.dma("sync", b1l[:], l1_b1c[blk], "d_l1c")
        pb.dma("gpsimd", b2r_b[:], l1_b2r[blk], "d_l1cg",
               wait=[("pjmm", None), ("q8ev", None), ("lnap", None)])

        stream_to_fm([("lnap", None), ("d_xl1", None), ("d_dbg", None)], xfmA)

        for at in range(8):
            for w in range(3):
                par = (at * 3 + w) % 2
                g = []
                if pb.val("qkmm") >= 64:
                    g.append(("qkmm", pb.val("qkmm") - 32))
                if at == 0 and w == 0:
                    g.append(("fcmm", None))
                pb.op("gpsimd", "dma_start", inc="d_qk%d" % par, winc=16,
                      wait=g or None,
                      out=wstr[:, par * 2048:par * 2048 + 1024],
                      in_=l1_qkvo[blk, w, at])
                pb.wait("tensor", "d_qk%d" % par)
                pb.wait("tensor", "lntpe")
                for tch in range(4):
                    pe = psum[par]
                    for k in range(8):
                        mmw = None
                        if k == 0:
                            mmw = [("qkev", None), ("pjev", None)]
                        pb.op("tensor", "matmul", pe[:, 0:512],
                              wstr[:, par * 2048 + k * 128:
                                   par * 2048 + (k + 1) * 128],
                              xfmA[:, k * 2048 + tch * 512:
                                   k * 2048 + tch * 512 + 512],
                              start=(k == 0), stop=(k == 7),
                              wait=mmw, inc="qkmm")
                    bcol = bql[:, w * 8 + at:w * 8 + at + 1]
                    sl = slice(tch * 512, tch * 512 + 512)
                    if w < 2:
                        dst = Kt if w == 1 else vt
                        pb.op("vector", "tensor_scalar",
                              wait=[("qkmm", None), ("d_l1c", None)],
                              inc="qkv1", out=sq32[:, 0:512], in0=pe[:, 0:512],
                              scalar1=bcol, scalar2=0.0, op0=ALU.add,
                              op1=ALU.min)
                        pb.op("scalar", "activation", wait=[("qkv1", None)],
                              inc="qks1", out=sq32[:, 512:1024],
                              in_=sq32[:, 0:512], func=AF.Exp)
                        pb.op("vector", "tensor_scalar", inc="qkv1",
                              out=dst[:, sl], in0=pe[:, 0:512],
                              scalar1=bcol, scalar2=0.0, op0=ALU.add,
                              op1=ALU.max)
                        if w == 0:
                            pb.op("vector", "tensor_tensor",
                                  wait=[("qks1", None)], inc="qkev",
                                  out=h_fm2[:, at * 2048 + tch * 512:
                                            at * 2048 + tch * 512 + 512],
                                  in0=dst[:, sl], in1=sq32[:, 512:1024],
                                  op=ALU.add, drain=True)
                        else:
                            pb.op("vector", "scalar_tensor_tensor",
                                  wait=[("qks1", None)], inc="qkev",
                                  out=Kt[:, sl], in0=Kt[:, sl], scalar=1.0,
                                  in1=sq32[:, 512:1024], op0=ALU.mult,
                                  op1=ALU.add, drain=True)
                    else:
                        pb.op("vector", "tensor_scalar",
                              wait=[("qkmm", None), ("d_l1c", None)],
                              inc="qkev", out=vt[:, sl], in0=pe[:, 0:512],
                              scalar1=bcol, scalar2=0.0, op0=ALU.add,
                              op1=ALU.add)
            pb.op("vector", "tensor_reduce",
                  wait=[("qkev", None), ("d_kvp", None)], inc="kvv",
                  out=kvcols[:, 8 + at:8 + at + 1], in_=Kt[:], axis=AXX,
                  op=ALU.add, drain=True)
            pb.op("vector", "scalar_tensor_tensor", inc="kvv",
                  out=Kt[:], in0=Kt[:], scalar=1.0, in1=vt[:],
                  op0=ALU.mult, op1=ALU.mult,
                  accum_out=kvcols[:, at:at + 1], drain=True)

        pb.dma("sync", kv_part.ap()[:, 0:16], kvcols[:, 0:16], "d_kvp",
               wait=[("kvv", None)])
        pb.wait("gpsimd", "d_kvp")
        pb.op("gpsimd", "collective_compute", "AllReduce", ALU.add,
              replica_groups=PAIRS, ins=[kv_part.ap().opt()],
              outs=[kv_sum.ap().opt()], inc="cc_kv")
        pb.dma("gpsimd", kvcols[:, 16:32], kv_sum.ap()[:, 0:16], "d_kvs",
               wait=[("cc_kv", None)])

        for at in range(8):
            qsl = slice(at * 2048, (at + 1) * 2048)
            pb.op("vector", "tensor_scalar", wait=[("d_kvs", None)], inc="vv",
                  drain=True, out=Kt[:], in0=h_fm2[:, qsl],
                  scalar1=kvcols[:, 24 + at:24 + at + 1], scalar2=1e-6,
                  op0=ALU.mult, op1=ALU.add)
            pb.op("vector", "reciprocal", inc="vv", out=Kt[:], in_=Kt[:],
                  drain=True)
            pb.op("vector", "tensor_scalar", inc="vv", out=vt[:],
                  drain=True, in0=h_fm2[:, qsl],
                  scalar1=kvcols[:, 16 + at:16 + at + 1], scalar2=0.0,
                  op0=ALU.mult, op1=ALU.add)
            pb.op("vector", "tensor_tensor", inc="vv", out=xfmB[:, qsl],
                  in0=vt[:], in1=Kt[:], op=ALU.mult, drain=True)

        for et in range(8):
            par = et % 2
            g = []
            if pb.val("womm") >= 64:
                g.append(("womm", pb.val("womm") - 32))
            pb.op("gpsimd", "dma_start", inc="d_wo%d" % par, winc=16,
                  wait=g or None, out=wstr[:, par * 2048:par * 2048 + 1024],
                  in_=l1_qkvo[blk, 3, et])
            pb.wait("tensor", "d_wo%d" % par)
            for tch in range(4):
                pe = psum[2 + par]
                for k in range(8):
                    mmw = [("vv", None), ("woev", None)] if k == 0 else None
                    pb.op("tensor", "matmul", pe[:, 0:512],
                          wstr[:, par * 2048 + k * 128:
                               par * 2048 + (k + 1) * 128],
                          xfmB[:, k * 2048 + tch * 512:
                               k * 2048 + tch * 512 + 512],
                          start=(k == 0), stop=(k == 7),
                          wait=mmw, inc="womm")
                bcol = bql[:, 24 + et:24 + et + 1]
                pb.op("scalar", "activation",
                      wait=[("womm", None), ("d_l1c", None)],
                      inc="wos", out=sq32[:, 0:512], in_=pe[:, 0:512],
                      func=AF.Gelu_apprx_tanh, bias=bcol, scale=1.0)
                pb.op("scalar", "activation", inc="woev", drain=True,
                      out=h_fm2[:, et * 2048 + tch * 512:
                                et * 2048 + tch * 512 + 512],
                      in_=sq32[:, 0:512], func=AF.Gelu_apprx_tanh)

        for j in range(16):
            for k in range(8):
                pk = psum[4 + (k % 2)]
                tp_wait = [("woev", None)]
                if pb.val("lntpe") >= 2:
                    tp_wait.append(("lntpe", pb.val("lntpe") - 1))
                pb.op("tensor", "transpose", pk[:, 0:128],
                      h_fm2[:, k * 2048 + j * 128: k * 2048 + (j + 1) * 128],
                      eye_sb[:], wait=tp_wait, inc="lntpp")
                pb.op("vector", "tensor_tensor", wait=[("lntpp", None)],
                      inc="lntpe",
                      out=stream[:, j * 1024 + k * 128: j * 1024 + (k + 1) * 128],
                      in0=stream[:, j * 1024 + k * 128:
                                 j * 1024 + (k + 1) * 128],
                      in1=pk[:, 0:128], op=ALU.add)

        pb.dma("gpsimd", sbc[:, 0:1024],
               l1_lnsb[blk, 0].broadcast_to([128, 1024]), "d_sbc",
               wait=[("lnsb", None), ("q8ev", None), ("lnap", None)])
        pb.dma("gpsimd", sbc[:, 1024:2048],
               l1_lnsb[blk, 1].broadcast_to([128, 1024]), "d_sbc")
        stream_ln([("lntpe", None), ("d_dbg", None)], inplace=True)
        for j in range(16):
            pb.op("vector", "tensor_tensor", wait=[("d_sbc", None)],
                  inc="lnsb", out=stream[:, j * 1024:(j + 1) * 1024],
                  in0=stream[:, j * 1024:(j + 1) * 1024],
                  in1=sbc[:, 0:1024], op=ALU.mult, drain=True)
            pb.op("vector", "tensor_tensor", inc="lnsb",
                  out=stream[:, j * 1024:(j + 1) * 1024],
                  in0=stream[:, j * 1024:(j + 1) * 1024],
                  in1=sbc[:, 1024:2048], op=ALU.add, drain=True)

        stream_to_fm([("lnsb", None)], xfmA)
        for tch in range(4):
            ts0 = tch * 512
            for ft in range(32):
                par = ft % 2
                g = []
                if pb.val("fcmm") >= 16:
                    g.append(("fcmm", pb.val("fcmm") - 8))
                if ft < 2:
                    g.append(("pjev", None))
                    if tch == 0:
                        g.append(("womm", None))
                pb.op("gpsimd", "dma_start", inc="d_fc%d" % par, winc=16,
                      wait=g or None,
                      out=wstr[:, par * 2048:par * 2048 + 1024],
                      in_=l1_w1[blk, ft])
                pb.wait("tensor", "d_fc%d" % par)
                pb.wait("tensor", "lntpe")
                for k in range(8):
                    mmw = None
                    if k == 0:
                        mmw = [("fcev", None), ("pjmm", None), ("pjev", None)]
                    pb.op("tensor", "matmul", psum[par][:, 0:512],
                          wstr[:, par * 2048 + k * 128:
                               par * 2048 + (k + 1) * 128],
                          xfmA[:, k * 2048 + ts0: k * 2048 + ts0 + 512],
                          start=(k == 0), stop=(k == 7), wait=mmw, inc="fcmm")
                pb.op("scalar", "activation",
                      wait=[("fcmm", None), ("pjmm", None), ("d_l1c", None)],
                      inc="fcev",
                      out=h_fm2[:, ft * 512:(ft + 1) * 512],
                      in_=psum[par][:, 0:512], func=AF.Gelu_apprx_tanh,
                      bias=b1l[:, ft:ft + 1], scale=1.0)
            for wave in range(2):
                for k in range(32):
                    par = k % 2
                    g = []
                    if pb.val("pjmm") >= 8:
                        g.append(("pjmm", pb.val("pjmm") - 4))
                    pb.op("gpsimd", "dma_start", inc="d_pj%d" % par, winc=16,
                          wait=g or None,
                          out=wstr[:, par * 2048 + 1024:par * 2048 + 2048],
                          in_=l1_w2[blk, k * 128:(k + 1) * 128, :])
                    pb.wait("tensor", "d_pj%d" % par)
                    for tt in range(2):
                        for eh in range(2):
                            mmw = None
                            if k == 0:
                                mmw = [("fcev", None), ("pjev", None),
                                       ("c2v", None), ("d_l1cg", None)]
                            pb.op("tensor", "matmul",
                                  psum[tt * 2 + eh][:, 0:512],
                                  h_fm2[:, k * 512 + (wave * 2 + tt) * 128:
                                        k * 512 + (wave * 2 + tt + 1) * 128],
                                  wstr[:, par * 2048 + 1024 + eh * 512:
                                       par * 2048 + 1024 + eh * 512 + 512],
                                  start=(k == 0), stop=False, wait=mmw,
                                  inc="pjmm")
                for tt in range(2):
                    for eh in range(2):
                        pb.op("tensor", "matmul", psum[tt * 2 + eh][:, 0:512],
                              ones_row[:],
                              b2r_b[:, eh * 512:eh * 512 + 512],
                              start=False, stop=True, inc="pjmm")
                for tt in range(2):
                    jj = 4 * tch + wave * 2 + tt
                    for eh in range(2):
                        slot = (tt * 2 + eh) % 2
                        gw = [("pjmm", None)]
                        if pb.val("pjev") >= 2:
                            gw.append(("pjev", pb.val("pjev") - 1))
                        pb.op("scalar", "activation", wait=gw, inc="pjs",
                              out=sq32[:, slot * 512:slot * 512 + 512],
                              in_=psum[tt * 2 + eh][:, 0:512],
                              func=AF.Gelu_apprx_tanh)
                        pb.op("vector", "tensor_tensor", wait=[("pjs", None)],
                              inc="pjev",
                              out=stream[:, jj * 1024 + eh * 512:
                                         jj * 1024 + eh * 512 + 512],
                              in0=stream[:, jj * 1024 + eh * 512:
                                         jj * 1024 + eh * 512 + 512],
                              in1=sq32[:, slot * 512:slot * 512 + 512],
                              op=ALU.add, drain=True)

        pb.dma("gpsimd", sbc[:, 0:1024],
               l1_lnsb[blk, 2].broadcast_to([128, 1024]), "d_sbc",
               wait=[("lnsb", None)])
        pb.dma("gpsimd", sbc[:, 1024:2048],
               l1_lnsb[blk, 3].broadcast_to([128, 1024]), "d_sbc")
        stream_ln([("pjev", None), ("d_dbg", None)], inplace=True)
        for j in range(16):
            pb.op("vector", "tensor_tensor", wait=[("d_sbc", None)],
                  inc="lnsb", out=stream[:, j * 1024:(j + 1) * 1024],
                  in0=stream[:, j * 1024:(j + 1) * 1024],
                  in1=sbc[:, 0:1024], op=ALU.mult, drain=True)
            pb.op("vector", "tensor_tensor", inc="lnsb",
                  out=stream[:, j * 1024:(j + 1) * 1024],
                  in0=stream[:, j * 1024:(j + 1) * 1024],
                  in1=sbc[:, 1024:2048], op=ALU.add, drain=True)

    for j in range(16):
        pb.dma("gpsimd", sq32[:], xl1_hbm.ap()[j * 128:(j + 1) * 128, :],
               "d_so3", wait=[("res3", None), ("lnsb", None)])
        pb.op("vector", "tensor_tensor", wait=[("lnsb", None), ("d_so3", None)],
              inc="res3", out=stream[:, j * 1024:(j + 1) * 1024],
              in0=stream[:, j * 1024:(j + 1) * 1024], in1=sq32[:], op=ALU.add,
              drain=True)
    stream_ln([("res3", None), ("d_dbg", None)], inplace=True)
    if "after_l1attn" in dbg:
        for j in range(16):
            pb.dma("sync", dbg["after_l1attn"][j * 128:(j + 1) * 128, :],
                   stream[:, j * 1024:(j + 1) * 1024], "d_dbg",
                   wait=[("lnap", None)])
    dstack.close()


    estack = contextlib.ExitStack()

    def ebt(name, shape, dt=F32):
        return estack.enter_context(nc.sbuf_tensor(name, list(shape), dt))

    wbig = ebt("wbig", [128, 24 * 1024], BF16)
    wt32 = ebt("wt32", [128, 3072])
    wt32b = ebt("wt32b", [128, 3072])
    g32 = ebt("g32", [128, 3072])
    rmax = ebt("rmax", [128, 96])
    rinv = ebt("rinv", [128, 96])
    r7 = ebt("r7", [128, 96])
    s7 = ebt("s7", [128, 96])
    wqb = ebt("wqb", [128, 3072], BF16)
    eye32_sb = ebt("eye32_sb", [128, 128])
    gcol = ebt("gcol", [128, 16])

    wupdq_hbm = nc.dram_tensor("wupdq_hbm", [1024, 3072], BF16)
    wgdq_hbm = nc.dram_tensor("wgdq_hbm", [1024, 3072], BF16)
    wddq_hbm = nc.dram_tensor("wddq_hbm", [3072, 1024], BF16)
    up_hbm = nc.dram_tensor("up_hbm", [TOWN, 3072], F32)
    gate_hbm = nc.dram_tensor("gate_hbm", [TOWN, 3072], F32)
    g2_hbm = nc.dram_tensor("g2_hbm", [TOWN, 3072], F32)
    upq_hbm = nc.dram_tensor("upq_hbm", [TOWN, 3072], F32)
    qs_part = nc.dram_tensor("qs_part", [1, 4], F32)
    qs_sum = nc.dram_tensor("qs_sum", [1, 4], F32)

    o_strip = xfmA
    pb.dma("sync", eye32_sb[:], eye128f, "d_ey", wait=[("lnap", None)])
    pb.op("vector", "memset", gcol[:, 0:8], 0.0, wait=[("lnap", None)],
          inc="einit")
    pb.dma("sync", qs_part.ap()[0:1, 0:4], gcol[0:1, 0:4], "d_qsp",
           wait=[("einit", None)])

    def dequant_weight(src_ap, ntiles, nfree, dst_hbm, tag):
        nblk = nfree // 32
        for pt in range(ntiles):
            pb.dma("gpsimd", wt32[:, 0:nfree],
                   src_ap[pt * 128:(pt + 1) * 128, :], "d_wt",
                   wait=[("dqv", None), ("lnap", None)])
            pb.op("vector", "tensor_reduce", wait=[("d_wt", None)], inc="dqv",
                  out=rmax[:, 0:nblk],
                  in_=wt32[:, 0:nfree].rearrange("p (b t) -> p b t", t=32),
                  axis=AXX, op=ALU.max, apply_absolute_value=True, drain=True)
            pb.op("vector", "reciprocal", inc="dqv", out=rinv[:, 0:nblk],
                  in_=rmax[:, 0:nblk], drain=True)
            pb.op("vector", "tensor_scalar", inc="dqv", out=r7[:, 0:nblk],
                  in0=rinv[:, 0:nblk], scalar1=7.0, scalar2=0.0,
                  op0=ALU.mult, op1=ALU.add, drain=True)
            pb.op("vector", "tensor_scalar", inc="dqv", out=s7[:, 0:nblk],
                  in0=rmax[:, 0:nblk], scalar1=1.0 / 7.0, scalar2=0.0,
                  op0=ALU.mult, op1=ALU.add, drain=True)
            pb.op("vector", "tensor_tensor", inc="dqv", out=wt32b[:, 0:nfree]
                  .rearrange("p (b t) -> p b t", t=32),
                  in0=wt32[:, 0:nfree].rearrange("p (b t) -> p b t", t=32),
                  in1=r7[:, 0:nblk].rearrange("p (b t) -> p b t", t=1)
                  .broadcast_to([128, nblk, 32]), op=ALU.mult, drain=True)
            pb.op("vector", "tensor_scalar", inc="dqv", out=wt32b[:, 0:nfree],
                  in0=wt32b[:, 0:nfree], scalar1=MAGIC, scalar2=MAGIC,
                  op0=ALU.add, op1=ALU.subtract, drain=True)
            pb.op("vector", "tensor_tensor", inc="dqv", out=wqb[:, 0:nfree]
                  .rearrange("p (b t) -> p b t", t=32),
                  in0=wt32b[:, 0:nfree].rearrange("p (b t) -> p b t", t=32),
                  in1=s7[:, 0:nblk].rearrange("p (b t) -> p b t", t=1)
                  .broadcast_to([128, nblk, 32]), op=ALU.mult, drain=True)
            for fc in range(nfree // 128):
                pk = psum[4 + (fc % 2)]
                tp_wait = [("dqv", None)]
                if pb.val("dqtp") >= 2:
                    tp_wait.append(("dqe", pb.val("dqe") - 1))
                pb.op("tensor", "transpose", pk[:, 0:128],
                      wqb[:, fc * 128:(fc + 1) * 128], eye_sb[:],
                      wait=tp_wait, inc="dqtp")
                pb.op("vector", "tensor_copy",
                      wait=[("dqtp", None), ("d_dq", None)],
                      inc="dqe", out=xnb16[:, 0:128], in_=pk[:, 0:128], drain=True)
                pb.dma("sync", dst_hbm.ap()[fc * 128:(fc + 1) * 128,
                                            pt * 128:(pt + 1) * 128],
                       xnb16[:, 0:128], "d_dq", wait=[("dqe", None)])

    dequant_weight(wup_in, 24, 1024, wupdq_hbm, "up")
    dequant_weight(wg_in, 24, 1024, wgdq_hbm, "g")
    dequant_weight(wd_in, 8, 3072, wddq_hbm, "d")

    for j in range(16):
        pb.op("vector", "tensor_reduce",
              wait=[("lnap", None), ("d_dbg", None), ("d_qsp", None)],
              inc="gxv", out=gcol[:, 0:1],
              in_=stream[:, j * 1024:(j + 1) * 1024], axis=AXX, op=ALU.max,
              apply_absolute_value=True, drain=True)
        if j == 0:
            pb.op("vector", "tensor_copy", inc="gxv", out=gcol[:, 1:2],
                  in_=gcol[:, 0:1], drain=True)
        else:
            pb.op("vector", "tensor_tensor", inc="gxv", out=gcol[:, 1:2],
                  in0=gcol[:, 1:2], in1=gcol[:, 0:1], op=ALU.max, drain=True)

    def part_max_to_slot(src_col, slot, tag):
        pb.op("tensor", "transpose", psum[2][0:1, 0:128], src_col, eye32_sb[:],
              wait=[(tag, None), ("pmx", None), ("d_ey", None)], inc="pmt")
        pb.op("vector", "tensor_copy",
              wait=[("pmt", None), ("d_ug", None), ("d_p1o", None),
                    ("d_p2o", None), ("d_wt", None)],
              inc="pmx",
              out=wt32[0:1, 0:128], in_=psum[2][0:1, 0:128], drain=True)
        pb.op("vector", "tensor_reduce", inc="pmx", out=wt32[0:1, 128:129],
              in_=wt32[0:1, 0:128], axis=AXX, op=ALU.max, drain=True)
        pb.dma("sync", qs_part.ap()[0:1, slot:slot + 1], wt32[0:1, 128:129],
               "d_qsp", wait=[("pmx", None), ("cc_q", None)])

    def ar_max():
        pb.wait("gpsimd", "d_qsp")
        pb.wait("gpsimd", "d_qsb")
        pb.op("gpsimd", "collective_compute", "AllReduce", ALU.max,
              replica_groups=ALLG, ins=[qs_part.ap().opt()],
              outs=[qs_sum.ap().opt()], inc="cc_q")

    part_max_to_slot(gcol[:, 1:2], 0, "gxv")
    ar_max()
    pb.dma("gpsimd", gcol[:, 2:3], qs_sum.ap()[0:1, 0:1].broadcast_to([128, 1]),
           "d_qsb", wait=[("cc_q", None)])
    pb.op("vector", "reciprocal", wait=[("d_qsb", None)], inc="qsv",
          out=gcol[:, 3:4], in_=gcol[:, 2:3], drain=True)
    pb.op("vector", "tensor_scalar", inc="qsv", out=gcol[:, 3:4],
          in0=gcol[:, 3:4], scalar1=32767.0, scalar2=0.0, op0=ALU.mult,
          op1=ALU.add, drain=True)
    pb.op("vector", "tensor_scalar", inc="qsv", out=gcol[:, 4:5],
          in0=gcol[:, 2:3], scalar1=1.0 / 32767.0, scalar2=0.0, op0=ALU.mult,
          op1=ALU.add, drain=True)

    for j in range(16):
        pb.op("vector", "tensor_scalar", wait=[("qsv", None)], inc="xqv",
              out=sq32[:], in0=stream[:, j * 1024:(j + 1) * 1024],
              scalar1=gcol[:, 3:4], scalar2=MAGIC, op0=ALU.mult, op1=ALU.add, drain=True)
        pb.op("vector", "tensor_scalar", inc="xqv", out=sq32[:],
              in0=sq32[:], scalar1=MAGIC, scalar2=0.0, op0=ALU.subtract,
              op1=ALU.add, drain=True)
        pb.op("vector", "tensor_copy", inc="xqv", out=xnb16[:], in_=sq32[:],
              wait=[("d_dq", None)], drain=True)
        for k in range(8):
            pk = psum[4 + (k % 2)]
            tp_wait = [("xqv", None)]
            if pb.val("lntpe") >= 2:
                tp_wait.append(("lntpe", pb.val("lntpe") - 1))
            pb.op("tensor", "transpose", pk[:, 0:128],
                  xnb16[:, k * 128:(k + 1) * 128], eye_sb[:],
                  wait=tp_wait, inc="lntpp")
            pb.op("vector", "tensor_copy", wait=[("lntpp", None)],
                  inc="lntpe",
                  out=xfmA[:, k * 2048 + j * 128: k * 2048 + (j + 1) * 128],
                  in_=pk[:, 0:128], drain=True)

    def upgate(dq_hbm, out_hbm, maxslot, tag):
        pb.op("gpsimd", "dma_start", inc="d_wb", winc=16,
              out=wbig[:].rearrange("p (c e) -> p c e", c=8),
              in_=dq_hbm.ap().rearrange("(c p) e -> p c e", p=128),
              wait=[("upmm", None), ("gamm", None), ("lnap", None)])
        pb.wait("tensor", "d_wb")
        pb.op("vector", "memset", gcol[:, 5:6], 0.0, inc=tag + "m")
        for j in range(16):
            for ih in range(6):
                pe = psum[ih % 2]
                for k in range(8):
                    mmw = None
                    if k == 0:
                        mmw = [("lntpe", None), (tag + "e", None)]
                    pb.op("tensor", "matmul", pe[:, 0:512],
                          xfmA[:, k * 2048 + j * 128: k * 2048 + (j + 1) * 128],
                          wbig[:, k * 3072 + ih * 512: k * 3072 + ih * 512 + 512],
                          start=(k == 0), stop=(k == 7), wait=mmw,
                          inc=tag + "mm")
                pb.op("vector", "tensor_scalar",
                      wait=[(tag + "mm", None), ("d_ug", None),
                            ("d_qsp", None)],
                      inc=tag + "e", out=wt32[:, 0:512], in0=pe[:, 0:512],
                      scalar1=gcol[:, 4:5], scalar2=0.0, op0=ALU.mult,
                      op1=ALU.add, drain=True)
                pb.op("vector", "tensor_reduce", inc=tag + "e",
                      out=gcol[:, 6:7], in_=wt32[:, 0:512], axis=AXX,
                      op=ALU.max, apply_absolute_value=True, drain=True)
                pb.op("vector", "tensor_tensor", inc=tag + "m",
                      out=gcol[:, 5:6], in0=gcol[:, 5:6], in1=gcol[:, 6:7],
                      op=ALU.max, drain=True)
                pb.dma("sync", out_hbm.ap()[j * 128:(j + 1) * 128,
                                            ih * 512:ih * 512 + 512],
                       wt32[:, 0:512], "d_ug", wait=[(tag + "e", None)])
        part_max_to_slot(gcol[:, 5:6], maxslot, tag + "m")

    upgate(wupdq_hbm, up_hbm, 1, "up")
    upgate(wgdq_hbm, gate_hbm, 2, "ga")
    ar_max()
    for slot, rc, sc in ((1, 8, 9), (2, 10, 11)):
        pb.dma("gpsimd", gcol[:, 7:8],
               qs_sum.ap()[0:1, slot:slot + 1].broadcast_to([128, 1]), "d_qsb",
               wait=[("cc_q", None), ("qsv", None)])
        pb.op("vector", "reciprocal", wait=[("d_qsb", None)], inc="qsv",
              out=gcol[:, rc:rc + 1], in_=gcol[:, 7:8], drain=True)
        pb.op("vector", "tensor_scalar", inc="qsv", out=gcol[:, rc:rc + 1],
              in0=gcol[:, rc:rc + 1], scalar1=32767.0, scalar2=0.0,
              op0=ALU.mult, op1=ALU.add, drain=True)
        pb.op("vector", "tensor_scalar", inc="qsv", out=gcol[:, sc:sc + 1],
              in0=gcol[:, 7:8], scalar1=1.0 / 32767.0, scalar2=0.0,
              op0=ALU.mult, op1=ALU.add, drain=True)

    pb.op("vector", "memset", gcol[:, 6:7], 0.0, wait=[("qsv", None)],
          inc="p1m")
    for j in range(16):
        pb.dma("gpsimd", wt32[:], up_hbm.ap()[j * 128:(j + 1) * 128, :],
               "d_p1", wait=[("p1v", None), ("d_p1o", None), ("d_ug", None),
                             ("d_qsp", None)])
        pb.dma("gpsimd", g32[:], gate_hbm.ap()[j * 128:(j + 1) * 128, :],
               "d_p1")
        pb.op("vector", "tensor_scalar", wait=[("d_p1", None)], inc="p1v",
              out=wt32[:], in0=wt32[:], scalar1=gcol[:, 8:9], scalar2=MAGIC,
              op0=ALU.mult, op1=ALU.add, drain=True)
        pb.op("vector", "tensor_scalar", inc="p1v", out=wt32[:], in0=wt32[:],
              scalar1=MAGIC, scalar2=gcol[:, 9:10], op0=ALU.subtract,
              op1=ALU.mult, drain=True)
        pb.op("vector", "tensor_scalar", inc="p1v", out=g32[:], in0=g32[:],
              scalar1=gcol[:, 10:11], scalar2=MAGIC, op0=ALU.mult,
              op1=ALU.add, drain=True)
        pb.op("vector", "tensor_scalar", inc="p1v", out=g32[:], in0=g32[:],
              scalar1=MAGIC, scalar2=gcol[:, 11:12], op0=ALU.subtract,
              op1=ALU.mult, drain=True)
        pb.op("scalar", "activation", wait=[("p1v", None)], inc="p1s",
              out=wt32b[:], in_=g32[:], func=AF.Sigmoid)
        pb.op("vector", "tensor_scalar", wait=[("p1s", None)], inc="p1v",
              out=wt32b[:], in0=wt32b[:], scalar1=65536.0, scalar2=MAGIC,
              op0=ALU.mult, op1=ALU.add, drain=True)
        pb.op("vector", "tensor_scalar", inc="p1v", out=wt32b[:],
              in0=wt32b[:], scalar1=MAGIC, scalar2=65535.0,
              op0=ALU.subtract, op1=ALU.min, drain=True)
        pb.op("vector", "tensor_scalar", inc="p1v", out=wt32b[:],
              in0=wt32b[:], scalar1=1.0 / 65536.0, scalar2=0.0,
              op0=ALU.mult, op1=ALU.add, drain=True)
        pb.op("vector", "tensor_tensor", inc="p1v", out=g32[:], in0=g32[:],
              in1=wt32b[:], op=ALU.mult, drain=True)
        pb.op("vector", "tensor_reduce", inc="p1v", out=gcol[:, 7:8],
              in_=g32[:], axis=AXX, op=ALU.max, apply_absolute_value=True, drain=True)
        pb.op("vector", "tensor_tensor", inc="p1m", out=gcol[:, 6:7],
              in0=gcol[:, 6:7], in1=gcol[:, 7:8], op=ALU.max, drain=True)
        pb.dma("sync", g2_hbm.ap()[j * 128:(j + 1) * 128, :], g32[:], "d_p1o",
               wait=[("p1v", None)])
        pb.dma("sync", upq_hbm.ap()[j * 128:(j + 1) * 128, :], wt32[:],
               "d_p1o")
    part_max_to_slot(gcol[:, 6:7], 3, "p1m")
    ar_max()
    pb.dma("gpsimd", gcol[:, 7:8],
           qs_sum.ap()[0:1, 3:4].broadcast_to([128, 1]), "d_qsb",
           wait=[("cc_q", None), ("qsv", None)])
    pb.op("vector", "reciprocal", wait=[("d_qsb", None)], inc="qsv",
          out=gcol[:, 12:13], in_=gcol[:, 7:8], drain=True)
    pb.op("vector", "tensor_scalar", inc="qsv", out=gcol[:, 12:13],
          in0=gcol[:, 12:13], scalar1=32767.0, scalar2=0.0, op0=ALU.mult,
          op1=ALU.add, drain=True)
    pb.op("vector", "tensor_scalar", inc="qsv", out=gcol[:, 13:14],
          in0=gcol[:, 7:8], scalar1=1.0 / 32767.0, scalar2=0.0,
          op0=ALU.mult, op1=ALU.add, drain=True)

    pb.op("vector", "memset", gcol[:, 5:6], 0.0, wait=[("qsv", None)],
          inc="p2m")
    for j in range(16):
        pb.dma("gpsimd", g32[:], g2_hbm.ap()[j * 128:(j + 1) * 128, :],
               "d_p2", wait=[("p2v", None), ("d_p2o", None), ("d_p1o", None),
                             ("d_qsp", None)])
        pb.dma("gpsimd", wt32[:], upq_hbm.ap()[j * 128:(j + 1) * 128, :],
               "d_p2")
        pb.op("vector", "tensor_scalar", wait=[("d_p2", None), ("d_ug", None)],
              inc="p2v", out=g32[:], in0=g32[:], scalar1=gcol[:, 12:13],
              scalar2=MAGIC, op0=ALU.mult, op1=ALU.add, drain=True)
        pb.op("vector", "tensor_scalar", inc="p2v", out=g32[:], in0=g32[:],
              scalar1=MAGIC, scalar2=gcol[:, 13:14], op0=ALU.subtract,
              op1=ALU.mult, drain=True)
        pb.op("vector", "tensor_tensor", inc="p2v", out=g32[:], in0=g32[:],
              in1=wt32[:], op=ALU.mult, drain=True)
        pb.op("vector", "tensor_reduce", inc="p2v", out=gcol[:, 7:8],
              in_=g32[:], axis=AXX, op=ALU.max, apply_absolute_value=True, drain=True)
        pb.op("vector", "tensor_tensor", inc="p2m", out=gcol[:, 5:6],
              in0=gcol[:, 5:6], in1=gcol[:, 7:8], op=ALU.max, drain=True)
        pb.dma("sync", up_hbm.ap()[j * 128:(j + 1) * 128, :], g32[:], "d_p2o",
               wait=[("p2v", None)])
    part_max_to_slot(gcol[:, 5:6], 0, "p2m")
    ar_max()
    pb.dma("gpsimd", gcol[:, 7:8],
           qs_sum.ap()[0:1, 0:1].broadcast_to([128, 1]), "d_qsb",
           wait=[("cc_q", None), ("qsv", None)])
    pb.op("vector", "reciprocal", wait=[("d_qsb", None)], inc="qsv",
          out=gcol[:, 14:15], in_=gcol[:, 7:8], drain=True)
    pb.op("vector", "tensor_scalar", inc="qsv", out=gcol[:, 14:15],
          in0=gcol[:, 14:15], scalar1=32767.0, scalar2=0.0, op0=ALU.mult,
          op1=ALU.add, drain=True)
    pb.op("vector", "tensor_scalar", inc="qsv", out=gcol[:, 15:16],
          in0=gcol[:, 7:8], scalar1=1.0 / 32767.0, scalar2=0.0,
          op0=ALU.mult, op1=ALU.add, drain=True)

    pb.op("gpsimd", "dma_start", inc="d_wd", winc=16,
          out=wbig[:].rearrange("p (c e) -> p c e", c=24),
          in_=wddq_hbm.ap().rearrange("(c p) e -> p c e", p=128),
          wait=[("upmm", None), ("gamm", None)])
    pb.wait("tensor", "d_wd")
    for j in range(16):
        pb.dma("gpsimd", g32[:], up_hbm.ap()[j * 128:(j + 1) * 128, :],
               "d_p3", wait=[("p3v", None), ("d_p2o", None), ("d_qsp", None)])
        pb.op("vector", "tensor_scalar", wait=[("d_p3", None), ("p2v", None)],
              inc="p3v", out=g32[:], in0=g32[:], scalar1=gcol[:, 14:15],
              scalar2=MAGIC, op0=ALU.mult, op1=ALU.add, drain=True)
        pb.op("vector", "tensor_scalar", inc="p3v", out=g32[:], in0=g32[:],
              scalar1=MAGIC, scalar2=gcol[:, 15:16], op0=ALU.subtract,
              op1=ALU.mult, drain=True)
        pb.op("vector", "tensor_copy", inc="p3v", out=o_strip[:, 0:3072]
              if False else wqb[:, 0:3072], in_=g32[:], drain=True)
        for fc in range(24):
            pk = psum[4 + (fc % 2)]
            tp_wait = [("p3v", None)]
            if pb.val("p3tp") >= 2:
                tp_wait.append(("p3e", pb.val("p3e") - 1))
            pb.op("tensor", "transpose", pk[:, 0:128],
                  wqb[:, fc * 128:(fc + 1) * 128], eye_sb[:],
                  wait=tp_wait, inc="p3tp")
            pb.op("vector", "tensor_copy",
                  wait=[("p3tp", None), ("upmm", None), ("gamm", None)],
                  inc="p3e",
                  out=o_strip[:, fc * 128:(fc + 1) * 128], in_=pk[:, 0:128], drain=True)
        for eh in range(2):
            pe = psum[eh]
            for k in range(24):
                mmw = None
                if k == 0:
                    mmw = [("p3e", None), ("dnev", None)]
                pb.op("tensor", "matmul", pe[:, 0:512],
                      o_strip[:, k * 128:(k + 1) * 128],
                      wbig[:, k * 1024 + eh * 512: k * 1024 + eh * 512 + 512],
                      start=(k == 0), stop=(k == 23), wait=mmw, inc="dnmm")
            pb.op("vector", "tensor_tensor", wait=[("dnmm", None)], inc="dnev",
                  out=stream[:, j * 1024 + eh * 512: j * 1024 + eh * 512 + 512],
                  in0=stream[:, j * 1024 + eh * 512: j * 1024 + eh * 512 + 512],
                  in1=pe[:, 0:512], op=ALU.add, drain=True)

    stream_ln([("dnev", None), ("d_dbg", None)], inplace=True)
    estack.close()

    for j in range(16):
        pb.dma("gpsimd", out[j * 128:(j + 1) * 128, :],
               stream[:, j * 1024:(j + 1) * 1024], "d_out",
               wait=[("lnap", None)])
    pb.wait("gpsimd", "d_out")
    if "after_mixer" in dbg:
        pb.wait("gpsimd", "d_dbg")

    pb.replay(stack.enter_context(nc.Block()))
    stack.close()
    return nc



def _prep_core_inputs(c, I):
    h, b = c % 2, c // 2
    f32 = np.float32
    perm = np.concatenate([np.arange(h * DH, (h + 1) * DH),
                           np.arange((1 - h) * DH, (2 - h) * DH)])
    own = perm[:DH]
    ipw = np.asarray(I["L0_attn_in_proj_w"], f32)
    xr_w = ipw[:DI][perm]
    z_w = ipw[DI:][own]
    winT_full = np.concatenate([xr_w, z_w], 0).T
    winT = np.ascontiguousarray(
        winT_full.reshape(8, 128, 24, 128).transpose(2, 1, 0, 3)
        .reshape(24, 128, 1024))
    conv_w = np.asarray(I["L0_attn_conv_w"], f32)[perm, 0, :]
    conv_w_cols = np.ascontiguousarray(
        conv_w.reshape(16, 128, 4).transpose(1, 0, 2).reshape(128, 64))
    conv_b_cols = np.ascontiguousarray(
        np.asarray(I["L0_attn_conv_b"], f32)[perm].reshape(16, 128).T)
    ln1w_col = np.ascontiguousarray(
        np.asarray(I["L0_attn_ln1_w"], f32).reshape(8, 128).T)
    xpw = np.asarray(I["L0_attn_x_proj_w"], f32)[:, perm]
    xpw_pad = np.zeros((112, DI), f32)
    xpw_pad[0:80] = xpw[0:80]
    xpw_pad[96:112] = xpw[80:96]
    xprojT = np.ascontiguousarray(xpw_pad.T)
    dtwT = np.ascontiguousarray(np.asarray(I["L0_attn_dt_proj_w"], f32)[own].T)
    dtb_col = np.ascontiguousarray(
        np.asarray(I["L0_attn_dt_proj_b"], f32)[own].reshape(8, 128).T)
    alog_scan = np.ascontiguousarray(
        np.asarray(I["L0_attn_A_log"], f32)[own].reshape(128, 128).T)
    dp_col = np.ascontiguousarray(
        np.asarray(I["L0_attn_Dp"], f32)[own].reshape(8, 128).T)
    woutT = np.ascontiguousarray(
        np.asarray(I["L0_attn_out_proj_w"], f32)[:, own].T)

    bf = ml_dtypes.bfloat16
    sel16 = np.zeros((16, 128), bf)
    for k in range(16):
        sel16[k, np.arange(k, 128, 16)] = 1
    sely = np.zeros((128, 16 * 128), bf)
    for s in range(16):
        for p in range(128):
            sely[p, s * 128 + s * 8 + p // 16] = 1
    eye128 = np.eye(128, dtype=bf)

    fcT = np.asarray(I["L0_attn_fc_w"], f32).T
    fcT_tiled = np.ascontiguousarray(
        fcT.reshape(8, 128, 32, 128).transpose(2, 1, 0, 3).reshape(32, 128, 1024))
    projT = np.ascontiguousarray(np.asarray(I["L0_attn_proj_w"], f32).T)
    w8T = np.ascontiguousarray(
        np.asarray(I["L0_ffn_weight"]).T.astype(ml_dtypes.bfloat16))
    ws_in = np.asarray(I["L0_ffn_weight_scale"], f32).reshape(1, 1)
    bias_row_in = np.ascontiguousarray(
        np.asarray(I["L0_ffn_bias"], f32).reshape(1, 1024))
    ln2w_col_in = np.ascontiguousarray(
        np.asarray(I["L0_attn_ln2_w"], f32).reshape(8, 128).T)

    def tile_w(w):
        return w.reshape(8, 128, 8, 128).transpose(2, 1, 0, 3).reshape(8, 128, 1024)

    l1_qkvo = np.zeros((4, 4, 8, 128, 1024), f32)
    for i in range(4):
        l1_qkvo[i, 0] = tile_w(np.asarray(I["L1_attn_Wq"], f32)[i])
        l1_qkvo[i, 1] = tile_w(np.asarray(I["L1_attn_Wk"], f32)[i])
        l1_qkvo[i, 2] = tile_w(np.asarray(I["L1_attn_Wv"], f32)[i])
        l1_qkvo[i, 3] = tile_w(np.asarray(I["L1_attn_Wo"], f32)[i])
    l1_w1 = np.zeros((4, 32, 128, 1024), f32)
    for i in range(4):
        w1 = np.asarray(I["L1_attn_W1"], f32)[i]
        l1_w1[i] = w1.reshape(8, 128, 32, 128).transpose(2, 1, 0, 3) \
            .reshape(32, 128, 1024)
    l1_w2 = np.ascontiguousarray(np.asarray(I["L1_attn_W2"], f32))
    l1_bqkvo = np.zeros((4, 4, 128, 8), f32)
    for i in range(4):
        for wi, nm in enumerate(("bq", "bk", "bv", "bo")):
            l1_bqkvo[i, wi] = np.asarray(I["L1_attn_" + nm], f32)[i] \
                .reshape(8, 128).T
    l1_b1c = np.zeros((4, 128, 32), f32)
    for i in range(4):
        l1_b1c[i] = np.asarray(I["L1_attn_b1"], f32)[i].reshape(32, 128).T
    l1_b2r = np.asarray(I["L1_attn_b2"], f32).reshape(4, 1, 1024)
    l1_lnsb = np.stack([
        np.stack([np.asarray(I["L1_attn_ln1_s"], f32)[i].reshape(1, 1024),
                  np.asarray(I["L1_attn_ln1_b"], f32)[i].reshape(1, 1024),
                  np.asarray(I["L1_attn_ln2_s"], f32)[i].reshape(1, 1024),
                  np.asarray(I["L1_attn_ln2_b"], f32)[i].reshape(1, 1024)])
        for i in range(4)])

    x = np.asarray(I["L0_attn_x"], f32)
    return {
        "eye128f": np.eye(128, dtype=f32),
        "wup_in": np.ascontiguousarray(np.asarray(I["L1_ffn_w_gate"], f32)) if False
        else np.ascontiguousarray(np.asarray(I["L1_ffn_w_up"], f32)),
        "wg_in": np.ascontiguousarray(np.asarray(I["L1_ffn_w_gate"], f32)),
        "wd_in": np.ascontiguousarray(np.asarray(I["L1_ffn_w_down"], f32)),
        "l1_qkvo": l1_qkvo, "l1_w1": l1_w1, "l1_w2": l1_w2,
        "l1_bqkvo": l1_bqkvo, "l1_b1c": l1_b1c, "l1_b2r": l1_b2r,
        "l1_lnsb": np.ascontiguousarray(l1_lnsb),
        "fcT_tiled": fcT_tiled, "projT": projT, "w8T": w8T, "ws_in": ws_in,
        "bias_row_in": bias_row_in, "ln2w_col_in": ln2w_col_in,
        "x_batch": np.ascontiguousarray(x[b]),
        "x_own": np.ascontiguousarray(x[b, h * TOWN:(h + 1) * TOWN]),
        "winT": winT, "conv_w_cols": conv_w_cols, "conv_b_cols": conv_b_cols,
        "ln1w_col": ln1w_col, "xprojT": xprojT, "dtwT": dtwT,
        "dtb_col": dtb_col, "alog_scan": alog_scan, "dp_col": dp_col,
        "woutT": woutT, "sel16": sel16, "sely": sely, "eye128": eye128,
    }


_DEBUG = ()


def kernel(**inputs):
    nc = build_program(debug=_DEBUG)
    in_maps = [_prep_core_inputs(c, inputs) for c in range(NC)]
    res = run_bass_kernel_spmd(nc, in_maps, core_ids=list(range(NC)))
    outs = [np.asarray(res.results[c]["out"]) for c in range(NC)]
    full = np.stack(outs).reshape(OUT_SHAPE)
    if _DEBUG:
        kernel._dbg = [{k: np.asarray(v) for k, v in res.results[c].items()}
                       for c in range(NC)]
    return full
